# version 20
# baseline (speedup 1.0000x reference)
import contextlib
import numpy as np
import concourse.bass as bass
import concourse.mybir as mybir
from concourse.bass_utils import run_bass_kernel_spmd
from concourse import library_config

F32 = mybir.dt.float32
BF16 = mybir.dt.bfloat16
ALU = mybir.AluOpType
AF = mybir.ActivationFunctionType
AX = mybir.AxisListType.X

MAGIC = 12582912.0
D_MODEL = 2048
HID = 4096
H = 64
P = 64
NSTATE = 128
KCONV = 4
Q = 128
CONV = 4352
DIN = 8512
I_FFN = 8192
NKT = 16
NOT = 67
LN_EPS = 1e-5
RMS_EPS_M = 1e-5
RMS_EPS_F = 1e-6
QEPS = 1e-5

ENGINES = ("tensor", "vector", "scalar", "gpsimd", "sync")


class _Call:
    __slots__ = ("op", "args", "kwargs", "incs", "auto_sem")

    def __init__(self, op, args, kwargs):
        self.op, self.args, self.kwargs = op, args, kwargs
        self.incs = []
        self.auto_sem = None

    def then_inc(self, sem, k):
        self.incs.append((sem, k))
        return self


_NOAUTO = {"wait_ge", "load_library", "collective_compute", "dma_start",
           "dma_start_transpose", "sem_inc", "engine_nop", "nop"}
_AUTO_ENGINES = {"vector", "scalar", "gpsimd"}


class _EngRec:
    def __init__(self, em, name):
        object.__setattr__(self, "_em", em)
        object.__setattr__(self, "_name", name)

    def __getattr__(self, op):
        em, name = self._em, self._name

        def call(*a, **k):
            auto = name in _AUTO_ENGINES and op not in _NOAUTO
            if auto:
                sem = em.auto_sems.get(name)
                if sem is not None and sem.n > 0:
                    em.streams[name].append(_Call("wait_ge", (sem.h, sem.n), {}))
            c = _Call(op, a, k)
            em.streams[name].append(c)
            if auto and sem is not None:
                c.incs.append((sem, 1))
                sem.n += 1
                c.auto_sem = sem
            return c
        return call


class Sem:
    def __init__(self, handle, queue=False):
        self.h = handle
        self.n = 0
        self.queue = queue

    def inc(self, call, k=1):
        if getattr(call, "auto_sem", None) is self:
            return self.n
        call.then_inc(self, k)
        self.n += k
        return self.n


class Emitter:
    def __init__(self):
        self.streams = {e: [] for e in ENGINES}
        self.auto_sems = {}
        self.t = _EngRec(self, "tensor")
        self.v = _EngRec(self, "vector")
        self.s = _EngRec(self, "scalar")
        self.g = _EngRec(self, "gpsimd")
        self.y = _EngRec(self, "sync")

    def wait(self, eng_rec, sem, val=None):
        v = sem.n if val is None else val
        assert v <= sem.n, "wait on future semaphore value"
        if sem.queue and 0 < v < sem.n:
            v = sem.n
        if v > 0:
            eng_rec.wait_ge(sem.h, v)

    def replay(self, block):
        def make(name):
            def body(eng):
                for idx, c in enumerate(self.streams[name]):
                    try:
                        instr = getattr(eng, c.op)(*c.args, **c.kwargs)
                    except Exception:
                        import sys as _s
                        print(f"REPLAY FAIL {name}[{idx}] op={c.op}", file=_s.stderr)
                        raise
                    for (sem, k) in c.incs:
                        instr.then_inc(sem.h, k)
            return body
        for name in ENGINES:
            if self.streams[name]:
                getattr(block, name)(make(name))


def ceil_div(a, b):
    return (a + b - 1) // b


def col_splits(total, maxn=512):
    out = []
    off = 0
    while off < total:
        n = min(maxn, total - off)
        out.append((off, n))
        off += n
    return out


def build_program(nch=16, dbg=(), mode="full"):
    T = nch * Q
    TC = T + 3
    assert nch % 2 == 0
    npr = nch // 2
    NTB = ceil_div(T, 128)
    LOCAL = (mode == "local")
    NMT = 96 if LOCAL else 12

    nc = bass.Bass("TRN2", target_bir_lowering=False)
    E = Emitter()
    dbg = set(dbg)

    dxT = nc.dram_tensor("xT", [NKT, 128, TC], F32, kind="ExternalInput").ap()
    dWin = nc.dram_tensor("WinT", [NOT, NKT, 128, 128], F32, kind="ExternalInput").ap()
    dWout = nc.dram_tensor("WoutT", [16, 32, 128, 128], F32, kind="ExternalInput").ap()
    nwsl = 64 if LOCAL else 8
    dwq = nc.dram_tensor("wq_slice", [3, 128, nwsl * 2048], F32, kind="ExternalInput").ap()
    dwms = nc.dram_tensor("wmod_s", [NMT, NKT, 128, 128], F32, kind="ExternalInput").ap()
    dSin = nc.dram_tensor("SinitT", [128, HID], F32, kind="ExternalInput").ap() if LOCAL else None
    dcT = nc.dram_tensor("cT", [D_MODEL, 4], F32, kind="ExternalInput").ap()
    dconvw = nc.dram_tensor("convw", [128, 136], F32, kind="ExternalInput").ap()
    ddtbA = nc.dram_tensor("dtbA", [64, 2], F32, kind="ExternalInput").ap()
    dDrep = nc.dram_tensor("Drep", [128, 32], F32, kind="ExternalInput").ap()
    dnormw = nc.dram_tensor("normw", [128, 32], F32, kind="ExternalInput").ap()
    dlnw = nc.dram_tensor("lnw", [128, 64], F32, kind="ExternalInput").ap()
    dmasks = nc.dram_tensor("masks", [128, 8], F32, kind="ExternalInput").ap()
    dout = nc.dram_tensor("out", [NKT, 128, T], F32, kind="ExternalOutput").ap()

    d_spill = nc.dram_tensor("spill_zx", [NOT, 128, TC], BF16).ap()
    d_y1 = nc.dram_tensor("spill_y1", [nch, 32, 128, Q], BF16).ap()
    d_yn = nc.dram_tensor("spill_yn", [32, 128, T], BF16).ap()
    d_x1 = nc.dram_tensor("spill_x1", [NKT, 128, T], F32).ap()
    d_hmid = nc.dram_tensor("spill_hmid", [64, 128, T], BF16).ap()
    shr = "Shared" if not LOCAL else "Local"
    d_armi = nc.dram_tensor("ag_mod_in", [128, 48], F32)
    d_armo = nc.dram_tensor("ag_mod_out", [8, 128, 48], F32, addr_space=shr)
    d_arwi = nc.dram_tensor("ar_w_in", [1, 128], F32)
    d_arwo = nc.dram_tensor("ar_w_out", [1, 128], F32, addr_space=shr)
    d_agqi = nc.dram_tensor("ag_wq_in", [3, 128, 16384], BF16)
    d_agqo = nc.dram_tensor("ag_wq_out", [8, 3, 128, 16384], BF16, addr_space=shr)
    d_arsi = nc.dram_tensor("ar_s_in", [128, HID], F32)
    d_arso = nc.dram_tensor("ar_s_out", [128, HID], F32)
    d_rowb = nc.dram_tensor("row_bounce", [32, 128], F32).ap()
    d_cum = nc.dram_tensor("cum_spill", [64, 16 * Q], F32).ap()
    d_ep = nc.dram_tensor("ep_spill", [64, 16 * Q], BF16).ap()

    dbg_outs = {}

    def dbg_tensor(name, shape, dtype=F32):
        if name in dbg:
            dbg_outs[name] = nc.dram_tensor("dbg_" + name, shape, dtype,
                                            kind="ExternalOutput").ap()
            return dbg_outs[name]
        return None

    root = contextlib.ExitStack()

    def mk(stk):
        def sb(name, shape, dt=F32):
            return stk.enter_context(nc.sbuf_tensor(name, shape, dt))

        def ps(name, shape, dt=F32):
            return stk.enter_context(nc.psum_tensor(name, shape, dt))
        return sb, ps

    sbr, psr = mk(root)

    def seg(side="left"):
        stk = contextlib.ExitStack()

        def sb(name, shape, dt=F32):
            return stk.enter_context(nc.sbuf_tensor(name, shape, dt, side=side))
        return stk, sb

    sems = {}

    def sem(name, queue=False):
        sems[name] = Sem(root.enter_context(nc.semaphore(name)), queue)
        return sems[name]

    sy = sem("q_sync", queue=True)
    sg = sem("q_gp", queue=True)
    szx = [sem("q_zx0", queue=True), sem("q_zx1", queue=True)]
    swi = [sem("q_wi0", queue=True), sem("q_wi1", queue=True)]
    sr3 = [sem("q_r30", queue=True), sem("q_r31", queue=True)]
    sr4 = [sem("q_r40", queue=True), sem("q_r41", queue=True)]
    sy1 = [sem("q_y10", queue=True), sem("q_y11", queue=True)]
    swo = [sem("q_wo0", queue=True), sem("q_wo1", queue=True)]
    sg7 = [sem("q_g70", queue=True), sem("q_g71", queue=True)]
    shm = [sem("q_hm0", queue=True), sem("q_hm1", queue=True)]
    swd = [sem("q_wd0", queue=True), sem("q_wd1", queue=True)]
    sxr = sem("q_xr", queue=True)
    scm = sem("q_cm", queue=True)
    sy1s = sem("q_y1s", queue=True)
    syns = sem("q_yns", queue=True)
    sx1s = sem("q_x1s", queue=True)
    smod = sem("q_mod", queue=True)
    srow = sem("q_row", queue=True)
    sx8 = sem("q_x8", queue=True)
    sout = sem("q_out", queue=True)
    ALLQ = ([sy, sg, sxr, sx8, sout, scm, sy1s, syns, sx1s, smod, srow] + szx +
            swi + sr3 + sr4 + sy1 + swo + sg7 + shm + swd)
    sT = sem("c_T")
    sV = sem("c_V")
    sA = sem("c_A")
    sG = sem("c_G")
    sCC = sem("c_CC")
    E.auto_sems = {"vector": sV, "scalar": sA, "gpsimd": sG}

    def phase_barrier():
        for rec in (E.t, E.v, E.s, E.g, E.y):
            for s_ in [sT, sV, sA, sG] + ALLQ:
                E.wait(rec, s_, s_.n)

    def bounce_bcast(rec_consumers, row_ap, rep_ap, nfree, doff=0):
        E.wait(E.y, sy, sy.n)
        E.y.dma_start(out=d_rowb.rearrange("a b -> (a b)")[doff:doff + nfree].unsqueeze(0),
                      in_=row_ap).then_inc(srow, 16)
        srow.n += 16
        E.wait(E.y, srow, srow.n)
        E.y.dma_start(out=rep_ap,
                      in_=d_rowb.rearrange("a b -> (a b)")[doff:doff + nfree]
                      .unsqueeze(0).broadcast_to([128, nfree])).then_inc(sy, 16)
        sy.n += 16
        return sy.n

    def dmay(out, in_, sem_=None):
        s_ = sem_ or sy
        E.y.dma_start(out=out, in_=in_).then_inc(s_, 16)
        s_.n += 16
        return s_.n

    def dmag(out, in_, sem_=None):
        s_ = sem_ or sg
        E.g.dma_start(out=out, in_=in_).then_inc(s_, 16)
        s_.n += 16
        return s_.n

    ones_col = sbr("ones_col", [128, 1])
    tri = sbr("tri", [128, 128])
    identf = sbr("identf", [128, 128])
    ident = sbr("ident", [128, 128], BF16)
    masks_sb = sbr("masks_s", [128, 8])
    convw_sb = sbr("convw_s", [128, 136])
    dtbA_sb = sbr("dtbA_s", [64, 2])
    Drep_sb = sbr("Drep_s", [128, 32])
    normw_sb = sbr("normw_s", [128, 32])
    lnw_sb = sbr("lnw_s", [128, 64])
    vecs = sbr("vecs", [128, 96])
    swB = sbr("swB", [128, 128])
    rowT = sbr("rowT", [1, max(T, 16 * Q)])

    sa_c = lambda k: vecs[:, 0 + k:1 + k]
    ca1_c = lambda k: vecs[:, 16 + k:17 + k]
    ga_c = lambda k: vecs[:, 32 + k:33 + k]
    sb_c = lambda k: vecs[:, 48 + k:49 + k]
    cb1_c = lambda k: vecs[:, 64 + k:65 + k]
    gb_c = lambda k: vecs[:, 80 + k:81 + k]

    dmay(masks_sb[:], dmasks[:])
    dmay(convw_sb[:], dconvw[:])
    dmay(dtbA_sb[:], ddtbA[:])
    dmay(Drep_sb[:], dDrep[:])
    dmay(normw_sb[:], dnormw[:])
    dmay(lnw_sb[:], dlnw[:])
    ld_small = sy.n

    sV.inc(E.v.memset(ones_col[:], 1.0))
    v0 = sV.inc(E.v.memset(identf[:], 1.0))
    E.wait(E.g, sV, v0)
    g0 = sG.inc(E.g.affine_select(identf[:], identf[:], [[1, 128]], ALU.is_equal, 0.0,
                                  base=0, channel_multiplier=-1))
    E.wait(E.v, sG, g0)
    sV.inc(E.v.tensor_copy(ident[:], identf[:]))
    v0 = sV.inc(E.v.memset(tri[:], 1.0))
    E.wait(E.g, sV, v0)
    g0 = sG.inc(E.g.affine_select(tri[:], tri[:], [[1, 128]], ALU.is_ge, 0.0,
                                  base=0, channel_multiplier=-1))
    E.wait(E.v, sG, g0)

    with contextlib.ExitStack() as stk:
        sb, ps = mk(stk)
        csb = sb("csb", [128, 64])
        dmay(csb[:].rearrange("p (kt b) -> p kt b", b=4),
             dcT.rearrange("(kt p) b -> p kt b", p=128))
        ld_c = sy.n
        siluc = sb("siluc", [128, 64])
        E.wait(E.s, sy, ld_c)
        a_sc = sA.inc(E.s.activation(siluc[:], csb[:], AF.Silu))
        wm_sb = sb("wm_sb", [128, 2 * NKT * 128])
        wm_sl = lambda p_: wm_sb[:, p_ * NKT * 128:(p_ + 1) * NKT * 128]
        modps = ps("modps", [128, 4])
        modpre = sb("modpre", [128, 48]) if not LOCAL else None
        ag4 = sb("ag4", [128, 4 * 96])
        ev_V = 0
        mm_Tm = [0, 0]
        for mt in range(NMT):
            mpar = mt % 2
            wm_sem = smod if mpar == 0 else srow
            E.wait(E.y, sT, mm_Tm[mpar])
            g_l = dmay(wm_sl(mpar).rearrange("p (a c) -> p a c", c=128),
                       dwms[mt].rearrange("a p c -> p a c"), wm_sem)
            E.wait(E.t, wm_sem, g_l)
            if mt == 0:
                E.wait(E.t, sA, a_sc)
            else:
                E.wait(E.t, sV, ev_V)
            for k in range(NKT):
                mm = E.t.matmul(modps[:], wm_sl(mpar)[:, k * 128:(k + 1) * 128],
                                siluc[:, k * 4:k * 4 + 4],
                                start=(k == 0), stop=(k == NKT - 1))
            t_mm = sT.inc(mm)
            mm_Tm[mpar] = t_mm
            E.wait(E.v, sT, t_mm)
            if LOCAL:
                ev_V = sV.inc(E.v.tensor_copy(
                    ag4[:].rearrange("p (b g) -> p b g", g=96)[:, :, mt], modps[:]))
            else:
                ev_V = sV.inc(E.v.tensor_copy(
                    modpre[:].rearrange("p (b mtx) -> p b mtx", mtx=12)[:, :, mt], modps[:]))
        if not LOCAL:
            E.wait(E.y, sV, ev_V)
            dmay(d_armi.ap()[:], modpre[:])
            E.wait(E.g, sy, sy.n)
            cc1 = sCC.inc(E.g.collective_compute(
                "AllGather", ALU.bypass, replica_groups=[list(range(8))],
                ins=[d_armi.ap().opt()], outs=[d_armo.ap().opt()]))
            E.wait(E.y, sCC, cc1)
            E.wait(E.y, sy, ld_small)
            for b in range(4):
                dmay(ag4[:, b * 96:(b + 1) * 96].rearrange("p (r mt) -> p r mt", mt=12),
                     d_armo.ap()[:, :, b * 12:(b + 1) * 12].rearrange("r p mt -> p r mt"))
        modsel = sb("modsel", [128, 96])
        E.wait(E.v, sy, max(sy.n if not LOCAL else 0, ld_small))
        E.wait(E.v, sV, ev_V)
        sV.inc(E.v.tensor_scalar_mul(modsel[:], ag4[:, 0:96], masks_sb[:, 3:4]))
        for b in range(1, 4):
            E.wait(E.v, sV, sV.n)
            sV.inc(E.v.scalar_tensor_tensor(
                modsel[:], ag4[:, b * 96:(b + 1) * 96], masks_sb[:, 3 + b:4 + b],
                modsel[:], ALU.mult, ALU.add))
        v_ms = sV.n
        E.wait(E.v, sV, v_ms)
        sV.inc(E.v.tensor_copy(vecs[:, 0:16], modsel[:, 0:16]))
        sV.inc(E.v.tensor_scalar_add(vecs[:, 16:32], modsel[:, 16:32], 1.0))
        sV.inc(E.v.tensor_copy(vecs[:, 48:64], modsel[:, 48:64]))
        sV.inc(E.v.tensor_scalar_add(vecs[:, 64:80], modsel[:, 64:80], 1.0))
        gtmp = sb("gtmp", [128, 32])
        gsig = sb("gsig", [128, 32])
        sV.inc(E.v.tensor_scalar(gsig[:, 0:16], modsel[:, 32:48], 0.0, 2.0,
                                 ALU.is_ge, ALU.mult))
        sV.inc(E.v.tensor_scalar(gsig[:, 16:32], modsel[:, 80:96], 0.0, 2.0,
                                 ALU.is_ge, ALU.mult))
        v_g1 = sV.inc(E.v.tensor_scalar_sub(gsig[:], gsig[:], 1.0))
        E.wait(E.s, sV, v_ms)
        sA.inc(E.s.activation(gtmp[:, 0:16], modsel[:, 32:48], AF.Abs))
        a_abs = sA.inc(E.s.activation(gtmp[:, 16:32], modsel[:, 80:96], AF.Abs))
        E.wait(E.v, sA, a_abs)
        v_g2 = sV.inc(E.v.tensor_scalar_max(gtmp[:], gtmp[:], 0.25))
        E.wait(E.v, sV, max(v_g1, v_g2))
        sV.inc(E.v.tensor_mul(vecs[:, 32:48], gsig[:, 0:16], gtmp[:, 0:16]))
        v_vecs = sV.inc(E.v.tensor_mul(vecs[:, 80:96], gsig[:, 16:32], gtmp[:, 16:32]))
        if dbg_tensor("vecs", [128, 96]):
            E.wait(E.y, sV, v_vecs)
            dmay(dbg_outs["vecs"][:], vecs[:])

    phase_barrier()
    seg0c = contextlib.ExitStack()

    def sb0c(name, shape, dt=F32):
        return seg0c.enter_context(nc.sbuf_tensor(name, shape, dt, side="left"))
    wqs = sb0c("wqs", [128, 2048])
    wqs2 = sb0c("wqs2", [128, 2048])
    wqo = sb0c("wqo", [128, 2 * 2048], BF16)
    wqs_h = lambda h: (wqs, wqs2)[h][:]
    wqo_h = lambda h: wqo[:, h * 2048:(h + 1) * 2048]
    wabs = sb0c("wabs", [128, 4])
    wtmp = sb0c("wtmp", [128, 2])
    wrow = sb0c("wrow", [1, 128])
    swrow = sb0c("swrow", [1, 128])
    wsum_ps = seg0c.enter_context(nc.psum_tensor("wsum_ps", [1, 4], F32))
    NSLAB = 3 * nwsl
    p1_red = [0]
    p1_ld = {}
    p1_vred = [0]
    sV.inc(E.v.memset(wabs[:, 3:4], 0.0))

    def p1_issue(i):
        ti, r = divmod(i, nwsl)
        E.wait(E.y, sV, p1_red[0])
        p1_ld[i] = dmay(wqs[:], dwq[ti, :, r * 2048:(r + 1) * 2048], smod)

    def p1_reduce(i):
        ti, r = divmod(i, nwsl)
        E.wait(E.v, smod, p1_ld.pop(i))
        if r == 0:
            p1_vred[0] = sV.inc(E.v.tensor_reduce(wabs[:, ti:ti + 1], wqs[:], AX,
                                                  ALU.add, apply_absolute_value=True))
        else:
            sV.inc(E.v.tensor_reduce(wtmp[:, 0:1], wqs[:], AX, ALU.add,
                                     apply_absolute_value=True))
            p1_vred[0] = sV.inc(E.v.tensor_add(wabs[:, ti:ti + 1], wabs[:, ti:ti + 1],
                                               wtmp[:, 0:1]))
        p1_red[0] = p1_vred[0]

    q2_v = [0, 0]
    q2_out = [0, 0]
    q2_next = [0]
    y_swb_r = [0]

    def q2_emit(i):
        par = i % 2
        ti, r = divmod(i, nwsl)
        E.wait(E.y, sV, q2_v[par] or p1_vred[0])
        g_l = dmay(wqs_h(par), dwq[ti, :, r * 2048:(r + 1) * 2048], smod)
        E.wait(E.v, smod, g_l)
        E.wait(E.v, sy, y_swb_r[0])
        sV.inc(E.v.tensor_scalar(wqs_h(par), wqs_h(par), swB[:, 8 + ti:9 + ti],
                                 MAGIC, ALU.mult, ALU.add))
        sV.inc(E.v.tensor_scalar(wqs_h(par), wqs_h(par), MAGIC, -1.0,
                                 ALU.subtract, ALU.max))
        E.wait(E.v, sy1[par], q2_out[par])
        q2_v[par] = sV.inc(E.v.tensor_scalar_min(wqo_h(par), wqs_h(par), 1.0))
        E.wait(E.y, sV, q2_v[par])
        if LOCAL:
            q2_out[par] = dmay(
                d_agqo.ap()[r // 8, ti][:, (r % 8) * 2048:(r % 8 + 1) * 2048],
                wqo_h(par), sy1[par])
        else:
            q2_out[par] = dmay(d_agqi.ap()[ti][:, r * 2048:(r + 1) * 2048],
                               wqo_h(par), sy1[par])

    def q2_pump(k):
        while q2_next[0] < NSLAB and k > 0:
            q2_emit(q2_next[0])
            q2_next[0] += 1
            k -= 1

    seg_dt, sb_dt = seg("left")
    dt_sb = sb_dt("dt_sb", [64, TC])
    with contextlib.ExitStack() as stk:
        sb, ps = mk(stk)
        h1T = sb("h1T", [128, NKT * TC], BF16)
        h1t = lambda k: h1T[:, k * TC:(k + 1) * TC]
        xst = sb("xst", [128, NKT * 512])
        xsq = sb("xsq", [128, NKT * 512])
        stats_stk = contextlib.ExitStack()
        st_ps = stats_stk.enter_context(nc.psum_tensor("st_ps", [33, TC], F32))
        blocks = col_splits(TC, 512)
        a_done = 0
        for (off, nb) in blocks:
            E.wait(E.y, sA, a_done)
            for k in range(NKT):
                dmay(xst[:, k * 512:k * 512 + nb], dxT[k, :, off:off + nb])
            E.wait(E.s, sy, sy.n)
            for k in range(NKT):
                a_done = sA.inc(E.s.activation(xsq[:, k * 512:k * 512 + nb],
                                               xst[:, k * 512:k * 512 + nb], AF.Square))
            E.wait(E.t, sA, a_done)
            for k in range(NKT):
                E.t.matmul(st_ps[0:1, off:off + nb], ones_col[:],
                           xst[:, k * 512:k * 512 + nb],
                           start=(k == 0), stop=(k == NKT - 1), skip_group_check=True)
                mm2 = E.t.matmul(st_ps[32:33, off:off + nb], ones_col[:],
                                 xsq[:, k * 512:k * 512 + nb],
                                 start=(k == 0), stop=(k == NKT - 1), skip_group_check=True)
            t_st = sT.inc(mm2)
        rows = stats_stk.enter_context(
            nc.sbuf_tensor("rows_ln1", [1, 2 * TC], F32, side="right"))
        mrow = rows[:, 0:TC]
        rrow = rows[:, TC:2 * TC]
        E.wait(E.v, sT, sT.n)
        sV.inc(E.v.tensor_scalar_mul(mrow, st_ps[0:1, :], 1.0 / D_MODEL))
        E.wait(E.v, sV, sV.n)
        sV.inc(E.v.tensor_mul(rrow, mrow, mrow))
        E.wait(E.v, sV, sV.n)
        sV.inc(E.v.scalar_tensor_tensor(rrow, st_ps[32:33, :], 1.0 / D_MODEL, rrow,
                                        ALU.mult, ALU.subtract))
        E.wait(E.v, sV, sV.n)
        sV.inc(E.v.tensor_scalar_add(rrow, rrow, LN_EPS))
        E.wait(E.v, sV, sV.n)
        v_r = sV.inc(E.v.reciprocal(rrow, rrow))
        E.wait(E.s, sV, v_r)
        a_r = sA.inc(E.s.activation(rrow, rrow, AF.Sqrt))
        mrep = sb("mrep", [128, TC])
        rrep = sb("rrep", [128, TC])
        E.wait(E.y, sA, a_r)
        E.wait(E.y, sV, v_r)
        bounce_bcast(None, mrow, mrep[:], TC)
        g_m = bounce_bcast(None, rrow, rrep[:], TC)
        g_r = g_m
        stats_stk.close()
        if dbg_tensor("ln1rows", [1, 2 * TC]):
            E.wait(E.y, sA, a_r)
            dmay(dbg_outs["ln1rows"][:], rows[:])
        for bi, (off, nb) in enumerate(blocks):
            E.wait(E.y, sA, sA.n)
            for k in range(NKT):
                dmay(xst[:, k * 512:k * 512 + nb], dxT[k, :, off:off + nb])
            yld = sy.n
            E.wait(E.v, sy, yld)
            sV.inc(E.v.tensor_tensor(
                xsq[:].rearrange("p (k c) -> p k c", c=512)[:, :, 0:nb],
                xst[:].rearrange("p (k c) -> p k c", c=512)[:, :, 0:nb],
                mrep[:, off:off + nb].unsqueeze(1).broadcast_to([128, NKT, nb]),
                ALU.subtract))
            vv = sV.inc(E.v.tensor_tensor(
                xsq[:].rearrange("p (k c) -> p k c", c=512)[:, :, 0:nb],
                xsq[:].rearrange("p (k c) -> p k c", c=512)[:, :, 0:nb],
                rrep[:, off:off + nb].unsqueeze(1).broadcast_to([128, NKT, nb]),
                ALU.mult))
            E.wait(E.s, sV, vv)
            for k in range(NKT):
                sA.inc(E.s.activation(h1t(k)[:, off:off + nb], xsq[:, k * 512:k * 512 + nb],
                                      AF.Identity, bias=sa_c(k), scale=ca1_c(k)))
        E.wait(E.v, sA, sA.n)
        for k in range(NKT):
            v_bm = sV.inc(E.v.tensor_scalar_mul(h1t(k)[:, 0:3], h1t(k)[:, 0:3],
                                                masks_sb[:, 0:1]))
        h1_done_V = sV.n
        if dbg_tensor("h1T", [NKT, 128, TC], BF16):
            E.wait(E.y, sV, h1_done_V)
            for k in range(NKT):
                dmay(dbg_outs["h1T"][k], h1t(k))

        win_sb = sb("win_sb", [128, 2 * NKT * 128], BF16)
        wv_sl = lambda par, k: win_sb[:, par * NKT * 128 + k * 128:par * NKT * 128 + (k + 1) * 128]
        zx_st = sb("zx_st", [128, 2 * TC], BF16)
        zx_sl = lambda par: zx_st[:, par * TC:(par + 1) * TC]
        pj = [ps(f"pj{i}", [128, 512]) for i in range(4)]
        pieces = col_splits(TC, 512)
        slab_V = [h1_done_V, h1_done_V]
        spill_Y = [0, 0]
        psum_V = [h1_done_V] * 4
        E.wait(E.g, sV, h1_done_V)
        for o in range(NOT):
            par = o % 2
            if not LOCAL:
                if o < NSLAB:
                    if o >= 1:
                        p1_reduce(o - 1)
                    p1_issue(o)
                elif o == NSLAB:
                    p1_reduce(NSLAB - 1)
                    E.wait(E.t, sV, p1_vred[0])
                    t_ws = sT.inc(E.t.matmul(wsum_ps[:], ones_col[:], wabs[:],
                                             start=True, stop=True))
                    E.wait(E.v, sT, t_ws)
                    sV.inc(E.v.memset(wrow[:], 0.0))
                    v_wr = sV.inc(E.v.tensor_copy(wrow[:, 0:4], wsum_ps[:]))
                    E.wait(E.y, sV, v_wr)
                    dmay(d_arwi.ap()[:], wrow[:], srow)
                    E.wait(E.g, srow, srow.n)
                    cc2 = sCC.inc(E.g.collective_compute(
                        "AllReduce", ALU.add, replica_groups=[list(range(8))],
                        ins=[d_arwi.ap().opt()], outs=[d_arwo.ap().opt()]))
                elif o == NSLAB + 4:
                    E.wait(E.s, sCC, cc2)
                    E.s.dma_start(out=swrow[:],
                                  in_=d_arwo.ap()[:]).then_inc(srow, 16)
                    srow.n += 16
                    E.wait(E.v, srow, srow.n)
                    nel = float(I_FFN * D_MODEL)
                    sV.inc(E.v.tensor_scalar(swrow[:, 0:3], swrow[:, 0:3], 1.0 / nel,
                                             QEPS, ALU.mult, ALU.max))
                    v_cm = sV.inc(E.v.reciprocal(swrow[:, 8:11], swrow[:, 0:3]))
                    E.wait(E.y, sV, v_cm)
                    y_swb_r[0] = bounce_bcast(None, swrow[:], swB[:], 128)
                elif NSLAB + 6 <= o < NSLAB + 6 + NSLAB:
                    q2_emit(o - NSLAB - 6)
                    q2_next[0] = o - NSLAB - 5
                elif o == NSLAB + 6 + NSLAB + 1:
                    E.wait(E.g, sy1[0], sy1[0].n)
                    E.wait(E.g, sy1[1], sy1[1].n)
                    cc3 = sCC.inc(E.g.collective_compute(
                        "AllGather", ALU.bypass, replica_groups=[list(range(8))],
                        ins=[d_agqi.ap().opt()], outs=[d_agqo.ap().opt()]))
            E.wait(E.g, sV, slab_V[par])
            g_l = dmag(win_sb[:, par * NKT * 128:(par + 1) * NKT * 128]
                       .rearrange("p (a c) -> p a c", c=128),
                       dWin[o].rearrange("a p c -> p a c"), swi[par])
            E.wait(E.t, swi[par], g_l)
            first_evac = True
            for pi, (poff, pnb) in enumerate(pieces):
                pb = pi % 4
                E.wait(E.t, sV, psum_V[pb])
                for k in range(NKT):
                    mm = E.t.matmul(pj[pb][:, 0:pnb], wv_sl(par, k),
                                    h1t(k)[:, poff:poff + pnb],
                                    start=(k == 0), stop=(k == NKT - 1))
                t_mm = sT.inc(mm)
                E.wait(E.v, sT, t_mm)
                if first_evac:
                    E.wait(E.v, szx[par], szx[par].n)
                    first_evac = False
                vv = sV.inc(E.v.tensor_copy(zx_sl(par)[:, poff:poff + pnb],
                                            pj[pb][:, 0:pnb]))
                if o == NOT - 1:
                    vv = sV.inc(E.v.tensor_copy(dt_sb[:, poff:poff + pnb],
                                                pj[pb][0:64, 0:pnb]))
                psum_V[pb] = vv
            E.wait(E.y, sV, psum_V[(len(pieces) - 1) % 4])
            spill_Y[par] = dmay(d_spill[o], zx_sl(par), szx[par])
            slab_V[par] = sV.n
        proj_spill_done = (szx[0].n, szx[1].n)
        if dbg_tensor("zx", [4, 128, TC], BF16):
            E.wait(E.y, szx[0], proj_spill_done[0])
            E.wait(E.y, szx[1], proj_spill_done[1])
            for i, o in enumerate((0, 32 + 1, 64, 65)):
                dmay(dbg_outs["zx"][i], d_spill[o])


    seg24, sb24 = seg("right")
    epbf = sb24("epbf", [64, T], BF16)
    E1t = sb24("E1t", [128, nch * 64], BF16)
    cumT = sb24("cumT", [128, nch * 64])
    E2t = sb24("E2t", [128, nch * 64], BF16)
    decB = sb24("decB", [128, nch * 64])
    DprB = sb24("DprB", [128, nch * 64])
    Sinit = sb24("Sinit", [128, HID])

    if LOCAL:
        for i in range(NSLAB):
            p1_issue(i)
            p1_reduce(i)
        E.wait(E.t, sV, p1_vred[0])
        t_ws = sT.inc(E.t.matmul(wsum_ps[:], ones_col[:], wabs[:], start=True, stop=True))
        E.wait(E.v, sT, t_ws)
        sV.inc(E.v.memset(wrow[:], 0.0))
        v_wr = sV.inc(E.v.tensor_copy(wrow[:, 0:4], wsum_ps[:]))
        E.wait(E.v, sV, v_wr)
        sV.inc(E.v.tensor_copy(swrow[:], wrow[:]))
        cc2 = 0

    phase_barrier()
    with contextlib.ExitStack() as stk:
        sb, ps = mk(stk)
        epfm = sb("epfm", [64, T])
        dt2 = sb("dt2", [64, T])
        acum = sb("acum", [64, T])
        E2fm = sb("E2fm", [64, T])
        zer64 = sb("zer64", [64, T])
        avt = sb("avt", [64, T])
        dcol = sb("dcol", [64, 34])
        negA = sb("negA", [64, 1])
        dcT = sb("dcT", [32, 64])
        tp2 = [ps("tp_ps0", [128, 128]), ps("tp_ps1", [128, 128])]
        E.wait(E.v, sV, sV.n)
        v_z = sV.inc(E.v.memset(zer64[:], 0.0))
        E.wait(E.s, sV, sV.n)
        E.wait(E.s, sy, ld_small)
        spx = sb("spx", [64, T])
        spr = sb("spr", [64, T])
        sA.inc(E.s.activation(spx[:], dt_sb[:, 3:TC], AF.Identity, bias=dtbA_sb[:, 0:1]))
        sA.inc(E.s.activation(spr[:], spx[:], AF.Relu))
        sA.inc(E.s.activation(dt2[:], spx[:], AF.Abs))
        sA.inc(E.s.activation(dt2[:], dt2[:], AF.Exp, scale=-1.0))
        a_sp0 = sA.inc(E.s.activation(dt2[:], dt2[:], AF.Ln, bias=1.0))
        E.wait(E.v, sA, a_sp0)
        v_sp = sV.inc(E.v.tensor_add(dt2[:], dt2[:], spr[:]))
        a_sp = a_sp0
        a_A = sA.inc(E.s.activation(negA[:], dtbA_sb[:, 1:2], AF.Exp))
        E.wait(E.v, sA, sA.n)
        v_negA = sV.inc(E.v.tensor_scalar_mul(negA[:], negA[:], -1.0))
        E.wait(E.v, sV, v_negA)
        v_av = sV.inc(E.v.tensor_scalar_mul(avt[:], dt2[:], negA[:]))
        E.wait(E.v, sV, max(v_z, v_av))
        for ci in range(nch):
            v_cum = sV.inc(E.v.tensor_tensor_scan(
                acum[:, ci * Q:(ci + 1) * Q], zer64[:, ci * Q:(ci + 1) * Q],
                avt[:, ci * Q:(ci + 1) * Q], 0.0, ALU.add, ALU.add))
        E.wait(E.s, sV, sV.n)
        a_ep = sA.inc(E.s.activation(epfm[:], acum[:], AF.Exp))
        E.wait(E.y, sV, sV.n)
        dmay(d_cum[:, 0:T], acum[:], scm)
        clcol = sb("clcol", [64, 16])
        for ci in range(nch):
            sV.inc(E.v.tensor_copy(clcol[:, ci:ci + 1],
                                   acum[:, ci * Q + Q - 1:ci * Q + Q]))
        E.wait(E.v, sA, a_ep)
        for ci in range(nch):
            sV.inc(E.v.tensor_copy(dcol[:, ci:ci + 1],
                                   epfm[:, ci * Q + Q - 1:ci * Q + Q]))
        v_dc = sV.inc(E.v.memset(dcol[:, nch:nch + 1], 1.0))
        for ci in range(1, nch):
            v_dc = sV.inc(E.v.tensor_mul(dcol[:, nch + ci:nch + ci + 1],
                                         dcol[:, nch + ci - 1:nch + ci],
                                         dcol[:, ci - 1:ci]))
        E.wait(E.s, sV, sV.n)
        for ci in range(nch):
            a_e2 = sA.inc(E.s.activation(E2fm[:, ci * Q:(ci + 1) * Q],
                                         acum[:, ci * Q:(ci + 1) * Q], AF.Exp,
                                         scale=-1.0, bias=clcol[:, ci:ci + 1]))
        E.wait(E.v, sA, a_e2)
        v_E2 = sV.inc(E.v.tensor_mul(E2fm[:], E2fm[:], dt2[:]))
        v_pre = sV.n

        if LOCAL:
            nel = float(I_FFN * D_MODEL)
            sV.inc(E.v.tensor_scalar(swrow[:, 0:3], swrow[:, 0:3], 1.0 / nel, QEPS,
                                     ALU.mult, ALU.max))
            v_cm = sV.inc(E.v.reciprocal(swrow[:, 8:11], swrow[:, 0:3]))
            E.wait(E.y, sV, v_cm)
            y_swb_r[0] = bounce_bcast(None, swrow[:], swB[:], 128)

        E.wait(E.t, sV, v_pre)
        t_dc = sT.inc(E.t.transpose(tp2[0][0:2 * nch, 0:64], dcol[:, 0:2 * nch],
                                    identf[0:64, 0:64]))
        E.wait(E.v, sT, t_dc)
        v_dct = sV.inc(E.v.tensor_copy(dcT[0:2 * nch, :], tp2[0][0:2 * nch, 0:64]))
        E.wait(E.y, sV, v_dct)
        E.wait(E.y, sy, y_swb_r[0])
        y_dc = dmay(d_rowb[0:2 * nch, 0:64], dcT[0:2 * nch, :], srow)
        E.wait(E.y, srow, y_dc)
        dmay(decB[:].rearrange("p (c k) -> p c k", k=64),
             d_rowb[0:nch, 0:64].unsqueeze(0).broadcast_to([128, nch, 64]))
        dmay(DprB[:].rearrange("p (c k) -> p c k", k=64),
             d_rowb[nch:2 * nch, 0:64].unsqueeze(0).broadcast_to([128, nch, 64]))

        tpV = [v_dct, v_pre]
        for cp in range(npr):
            c0 = 2 * cp
            for j, (src, dst) in enumerate(((dt2, E1t), (E2fm, E2t), (acum, cumT))):
                pb = (cp * 3 + j) % 2
                E.wait(E.t, sV, tpV[pb])
                E.t.transpose(tp2[pb][:, 0:64], src[:, c0 * Q:(c0 + 1) * Q],
                              identf[0:64, 0:64])
                tr = sT.inc(E.t.transpose(tp2[pb][:, 64:128],
                                          src[:, (c0 + 1) * Q:(c0 + 2) * Q],
                                          identf[0:64, 0:64]))
                E.wait(E.v, sT, tr)
                tpV[pb] = sV.inc(E.v.tensor_copy(dst[:, c0 * 64:(c0 + 2) * 64],
                                                 tp2[pb][:]))
            if LOCAL:
                q2_pump(3)
        if LOCAL:
            q2_pump(NSLAB)
            cc3 = 0
        if dbg_tensor("negacheck", [64, 2 * T + 2]):
            E.wait(E.y, sV, sV.n)
            E.wait(E.y, sA, sA.n)
            E.y.dma_start(out=dbg_outs["negacheck"][:, 0:1], in_=negA[:],
                          allow_slow_non_contiguous=True).then_inc(sy, 16)
            sy.n += 16
            E.y.dma_start(out=dbg_outs["negacheck"][:, 1:2], in_=dtbA_sb[:, 1:2],
                          allow_slow_non_contiguous=True).then_inc(sy, 16)
            sy.n += 16
            dmay(dbg_outs["negacheck"][:, 2:T + 2], avt[:])
            dmay(dbg_outs["negacheck"][:, T + 2:2 * T + 2], dt2[:])
        if dbg_tensor("dtprep", [64, 4 * T]):
            E.wait(E.y, sA, sA.n)
            E.wait(E.y, sV, sV.n)
            dmay(dbg_outs["dtprep"][:, 0:T], dt2[:])
            dmay(dbg_outs["dtprep"][:, T:2 * T], acum[:])
            dmay(dbg_outs["dtprep"][:, 2 * T:3 * T], E2fm[:])
            dmay(dbg_outs["dtprep"][:, 3 * T:4 * T], epfm[:])
        E.wait(E.v, sA, sA.n)
        E.wait(E.v, sV, sV.n)
        v_epbf = sV.inc(E.v.tensor_copy(epbf[:], epfm[:]))
        E.wait(E.y, sV, v_epbf)
        dmay(d_ep[:, 0:T], epbf[:], scm)
    dtprep_G = sG.n
    dtprep_A = sA.n
    seg_dt.close()
    seg0c.close()

    phase_barrier()
    with contextlib.ExitStack() as stk:
        sb, ps = mk(stk)
        XBW = 34 * 131
        xbc_st = sb("xbc_st", [128, 2 * XBW], BF16)
        xb_sl = lambda par: xbc_st[:, par * XBW:(par + 1) * XBW]
        convoX = sb("convoX", [128, 32 * 128], BF16)
        convoBC = sb("convoBC", [128, 2 * 256], BF16)
        bc_sl = lambda par: convoBC[:, par * 256:(par + 1) * 256]
        cacc = sb("cacc", [128, CONV])
        csc = sb("csc", [128, CONV], BF16)
        Mt = sb("Mt", [128, 128], BF16)
        CROW = sb("CROW", [128, 32 * 128])
        Mtb2 = sb("Mtb2", [128, 2 * 32 * 128], BF16)
        mtb_sl = lambda b: Mtb2[:, b * 32 * 128:(b + 1) * 32 * 128]
        xs1 = sb("xs1", [128, 2 * HID], BF16)
        xs1_sl = lambda par: xs1[:, par * HID:(par + 1) * HID]
        xs2 = sb("xs2", [128, HID], BF16)
        Btm = sb("Btm", [128, 128], BF16)
        S = sb("S", [128, HID])
        Sbf = sb("Sbf", [128, 2 * HID], BF16)
        sbf_sl = lambda par: Sbf[:, par * HID:(par + 1) * HID]
        Erep = sb("Erep", [128, 32 * 128], BF16)
        y1sb = sb("y1sb", [128, HID], BF16)
        ysc = sb("ysc", [128, 512])
        ps_y = [ps(f"ps_y{i}", [128, 512]) for i in range(2)]
        ps_s = [ps(f"ps_s{i}", [128, 512]) for i in range(2)]
        ps_cb = ps("ps_cb", [128, 128])
        ps_tp = ps("ps_tp", [128, 128], BF16)
        ps_t4 = ps("ps_t4", [128, 512], BF16)

        E.wait(E.v, sG, dtprep_G)
        sV.inc(E.v.memset(S[:], 0.0))
        v_S0 = sV.inc(E.v.tensor_copy(sbf_sl(1), S[:]))

        def reload3(ci):
            par = ci % 2
            E.wait(E.g, sV, v_conv[par])
            if ci < 2:
                E.wait(E.g, szx[0], proj_spill_done[0])
                E.wait(E.g, szx[1], proj_spill_done[1])
            for piece in range(4):
                t0 = piece * 9
                t1 = min(34, t0 + 9)
                dmag(xb_sl(par).rearrange("p (t c) -> p t c", c=131)[:, t0:t1, :],
                     d_spill[32 + t0:32 + t1, :, ci * Q:ci * Q + 131]
                     .rearrange("t p c -> p t c"), sr3[par])
            reload_G[par] = sr3[par].n

        reload_G = [0, 0]
        v_conv = [v_S0, v_S0]
        a_cv_prev = 0
        conv_T = [0, 0]
        conv_TX = 0
        v_drep = 0
        mm_btm_T = 0
        v_mt_prev = 0
        tB_prev = 0
        xs1_T = [0, 0]
        sbf_T = [0, 0]
        a_sbf = [0, 0]
        mtb_T = [0, 0]
        v_sub_last = v_S0
        er_evV = 0
        hg_evV = [v_S0, v_S0]
        st_rdV = [v_S0, v_S0]
        reload3(0)

        for ci in range(nch):
            par = ci % 2
            if ci + 1 < nch:
                reload3(ci + 1)
            E.wait(E.v, sr3[par], reload_G[par])
            E.wait(E.v, sA, a_cv_prev)
            sl_in = xb_sl(par).rearrange("p (t c) -> p t c", c=131)
            wv3 = convw_sb[:].rearrange("p (t k) -> p t k", k=4)
            v_c = sV.inc(E.v.tensor_tensor(
                cacc[:].rearrange("p (t c) -> p t c", c=128),
                sl_in[:, :, 0:Q],
                wv3[:, 0:34, 0:1].broadcast_to([128, 34, Q]), ALU.mult))
            for kk in range(1, KCONV):
                E.wait(E.v, sV, v_c)
                sV.inc(E.v.tensor_tensor(
                    csc[:].rearrange("p (t c) -> p t c", c=128),
                    sl_in[:, :, kk:kk + Q],
                    wv3[:, 0:34, kk:kk + 1].broadcast_to([128, 34, Q]), ALU.mult))
                E.wait(E.v, sV, sV.n)
                v_c = sV.inc(E.v.tensor_add(cacc[:], cacc[:], csc[:]))
            v_conv[par] = v_c
            E.wait(E.s, sV, v_c)
            E.wait(E.s, sT, conv_TX)
            E.wait(E.s, sV, v_drep)
            a_cvx = sA.inc(E.s.activation(convoX[:], cacc[:, 0:32 * 128], AF.Silu))
            E.wait(E.s, sT, conv_T[par])
            a_cv = sA.inc(E.s.activation(bc_sl(par), cacc[:, 32 * 128:CONV], AF.Silu))
            a_cv_prev = a_cv
            E.wait(E.v, sy1s, sy1s.n)
            E.wait(E.v, sA, a_cvx)
            v_drep = sV.inc(E.v.tensor_tensor(
                y1sb[:].rearrange("p (h c) -> p h c", c=128),
                convoX[:].rearrange("p (h c) -> p h c", c=128),
                Drep_sb[:, 0:32].rearrange("p (h one) -> p h one", one=1)
                .broadcast_to([128, 32, 128]), ALU.mult))
            E.wait(E.t, sA, a_cv)
            E.wait(E.t, sV, tB_prev)
            tB = sT.inc(E.t.transpose(ps_tp[:], bc_sl(par)[:, 0:128], ident[:]))
            E.wait(E.v, sT, tB)
            E.wait(E.v, sT, mm_btm_T)
            tB_prev = sV.inc(E.v.tensor_copy(Btm[:], ps_tp[:]))
            E.wait(E.t, sV, max(tB_prev, v_mt_prev))
            t_cb = sT.inc(E.t.matmul(ps_cb[:], bc_sl(par)[:, 0:128],
                                     bc_sl(par)[:, 128:256], start=True, stop=True))
            E.wait(E.v, sT, t_cb)
            v_mt_prev = sV.inc(E.v.tensor_mul(Mt[:], ps_cb[:], tri[:]))
            vx = sV.n
            for g4 in range(8):
                E.wait(E.t, sV, vx)
                for q4 in range(4):
                    xt = g4 * 4 + q4
                    tx = sT.inc(E.t.transpose(ps_t4[:, q4 * 128:(q4 + 1) * 128],
                                              convoX[:, xt * 128:(xt + 1) * 128], ident[:]))
                E.wait(E.v, sT, tx)
                if g4 == 0:
                    E.wait(E.v, sT, mm_btm_T)
                    E.wait(E.v, sT, xs1_T[par])
                e1sl = E1t[:, ci * 64 + 8 * g4:ci * 64 + 8 * g4 + 8]
                e2sl = E2t[:, ci * 64 + 8 * g4:ci * 64 + 8 * g4 + 8]
                sV.inc(E.v.tensor_tensor(
                    xs1_sl(par)[:, g4 * 512:(g4 + 1) * 512].rearrange("p (h q) -> p h q", h=8),
                    ps_t4[:].rearrange("p (h q) -> p h q", h=8),
                    e1sl.rearrange("p (h one) -> p h one", one=1).broadcast_to([128, 8, 64]),
                    ALU.mult))
                vx = sV.inc(E.v.tensor_tensor(
                    xs2[:, g4 * 512:(g4 + 1) * 512].rearrange("p (h q) -> p h q", h=8),
                    ps_t4[:].rearrange("p (h q) -> p h q", h=8),
                    e2sl.rearrange("p (h one) -> p h one", one=1).broadcast_to([128, 8, 64]),
                    ALU.mult))
            conv_TX = tx
            xs_V = sV.n
            st_V = sV.n
            for sgi in range(8):
                pb2 = sgi % 2
                E.wait(E.t, sV, xs_V if sgi < 2 else st_V)
                E.wait(E.t, sV, hg_evV[pb2])
                mm_s = E.t.matmul(ps_s[pb2][:], Btm[:], xs2[:, sgi * 512:(sgi + 1) * 512],
                                  start=True, stop=True)
                t_s = sT.inc(mm_s)
                if sgi == 7:
                    mm_btm_T = t_s
                E.wait(E.v, sT, t_s)
                sV.inc(E.v.tensor_tensor(
                    S[:, sgi * 512:(sgi + 1) * 512].rearrange("p (h q) -> p h q", q=64),
                    S[:, sgi * 512:(sgi + 1) * 512].rearrange("p (h q) -> p h q", q=64),
                    decB[:, ci * 64 + sgi * 8:ci * 64 + sgi * 8 + 8]
                    .rearrange("p (h one) -> p h one", one=1).broadcast_to([128, 8, 64]),
                    ALU.mult))
                st_V = sV.inc(E.v.tensor_add(S[:, sgi * 512:(sgi + 1) * 512],
                                             S[:, sgi * 512:(sgi + 1) * 512],
                                             ps_s[pb2][:]))
                st_rdV[pb2] = st_V
            E.wait(E.s, sV, st_V)
            E.wait(E.s, sT, sbf_T[par])
            a_sbf[par] = sA.inc(E.s.activation(sbf_sl(par), S[:], AF.Identity))
            E.wait(E.g, sV, er_evV)
            dmag(Erep[0:64, :].rearrange("p (j2 i) -> p j2 i", i=Q),
                 d_ep[0:64:2, ci * Q:(ci + 1) * Q]
                 .unsqueeze(0).broadcast_to([64, 32, Q]))
            er_Y = dmag(Erep[64:128, :].rearrange("p (j2 i) -> p j2 i", i=Q),
                        d_ep[1:64:2, ci * Q:(ci + 1) * Q]
                        .unsqueeze(0).broadcast_to([64, 32, Q]))
            ev_V = sV.n
            sbprev = sbf_sl(1 - par)
            for hh in range(2):
                b = hh
                E.wait(E.y, sV, v_sub_last)
                g_pb = dmay(CROW[:].rearrange("p (h i) -> p h i", i=Q),
                            d_cum[hh * 32:(hh + 1) * 32, ci * Q:(ci + 1) * Q]
                            .unsqueeze(0).broadcast_to([128, 32, Q]), scm)
                E.wait(E.v, scm, g_pb)
                E.wait(E.v, sT, mtb_T[b])
                v_sub_last = sV.inc(E.v.tensor_tensor(
                    mtb_sl(b).rearrange("p (h i) -> p h i", i=Q),
                    CROW[:].rearrange("p (h i) -> p h i", i=Q),
                    cumT[:, ci * 64 + hh * 32:ci * 64 + hh * 32 + 32]
                    .rearrange("p (h one) -> p h one", one=1).broadcast_to([128, 32, Q]),
                    ALU.subtract))
                E.wait(E.s, sV, v_sub_last)
                sA.inc(E.s.activation(mtb_sl(b), mtb_sl(b), AF.Relu, scale=-1.0))
                a_E = sA.inc(E.s.activation(mtb_sl(b), mtb_sl(b), AF.Exp, scale=-1.0))
                E.wait(E.v, sA, a_E)
                v_mtb = sV.inc(E.v.tensor_tensor(
                    mtb_sl(b).rearrange("p (h i) -> p h i", i=Q),
                    mtb_sl(b).rearrange("p (h i) -> p h i", i=Q),
                    Mt[:].unsqueeze(1).broadcast_to([128, 32, Q]), ALU.mult))
                for hg in range(4):
                    pb = hg % 2
                    E.wait(E.t, sV, max(v_mtb, hg_evV[pb]))
                    if hh == 0 and hg < 2:
                        E.wait(E.t, sA, a_sbf[1 - par])
                        E.wait(E.t, sV, st_rdV[pb])
                    for j in range(4):
                        hpair = (hh * 4 + hg) * 4 + j
                        hl = hg * 8 + 2 * j
                        he, ho = 2 * hpair, 2 * hpair + 1
                        E.t.matmul(ps_y[pb][0:64, j * 128:(j + 1) * 128],
                                   xs1_sl(par)[:, he * 64:(he + 1) * 64],
                                   mtb_sl(b)[:, (hg * 8 + 2 * j) * 128:(hg * 8 + 2 * j + 1) * 128],
                                   start=True, stop=True)
                        E.t.matmul(ps_s[pb][0:64, j * 128:(j + 1) * 128],
                                   sbprev[:, he * 64:(he + 1) * 64],
                                   bc_sl(par)[:, 128:256], start=True, stop=True)
                        E.t.matmul(ps_y[pb][64:128, j * 128:(j + 1) * 128],
                                   xs1_sl(par)[:, ho * 64:(ho + 1) * 64],
                                   mtb_sl(b)[:, (hg * 8 + 2 * j + 1) * 128:(hg * 8 + 2 * j + 2) * 128],
                                   start=True, stop=True)
                        mmy = E.t.matmul(ps_s[pb][64:128, j * 128:(j + 1) * 128],
                                         sbprev[:, ho * 64:(ho + 1) * 64],
                                         bc_sl(par)[:, 128:256], start=True,
                                         stop=True)
                    t_y = sT.inc(mmy)
                    mtb_T[b] = t_y
                    xs1_T[par] = t_y
                    sbf_T[1 - par] = t_y
                    conv_T[par] = t_y
                    hq = (hh * 4 + hg) * 4
                    E.wait(E.v, sT, t_y)
                    if hh == 0 and hg == 0:
                        E.wait(E.v, sg, er_Y)
                    sV.inc(E.v.tensor_tensor(ysc[:], ps_s[pb][:],
                                             Erep[:, hq * 128:hq * 128 + 512], ALU.mult))
                    sV.inc(E.v.tensor_add(ysc[:], ysc[:], ps_y[pb][:]))
                    ev_V = sV.inc(E.v.tensor_add(y1sb[:, hq * 128:hq * 128 + 512],
                                                 y1sb[:, hq * 128:hq * 128 + 512],
                                                 ysc[:]))
                    hg_evV[pb] = ev_V
                    er_evV = ev_V
            if ci == 0 and dbg_tensor("ssm0", [4, 128, HID], BF16):
                E.wait(E.y, sV, sV.n)
                E.wait(E.y, sA, sA.n)
                dmay(dbg_outs["ssm0"][0, :, 0:128], Mt[:])
                dmay(dbg_outs["ssm0"][1], xs1_sl(par))
                dmay(dbg_outs["ssm0"][2], Erep[:])
                dmay(dbg_outs["ssm0"][3, :, 0:32 * 128], mtb_sl(1))
                E.wait(E.v, sy, sy.n)
                E.wait(E.s, sy, sy.n)
                E.wait(E.g, sy, sy.n)
            E.wait(E.g, sV, ev_V)
            dmag(d_y1[ci].rearrange("t p c -> p t c"),
                 y1sb[:].rearrange("p (t c) -> p t c", c=Q), sy1s)
        if dbg_tensor("Stot", [128, HID]):
            E.wait(E.y, sV, sV.n)
            dmay(dbg_outs["Stot"][:], S[:])
        if dbg_tensor("y1", [nch, 32, 128, Q], BF16):
            E.wait(E.y, sy1s, sy1s.n)
            for ci2 in range(nch):
                dmay(dbg_outs["y1"][ci2], d_y1[ci2])
        if LOCAL:
            dmay(Sinit[:], dSin[:])
            E.wait(E.v, sy, sy.n)
            v_sinit = sV.inc(E.v.tensor_scalar_mul(Sinit[:], Sinit[:], masks_sb[:, 1:2]))
        else:
            E.wait(E.v, sV, sV.n)
            E.wait(E.v, sA, sA.n)
            v_ari = sV.inc(E.v.tensor_scalar_mul(cacc[:, 0:HID], S[:], masks_sb[:, 2:3]))
            E.wait(E.y, sV, v_ari)
            dmay(d_arsi.ap()[:], cacc[:, 0:HID])
            E.wait(E.g, sy, sy.n)
            cc4 = sCC.inc(E.g.collective_compute(
                "AllReduce", ALU.add, replica_groups=[[0, 1], [2, 3], [4, 5], [6, 7]],
                ins=[d_arsi.ap().opt()], outs=[d_arso.ap().opt()]))
            E.wait(E.y, sCC, cc4)
            dmay(Sinit[:], d_arso.ap()[:])
            E.wait(E.v, sy, sy.n)
            v_sinit = sV.inc(E.v.tensor_scalar_mul(Sinit[:], Sinit[:], masks_sb[:, 1:2]))

    phase_barrier()
    seg45, sb45 = seg("left")
    rstd_rep = sb45("rstd_rep", [128, T])
    with contextlib.ExitStack() as stk:
        sb, ps = mk(stk)
        z_st = sb("z_st", [128, 2 * 32 * 256], BF16)
        z_sl = lambda par: z_st[:, par * 32 * 256:(par + 1) * 32 * 256]
        c_st = sb("c_st", [128, 2 * 259], BF16)
        c_sl = lambda par: c_st[:, par * 259:(par + 1) * 259]
        y1_st = sb("y1_st", [128, 2 * HID], BF16)
        y1_sl = lambda loc: y1_st[:, loc * HID:(loc + 1) * HID]
        siluz2 = sb("siluz2", [128, 2 * HID], BF16)
        sz_sl = lambda par: siluz2[:, par * HID:(par + 1) * HID]
        Sini_c = sb("Sini_c", [128, HID], BF16)
        ysum = sb("ysum", [128, HID])
        yzsq = sb("yzsq", [128, HID])
        ynbf = sb("ynbf", [128, HID], BF16)
        ssqrow = sb("ssqrow", [1, T])
        cacc2 = sb("cacc2", [128, Q])
        csc2 = sb("csc2", [128, Q])
        convC2 = sb("convC2", [128, 2 * Q], BF16)
        cC_sl = lambda par: convC2[:, par * Q:(par + 1) * Q]
        Erep42 = sb("Erep42", [128, 2 * 32 * 128], BF16)
        er_sl = lambda par: Erep42[:, par * 4096:(par + 1) * 4096]
        ysc2 = sb("ysc2", [128, 2048])
        ps_y = ps("p4_y", [128, 2048])
        ps_r = ps("p4_r", [1, 256])

        def reload4(pp):
            par = pp % 2
            E.wait(E.g, sA, a_z_pair[par])
            for piece in range(2):
                t0, t1 = piece * 16, piece * 16 + 16
                dmag(z_sl(par).rearrange("p (t c) -> p t c", c=256)[:, t0:t1, :],
                     d_spill[t0:t1, :, 3 + pp * 256:3 + pp * 256 + 256]
                     .rearrange("t p c -> p t c"), sr4[par])
            dmag(c_sl(par), d_spill[65, :, pp * 256:pp * 256 + 259], sr4[par])
            reload_G[par] = sr4[par].n

        reload_G = [0, 0]
        y1_Y = [0, 0]
        y1_rdV = [0, 0]
        a_z_pair = [0, 0]
        a_c_prev = 0
        conv_T4 = [0, 0]
        silz_V = [0, 0]
        sin_T = 0
        a_sq_prev = 0
        t_r_prev = 0
        row_cpV = [0, 0]
        er_evV4 = [0, 0]
        reload4(0)
        if npr > 1:
            reload4(1)
        for ci in range(nch):
            pp, local = divmod(ci, 2)
            par = pp % 2
            E.wait(E.y, sV, y1_rdV[local])
            y1_Y[local] = dmay(y1_sl(local).rearrange("p (t c) -> p t c", c=Q),
                               d_y1[ci].rearrange("t p c -> p t c"), sy1[local])
            base = local * Q
            E.wait(E.v, sr4[par], reload_G[par])
            E.wait(E.v, sA, a_c_prev)
            v_c = sV.inc(E.v.tensor_tensor(
                cacc2[:], c_sl(par)[:, base:base + Q],
                convw_sb[:, 33 * 4:33 * 4 + 1].broadcast_to([128, Q]), ALU.mult))
            for kk in range(1, KCONV):
                E.wait(E.v, sV, v_c)
                sV.inc(E.v.tensor_tensor(
                    csc2[:], c_sl(par)[:, base + kk:base + kk + Q],
                    convw_sb[:, 33 * 4 + kk:33 * 4 + kk + 1].broadcast_to([128, Q]),
                    ALU.mult))
                E.wait(E.v, sV, sV.n)
                v_c = sV.inc(E.v.tensor_add(cacc2[:], cacc2[:], csc2[:]))
            cpar = ci % 2
            E.wait(E.s, sV, v_c)
            E.wait(E.s, sT, conv_T4[cpar])
            a_c_prev = sA.inc(E.s.activation(cC_sl(cpar), cacc2[:], AF.Silu))
            a_c = a_c_prev
            E.wait(E.s, sV, silz_V[cpar])
            a_z = sA.inc(E.s.activation(
                sz_sl(cpar).rearrange("p (t c) -> p t c", c=Q),
                z_sl(par).rearrange("p (t c) -> p t c", c=256)[:, :, base:base + Q],
                AF.Silu))
            if local == 1:
                a_z_pair[par] = a_z
                if pp + 2 < npr:
                    reload4(pp + 2)
            E.wait(E.v, sT, sin_T)
            v_sc = sV.inc(E.v.tensor_tensor(
                Sini_c[:].rearrange("p (h q) -> p h q", q=64),
                Sinit[:].rearrange("p (h q) -> p h q", q=64),
                DprB[:, ci * 64:(ci + 1) * 64].rearrange("p (h one) -> p h one", one=1)
                .broadcast_to([128, 64, 64]), ALU.mult))
            epar = ci % 2
            er_sem = scm if epar == 0 else sxr
            E.wait(E.y, sV, er_evV4[epar])
            dmay(er_sl(epar)[0:64, :].rearrange("p (j2 i) -> p j2 i", i=Q),
                 d_ep[0:64:2, ci * Q:(ci + 1) * Q]
                 .unsqueeze(0).broadcast_to([64, 32, Q]), er_sem)
            er_Y = dmay(er_sl(epar)[64:128, :].rearrange("p (j2 i) -> p j2 i", i=Q),
                        d_ep[1:64:2, ci * Q:(ci + 1) * Q]
                        .unsqueeze(0).broadcast_to([64, 32, Q]), er_sem)
            E.wait(E.t, sA, a_c)
            E.wait(E.t, sV, v_sc)
            E.wait(E.v, syns, syns.n)
            ev_V = sV.n
            ev_q = [ev_V, ev_V]
            for qr in range(4):
                rp = qr % 2
                ro = rp * 1024
                E.wait(E.t, sV, ev_q[rp])
                for hg4 in range(2):
                    for j in range(4):
                        hpair = qr * 8 + hg4 * 4 + j
                        he, ho = 2 * hpair, 2 * hpair + 1
                        co = ro + hg4 * 512 + j * 128
                        E.t.matmul(ps_y[0:64, co:co + 128],
                                   Sini_c[:, he * 64:(he + 1) * 64], cC_sl(cpar),
                                   start=True, stop=True)
                        mmy = E.t.matmul(ps_y[64:128, co:co + 128],
                                         Sini_c[:, ho * 64:(ho + 1) * 64], cC_sl(cpar),
                                         start=True, stop=True)
                t_y = sT.inc(mmy)
                sin_T = t_y
                conv_T4[cpar] = t_y
                sl4 = slice(qr * 1024, qr * 1024 + 1024)
                E.wait(E.v, sT, t_y)
                if qr == 0:
                    E.wait(E.v, er_sem, er_Y)
                    E.wait(E.v, sy1[local], y1_Y[local])
                    E.wait(E.v, sA, max(a_z, a_sq_prev))
                sV.inc(E.v.tensor_tensor(ysc2[:, ro:ro + 1024], ps_y[:, ro:ro + 1024],
                                         er_sl(epar)[:, sl4], ALU.mult))
                sV.inc(E.v.tensor_add(ysum[:, sl4], ysc2[:, ro:ro + 1024],
                                      y1_sl(local)[:, sl4]))
                sV.inc(E.v.tensor_mul(ysum[:, sl4], ysum[:, sl4], sz_sl(cpar)[:, sl4]))
                ev_V = sV.inc(E.v.tensor_tensor(
                    ynbf[:, sl4].rearrange("p (j c) -> p j c", c=128),
                    ysum[:, sl4].rearrange("p (j c) -> p j c", c=128),
                    normw_sb[:, qr * 8:qr * 8 + 8]
                    .rearrange("p (j one) -> p j one", one=1)
                    .broadcast_to([128, 8, 128]), ALU.mult))
                ev_q[rp] = ev_V
                er_evV4[epar] = ev_V
            y1_rdV[local] = ev_V
            silz_V[cpar] = ev_V
            E.wait(E.s, sV, ev_V)
            E.wait(E.s, sT, t_r_prev)
            a_sq = sA.inc(E.s.activation(yzsq[:], ysum[:], AF.Square))
            a_sq_prev = a_sq
            E.wait(E.g, sV, ev_V)
            dmag(d_yn[:, :, ci * Q:(ci + 1) * Q].rearrange("t p c -> p t c"),
                 ynbf[:].rearrange("p (t c) -> p t c", c=Q), syns)
            E.wait(E.t, sA, a_sq)
            E.wait(E.t, sV, row_cpV[ci % 2])
            for xt in range(32):
                mm_r = E.t.matmul(ps_r[:, (ci % 2) * 128:(ci % 2 + 1) * 128], ones_col[:],
                                  yzsq[:, xt * 128:(xt + 1) * 128],
                                  start=(xt == 0), stop=(xt == 31))
            t_r = sT.inc(mm_r)
            t_r_prev = t_r
            E.wait(E.v, sT, t_r)
            row_cpV[ci % 2] = sV.inc(E.v.tensor_copy(
                ssqrow[:, ci * Q:(ci + 1) * Q],
                ps_r[:, (ci % 2) * 128:(ci % 2 + 1) * 128]))
        if dbg_tensor("yn", [32, 128, T], BF16):
            E.wait(E.y, syns, syns.n)
            for t0 in range(32):
                dmay(dbg_outs["yn"][t0], d_yn[t0])
        E.wait(E.v, sV, sV.n)
        sV.inc(E.v.tensor_scalar(ssqrow[:], ssqrow[:], 1.0 / HID, RMS_EPS_M,
                                 ALU.mult, ALU.add))
        E.wait(E.v, sV, sV.n)
        v_rs = sV.inc(E.v.reciprocal(ssqrow[:], ssqrow[:]))
        E.wait(E.s, sV, v_rs)
        a_rs = sA.inc(E.s.activation(ssqrow[:], ssqrow[:], AF.Sqrt))
        E.wait(E.y, sA, a_rs)
        g_rs = bounce_bcast(None, ssqrow[:], rstd_rep[:], T)
        yn_spill_Y = sy.n

    seg24.close()

    phase_barrier()
    seg56, sb56 = seg("right")
    rows2 = sb56("rows_ln2", [1, 2 * T])
    with contextlib.ExitStack() as stk:
        sb, ps = mk(stk)
        ynres = sb("ynres", [128, 32 * T], BF16)
        yn_sl = lambda k: ynres[:, k * T:(k + 1) * T]
        wo_sb = sb("wo_sb", [128, 2 * 32 * 128], BF16)
        wo_sl = lambda par, k: wo_sb[:, par * 32 * 128 + k * 128:par * 32 * 128 + (k + 1) * 128]
        x1sb = sb("x1sb", [128, T])
        x1sq = sb("x1sq", [128, T])
        xres = sb("xres", [128, 2 * 512])
        xr_sl = lambda xb: xres[:, xb * 512:(xb + 1) * 512]
        ps_m2 = [ps(f"ps_m{i}", [128, 512]) for i in range(2)]
        ps_l2 = ps("ps_l2", [33, T])
        E.wait(E.y, syns, syns.n)
        for t0 in range(32):
            dmay(yn_sl(t0), d_yn[t0])
        yn_loaded = sy.n
        mm_T5 = [0, 0]
        xr_V = [0, 0]
        ev_V5 = [0, 0]
        v_x1 = 0
        for m in range(16):
            par = m % 2
            E.wait(E.g, sT, mm_T5[par])
            g_l = dmag(wo_sb[:, par * 32 * 128:(par + 1) * 32 * 128]
                       .rearrange("p (a c) -> p a c", c=128),
                       dWout[m].rearrange("a p c -> p a c"), swo[par])
            E.wait(E.t, swo[par], g_l)
            if m == 0:
                E.wait(E.t, sy, yn_loaded)
            for pi, (coff, cnb) in enumerate(col_splits(T, 512)):
                pb = pi % 2
                E.wait(E.t, sV, ev_V5[pb])
                if pi == 0:
                    E.wait(E.t, sV, v_x1)
                    E.wait(E.t, sx1s, sx1s.n)
                for k in range(32):
                    mm = E.t.matmul(ps_m2[pb][:, 0:cnb], wo_sl(par, k),
                                    yn_sl(k)[:, coff:coff + cnb],
                                    start=(k == 0), stop=(k == 31), skip_group_check=True)
                t_m = sT.inc(mm)
                mm_T5[par] = t_m
                E.wait(E.y, sV, xr_V[pb])
                y_x = dmay(xr_sl(pb)[:, 0:cnb], dxT[m, :, 3 + coff:3 + coff + cnb], sxr)
                E.wait(E.v, sT, t_m)
                E.wait(E.v, sxr, y_x)
                E.wait(E.v, sy, g_rs)
                sV.inc(E.v.tensor_tensor(x1sq[:, coff:coff + cnb], ps_m2[pb][:, 0:cnb],
                                         rstd_rep[:, coff:coff + cnb], ALU.mult))
                E.wait(E.v, sV, sV.n)
                v_x1 = sV.inc(E.v.scalar_tensor_tensor(
                    x1sb[:, coff:coff + cnb], x1sq[:, coff:coff + cnb], ga_c(m),
                    xr_sl(pb)[:, 0:cnb], ALU.mult, ALU.add))
                ev_V5[pb] = v_x1
                xr_V[pb] = v_x1
            E.wait(E.y, sV, v_x1)
            dmay(d_x1[m], x1sb[:], sx1s)
            E.wait(E.s, sV, v_x1)
            a_sq = sA.inc(E.s.activation(x1sq[:], x1sb[:], AF.Square))
            E.wait(E.t, sA, a_sq)
            for (coff, cnb) in col_splits(T, 512):
                E.t.matmul(ps_l2[0:1, coff:coff + cnb], ones_col[:],
                           x1sb[:, coff:coff + cnb], start=(m == 0), stop=(m == 15),
                           skip_group_check=True)
                mm2 = E.t.matmul(ps_l2[32:33, coff:coff + cnb], ones_col[:],
                                 x1sq[:, coff:coff + cnb], start=(m == 0), stop=(m == 15),
                                 skip_group_check=True)
            t_l2 = sT.inc(mm2)
            slab_V[par] = sV.n
        x1_spill_Y = sx1s.n
        if dbg_tensor("x1", [NKT, 128, T]):
            E.wait(E.y, sx1s, x1_spill_Y)
            for m in range(16):
                dmay(dbg_outs["x1"][m], d_x1[m])
        m2row = rows2[:, 0:T]
        r2row = rows2[:, T:2 * T]
        E.wait(E.v, sT, sT.n)
        sV.inc(E.v.tensor_scalar_mul(m2row, ps_l2[0:1, 0:T], 1.0 / D_MODEL))
        E.wait(E.v, sV, sV.n)
        sV.inc(E.v.tensor_mul(r2row, m2row, m2row))
        E.wait(E.v, sV, sV.n)
        sV.inc(E.v.scalar_tensor_tensor(r2row, ps_l2[32:33, 0:T], 1.0 / D_MODEL, r2row,
                                        ALU.mult, ALU.subtract))
        E.wait(E.v, sV, sV.n)
        sV.inc(E.v.tensor_scalar_add(r2row, r2row, LN_EPS))
        E.wait(E.v, sV, sV.n)
        v_r2 = sV.inc(E.v.reciprocal(r2row, r2row))
        E.wait(E.s, sV, v_r2)
        a_r2 = sA.inc(E.s.activation(r2row, r2row, AF.Sqrt))

    seg45.close()

    phase_barrier()
    seg67, sb67 = seg("left")
    qx2 = sb67("qx2", [128, NKT * T], BF16)
    qx_sl = lambda k: qx2[:, k * T:(k + 1) * T]
    with contextlib.ExitStack() as stk:
        sb, ps = mk(stk)
        m2rep = sb("m2rep", [128, T])
        r2rep = sb("r2rep", [128, T])
        s2rep = sb("s2rep", [128, T])
        i2rep = sb("i2rep", [128, T])
        h2t = sb("h2t", [128, T])
        h2s = sb("h2s", [128, T])
        rmax2 = sb("rmax2", [128, T])
        s2tm = sb("s2tm", [128, NTB])
        i2tm = sb("i2tm", [128, NTB])
        xpose = sb("xpose", [128, 128])
        ps_tp = ps("p6_tp", [128, 128])
        E.wait(E.y, sA, sA.n)
        E.wait(E.y, sV, sV.n)
        bounce_bcast(None, rows2[:, 0:T], m2rep[:], T)
        g_r2 = bounce_bcast(None, rows2[:, T:2 * T], r2rep[:], T)
        rmin2 = sb("rmin2", [128, T])
        mn2tm = sb("mn2tm", [128, NTB])
        E.wait(E.v, sy, g_r2)
        sV.inc(E.v.memset(rmax2[:], 0.0))
        v_rm = sV.inc(E.v.memset(rmin2[:], 0.0))

        def ln2_tile(m, dest, v_guard):
            E.wait(E.y, sV, sV.n)
            y_x = dmay(h2s[:], d_x1[m])
            E.wait(E.v, sy, y_x)
            sV.inc(E.v.tensor_sub(h2t[:], h2s[:], m2rep[:]))
            vv = sV.inc(E.v.tensor_mul(h2t[:], h2t[:], r2rep[:]))
            E.wait(E.s, sV, vv)
            aa = sA.inc(E.s.activation(h2t[:], h2t[:], AF.Identity,
                                       bias=sb_c(m), scale=cb1_c(m)))
            return aa

        for m in range(16):
            aa = ln2_tile(m, None, v_rm)
            E.wait(E.v, sA, aa)
            sV.inc(E.v.tensor_tensor(rmax2[:], rmax2[:], h2t[:], ALU.max))
            v_rm = sV.inc(E.v.tensor_tensor(rmin2[:], rmin2[:], h2t[:], ALU.min))
        v_tp = v_rm
        for tb in range(NTB):
            E.wait(E.t, sV, v_tp)
            t_t = sT.inc(E.t.transpose(ps_tp[:], rmax2[:, tb * 128:(tb + 1) * 128], identf[:]))
            E.wait(E.v, sT, t_t)
            v_tp = sV.inc(E.v.reduce_max(s2tm[:, tb:tb + 1], ps_tp[:], AX))
            E.wait(E.t, sV, v_tp)
            t_m2 = sT.inc(E.t.transpose(ps_tp[:], rmin2[:, tb * 128:(tb + 1) * 128], identf[:]))
            E.wait(E.v, sT, t_m2)
            v_tp = sV.inc(E.v.tensor_reduce(mn2tm[:, tb:tb + 1], ps_tp[:], AX, ALU.min, negate=True))
        E.wait(E.v, sV, v_tp)
        sV.inc(E.v.tensor_tensor(s2tm[:], s2tm[:], mn2tm[:], ALU.max))
        E.wait(E.v, sV, sV.n)
        sV.inc(E.v.tensor_scalar_max(s2tm[:], s2tm[:], QEPS))
        E.wait(E.v, sV, sV.n)
        sV.inc(E.v.tensor_scalar_mul(i2tm[:], s2tm[:], 1.0 / 127.0))
        E.wait(E.v, sV, sV.n)
        v_s2 = sV.inc(E.v.reciprocal(s2tm[:], i2tm[:]))
        for (tmt, rep) in ((s2tm, s2rep), (i2tm, i2rep)):
            E.wait(E.t, sV, sV.n)
            t_b = sT.inc(E.t.transpose(ps_tp[0:NTB, 0:128], tmt[:, 0:NTB], identf[:]))
            E.wait(E.v, sT, t_b)
            E.wait(E.v, sy, sy.n)
            v_b = sV.inc(E.v.tensor_copy(xpose[0:NTB, 0:128], ps_tp[0:NTB, 0:128]))
            E.wait(E.y, sV, v_b)
            E.wait(E.y, sy, sy.n)
            dmay(d_rowb[0:NTB], xpose[0:NTB, 0:128], srow)
            E.wait(E.y, sy, sy.n)
            E.wait(E.y, srow, srow.n)
            g_b = dmay(rep[:], d_rowb.rearrange("a b -> (a b)")[0:T]
                       .unsqueeze(0).broadcast_to([128, T]))
        rep_G = sy.n
        v_q = sV.n
        for m in range(16):
            aa = ln2_tile(m, None, v_q)
            E.wait(E.v, sA, aa)
            E.wait(E.v, sy, rep_G)
            sV.inc(E.v.tensor_mul(h2t[:], h2t[:], s2rep[:]))
            E.wait(E.v, sV, sV.n)
            sV.inc(E.v.tensor_scalar(h2t[:], h2t[:], MAGIC, MAGIC, ALU.add, ALU.subtract))
            E.wait(E.v, sV, sV.n)
            sV.inc(E.v.tensor_scalar(h2t[:], h2t[:], -128.0, 127.0, ALU.max, ALU.min))
            E.wait(E.v, sV, sV.n)
            v_q = sV.inc(E.v.tensor_mul(qx_sl(m), h2t[:], i2rep[:]))
        qx2_V = sV.n
        if dbg_tensor("qx2", [NKT, 128, T], BF16):
            E.wait(E.y, sV, qx2_V)
            for m in range(16):
                dmay(dbg_outs["qx2"][m], qx_sl(m))

    seg56.close()

    phase_barrier()
    seg7b, sb7b = seg("right")
    rmax3 = sb7b("rmax3", [128, T])
    rmin3 = sb7b("rmin3", [128, T])
    ssq3 = sb7b("ssq3", [128, T])
    with contextlib.ExitStack() as stk:
        sb, ps = mk(stk)
        wg_sb = sb("wg_sb", [128, 2 * NKT * 128], BF16)
        wu_sb = sb("wu_sb", [128, 2 * NKT * 128], BF16)
        wg_sl = lambda par, k: wg_sb[:, par * NKT * 128 + k * 128:par * NKT * 128 + (k + 1) * 128]
        wu_sl = lambda par, k: wu_sb[:, par * NKT * 128 + k * 128:par * NKT * 128 + (k + 1) * 128]
        hmid = sb("hmid", [128, T], BF16)
        gsc = sb("gsc", [128, T])
        silug = sb("silug", [128, T])
        hsq = sb("hsq", [128, T])
        ps_g2 = [ps(f"ps_g{i}", [128, 1024]) for i in range(2)]
        ps_u2 = [ps(f"ps_u{i}", [128, 1024]) for i in range(2)]
        E.wait(E.v, sV, qx2_V)
        sV.inc(E.v.memset(rmax3[:], 0.0))
        sV.inc(E.v.memset(rmin3[:], 0.0))
        v_f0 = sV.inc(E.v.memset(ssq3[:], 0.0))
        E.wait(E.s, sV, v_f0)
        E.wait(E.g, sCC, cc3)
        pieces = col_splits(T, 512)
        groups = [pieces[i:i + 2] for i in range(0, len(pieces), 2)]
        mm_T = [0, 0]
        ev_Vp = [v_f0, v_f0]
        ev_Ap = [0, 0]
        hm_y = 0
        for it in range(64):
            par = it % 2
            E.wait(E.y, sT, mm_T[par])
            r, loc = divmod(it, 8)
            g_lg = dmay(wg_sb[:, par * NKT * 128:(par + 1) * NKT * 128],
                        d_agqo.ap()[r, 0][:, loc * 2048:(loc + 1) * 2048], sg7[par])
            g_lu = dmay(wu_sb[:, par * NKT * 128:(par + 1) * NKT * 128],
                        d_agqo.ap()[r, 1][:, loc * 2048:(loc + 1) * 2048], shm[par])
            E.wait(E.t, sg7[par], g_lg)
            E.wait(E.t, shm[par], g_lu)
            v_h = 0
            for gi, grp in enumerate(groups):
                pb = gi % 2
                g0 = grp[0][0]
                gw = sum(n for _, n in grp)
                E.wait(E.t, sV, ev_Vp[pb])
                E.wait(E.t, sA, ev_Ap[pb])
                for (coff, cnb) in grp:
                    po = coff - g0
                    for k in range(NKT):
                        E.t.matmul(ps_g2[pb][:, po:po + cnb], wg_sl(par, k),
                                   qx_sl(k)[:, coff:coff + cnb],
                                   start=(k == 0), stop=(k == NKT - 1),
                                   skip_group_check=True)
                    for k in range(NKT):
                        mmu = E.t.matmul(ps_u2[pb][:, po:po + cnb], wu_sl(par, k),
                                         qx_sl(k)[:, coff:coff + cnb],
                                         start=(k == 0), stop=(k == NKT - 1),
                                         skip_group_check=True)
                t_gu = sT.inc(mmu)
                mm_T[par] = t_gu
                E.wait(E.s, sT, t_gu)
                a_g = sA.inc(E.s.activation(silug[:, g0:g0 + gw], ps_g2[pb][:, 0:gw],
                                            AF.Silu, scale=swB[:, 0:1]))
                E.wait(E.v, sA, a_g)
                if gi == 0:
                    E.wait(E.v, sy1s, hm_y)
                v_h = sV.inc(E.v.scalar_tensor_tensor(
                    hmid[:, g0:g0 + gw], ps_u2[pb][:, 0:gw], swB[:, 1:2],
                    silug[:, g0:g0 + gw], ALU.mult, ALU.mult))
                E.wait(E.s, sV, v_h)
                sA.inc(E.s.activation(gsc[:, g0:g0 + gw], hmid[:, g0:g0 + gw],
                                      AF.Identity, scale=lnw_sb[:, it:it + 1]))
                a_sq2 = sA.inc(E.s.activation(hsq[:, g0:g0 + gw], hmid[:, g0:g0 + gw],
                                              AF.Square))
                ev_Ap[pb] = a_g
                E.wait(E.v, sA, a_sq2)
                sV.inc(E.v.tensor_add(ssq3[:, g0:g0 + gw], ssq3[:, g0:g0 + gw],
                                      hsq[:, g0:g0 + gw]))
                sV.inc(E.v.tensor_tensor(rmax3[:, g0:g0 + gw], rmax3[:, g0:g0 + gw],
                                         gsc[:, g0:g0 + gw], ALU.max))
                ev_Vp[pb] = sV.inc(E.v.tensor_tensor(rmin3[:, g0:g0 + gw],
                                                     rmin3[:, g0:g0 + gw],
                                                     gsc[:, g0:g0 + gw], ALU.min))
            E.wait(E.g, sV, v_h)
            hm_y = dmag(d_hmid[it], hmid[:], sy1s)
        ffn1_G = sG.n
        ffn1_V = sV.n
        hmid_Y = sy.n

    seg67.close()

    phase_barrier()
    seg78, sb78 = seg("left")
    rs3rep = sb78("rs3rep", [128, T])
    cmb_rep = sb78("cmb_rep", [128, T])
    with contextlib.ExitStack() as stk:
        sb, ps = mk(stk)
        s3tm = sb("s3tm", [128, NTB])
        r3tm = sb("r3tm", [128, NTB])
        xpose = sb("xpose7", [128, 128])
        ps_tp = ps("p7_tp", [128, 128])
        mn3tm = sb("mn3tm", [128, NTB])
        v_tp = ffn1_V
        for tb in range(NTB):
            E.wait(E.t, sV, v_tp)
            E.wait(E.t, sG, ffn1_G)
            t_t = sT.inc(E.t.transpose(ps_tp[:], rmax3[:, tb * 128:(tb + 1) * 128], identf[:]))
            E.wait(E.v, sT, t_t)
            v_tp = sV.inc(E.v.reduce_max(s3tm[:, tb:tb + 1], ps_tp[:], AX))
            E.wait(E.t, sV, v_tp)
            t_m2 = sT.inc(E.t.transpose(ps_tp[:], rmin3[:, tb * 128:(tb + 1) * 128], identf[:]))
            E.wait(E.v, sT, t_m2)
            v_tp = sV.inc(E.v.tensor_reduce(mn3tm[:, tb:tb + 1], ps_tp[:], AX, ALU.min, negate=True))
            E.wait(E.t, sV, v_tp)
            t_t2 = sT.inc(E.t.transpose(ps_tp[:], ssq3[:, tb * 128:(tb + 1) * 128], identf[:]))
            E.wait(E.v, sT, t_t2)
            v_tp = sV.inc(E.v.tensor_reduce(r3tm[:, tb:tb + 1], ps_tp[:], AX, ALU.add))
        E.wait(E.v, sV, v_tp)
        sV.inc(E.v.tensor_tensor(s3tm[:], s3tm[:], mn3tm[:], ALU.max))
        E.wait(E.v, sV, sV.n)
        sV.inc(E.v.tensor_scalar(r3tm[:], r3tm[:], 1.0 / I_FFN, RMS_EPS_F, ALU.mult, ALU.add))
        E.wait(E.v, sV, sV.n)
        v_r3 = sV.inc(E.v.reciprocal(r3tm[:], r3tm[:]))
        E.wait(E.s, sV, v_r3)
        a_r3 = sA.inc(E.s.activation(r3tm[:], r3tm[:], AF.Sqrt))
        i3tm = sb("i3tm", [128, NTB])
        E.wait(E.v, sA, a_r3)
        sV.inc(E.v.tensor_mul(s3tm[:], s3tm[:], r3tm[:]))
        sV.inc(E.v.tensor_scalar_max(s3tm[:], s3tm[:], QEPS))
        sV.inc(E.v.tensor_scalar_mul(i3tm[:], s3tm[:], 1.0 / 127.0))
        sV.inc(E.v.reciprocal(s3tm[:], i3tm[:]))
        v_rs3 = sV.inc(E.v.tensor_mul(s3tm[:], s3tm[:], r3tm[:]))
        for (tmt, rep) in ((s3tm, rs3rep), (i3tm, cmb_rep)):
            E.wait(E.t, sV, sV.n)
            t_b = sT.inc(E.t.transpose(ps_tp[0:NTB, 0:128], tmt[:, 0:NTB], identf[:]))
            E.wait(E.v, sT, t_b)
            E.wait(E.v, sy, sy.n)
            v_b = sV.inc(E.v.tensor_copy(xpose[0:NTB, 0:128], ps_tp[0:NTB, 0:128]))
            E.wait(E.y, sV, v_b)
            E.wait(E.y, sy, sy.n)
            dmay(d_rowb[0:NTB], xpose[0:NTB, 0:128], srow)
            E.wait(E.y, sy, sy.n)
            E.wait(E.y, srow, srow.n)
            g_b = dmay(rep[:], d_rowb.rearrange("a b -> (a b)")[0:T]
                       .unsqueeze(0).broadcast_to([128, T]))
        E.wait(E.v, sy, sy.n)
        v_cmb = sV.inc(E.v.tensor_scalar_mul(cmb_rep[:], cmb_rep[:], swB[:, 2:3]))
        if dbg_tensor("ffnrows", [2, 128, T]):
            E.wait(E.y, sV, sV.n)
            dmay(dbg_outs["ffnrows"][0], rs3rep[:])
            dmay(dbg_outs["ffnrows"][1], cmb_rep[:])

    seg7b.close()

    phase_barrier()
    with contextlib.ExitStack() as stk:
        sb, ps = mk(stk)
        n_tq = ceil_div(T, 512)
        n_tqp = ceil_div(n_tq, 2)
        qh2 = sb("qh2", [128, min(2, n_tq) * 64 * 512], BF16)
        qh_sl = lambda ti, k: qh2[:, (ti * 64 + k) * 512:(ti * 64 + k + 1) * 512]
        wd_sb = sb("wd_sb", [128, 2 * 512], BF16)
        wd_sl = lambda par, m: wd_sb[:, par * 512 + m * 128:par * 512 + (m + 1) * 128]
        hm_st = sb("hm_st", [128, 4 * 1024], BF16)
        hm_sl = lambda s: hm_st[:, s * 1024:(s + 1) * 1024]
        x1_st = sb("x1_st", [128, 512])
        outsb = sb("outsb", [128, 2 * 512])
        o_sl = lambda par: outsb[:, par * 512:(par + 1) * 512]
        qsc = sb("qsc", [128, 4 * 1024])
        qs_sl = lambda s: qsc[:, s * 1024:(s + 1) * 1024]
        ps_d = [ps(f"ps_d{i}", [128, 512]) for i in range(8)]
        aq_s = [0, 0, 0, 0]
        qv_s = [0, 0, 0, 0]
        ev_ps = [0] * 8
        wd_T = [0, 0]
        dn_V = sV.n
        dn_T = sT.n
        qh_T = sT.n
        for tqp in range(n_tqp):
            tqs = [t for t in (2 * tqp, 2 * tqp + 1) if t < n_tq]
            for mq in range(4):
                for k in range(64):
                    par = k % 2
                    if mq == 0 and par == 0:
                        for ti, tq in enumerate(tqs):
                            s = ti * 2 + (k // 2) % 2
                            coff = tq * 512
                            cnb = min(512, T - coff)
                            hm3 = hm_sl(s).rearrange("p (j c) -> p j c", c=512)
                            qs3 = qs_sl(s).rearrange("p (j c) -> p j c", c=512)
                            E.wait(E.y, sA, aq_s[s])
                            E.wait(E.y, sT, qh_T)
                            y_h = dmay(hm3[:, :, 0:cnb],
                                       d_hmid[k:k + 2, :, coff:coff + cnb]
                                       .rearrange("j p c -> p j c"), shm[(k // 2) % 2])
                            E.wait(E.s, shm[(k // 2) % 2], y_h)
                            E.wait(E.s, sV, qv_s[s])
                            for dk in range(2):
                                aq = sA.inc(E.s.activation(
                                    qs_sl(s)[:, dk * 512:dk * 512 + cnb],
                                    hm_sl(s)[:, dk * 512:dk * 512 + cnb],
                                    AF.Identity, scale=lnw_sb[:, k + dk:k + dk + 1]))
                            aq_s[s] = aq
                            E.wait(E.v, sA, aq)
                            E.wait(E.v, sT, qh_T)
                            sV.inc(E.v.tensor_tensor(
                                qs3[:, :, 0:cnb], qs3[:, :, 0:cnb],
                                rs3rep[:, coff:coff + cnb].unsqueeze(1)
                                .broadcast_to([128, 2, cnb]), ALU.mult))
                            sV.inc(E.v.tensor_scalar(
                                qs3[:, :, 0:cnb], qs3[:, :, 0:cnb],
                                MAGIC, MAGIC, ALU.add, ALU.subtract))
                            qv_s[s] = sV.inc(E.v.tensor_scalar(
                                qh2[:, (ti * 64 + k) * 512:(ti * 64 + k + 2) * 512]
                                .rearrange("p (j c) -> p j c", c=512)[:, :, 0:cnb],
                                qs3[:, :, 0:cnb],
                                -128.0, 127.0, ALU.max, ALU.min))
                        qv_k = qv_s[(len(tqs) - 1) * 2 + (k // 2) % 2]
                    r, loc = divmod(k, 8)
                    wsrc = d_agqo.ap()[r, 2][:, loc * 2048 + mq * 512:
                                             loc * 2048 + mq * 512 + 512]
                    if par == 0:
                        E.wait(E.g, sT, wd_T[0])
                        g_l = dmag(wd_sb[:, 0:512], wsrc, swd[0])
                        E.wait(E.t, swd[0], g_l)
                    else:
                        E.wait(E.y, sT, wd_T[1])
                        g_l = dmay(wd_sb[:, 512:1024], wsrc, swd[1])
                        E.wait(E.t, swd[1], g_l)
                    if mq == 0:
                        E.wait(E.t, sV, qv_k)
                    for m4 in range(4):
                        for ti, tq in enumerate(tqs):
                            cnb = min(512, T - tq * 512)
                            if k == 0 and (mq or tqp):
                                E.wait(E.t, sV, ev_ps[ti * 4 + m4])
                            mm = E.t.matmul(ps_d[ti * 4 + m4][:, 0:cnb],
                                            wd_sl(par, m4), qh_sl(ti, k)[:, 0:cnb],
                                            start=(k == 0), stop=(k == 63),
                                            skip_group_check=True)
                    dn_T = sT.inc(mm)
                    wd_T[par] = dn_T
                for m4 in range(4):
                    gm = mq * 4 + m4
                    for ti, tq in enumerate(tqs):
                        coff = tq * 512
                        cnb = min(512, T - coff)
                        opar = (m4 + ti) % 2
                        E.wait(E.s, sV, dn_V)
                        E.s.dma_start(out=x1_st[:, 0:cnb],
                                      in_=d_x1[gm, :, coff:coff + cnb]).then_inc(sx8, 16)
                        sx8.n += 16
                        y_x = sx8.n
                        E.wait(E.v, sT, dn_T)
                        E.wait(E.v, sx8, y_x)
                        E.wait(E.v, sout, sout.n)
                        sV.inc(E.v.tensor_tensor(o_sl(opar)[:, 0:cnb],
                                                 ps_d[ti * 4 + m4][:, 0:cnb],
                                                 cmb_rep[:, coff:coff + cnb], ALU.mult))
                        v_o = sV.inc(E.v.scalar_tensor_tensor(
                            o_sl(opar)[:, 0:cnb], o_sl(opar)[:, 0:cnb], gb_c(gm),
                            x1_st[:, 0:cnb], ALU.mult, ALU.add))
                        ev_ps[ti * 4 + m4] = v_o
                        E.wait(E.s, sV, v_o)
                        E.s.dma_start(out=dout[gm, :, coff:coff + cnb],
                                      in_=o_sl(opar)[:, 0:cnb]).then_inc(sout, 16)
                        sout.n += 16
                        dn_V = v_o
            qh_T = dn_T
        for s_ in ALLQ:
            E.wait(E.y, s_, s_.n)
            E.wait(E.g, s_, s_.n)

    seg78.close()
    with nc.Block() as block:
        E.replay(block)
    root.close()
    return nc, sorted(dbg_outs.keys())



def stage_inputs(inputs, nch=16, mode="full", sinit_list=None):
    T = nch * Q
    local = (mode == "local")
    f32 = lambda k: np.ascontiguousarray(np.asarray(inputs[k], dtype=np.float32))
    x = f32("L0_attn_x")
    W_in = f32("L0_attn_W_in")
    conv_w = f32("L0_attn_conv_w")
    dt_bias = f32("L0_attn_dt_bias")
    A_log = f32("L0_attn_A_log")
    D = f32("L0_attn_D")
    norm_w = f32("L0_attn_norm_w")
    W_out = f32("L0_attn_W_out")
    w_gate = f32("L0_ffn_w_gate")
    w_up = f32("L0_ffn_w_up")
    w_down = f32("L0_ffn_w_down")
    ln_w = f32("L0_ffn_ln_weight")
    c = f32("L0_c")
    wmod = f32("L0_wmod")

    WTp = np.zeros((D_MODEL, NOT * 128), dtype=np.float32)
    WTp[:, :DIN] = W_in.T
    WinT = np.ascontiguousarray(WTp.reshape(NKT, 128, NOT, 128).transpose(2, 0, 1, 3))
    WoutT = np.ascontiguousarray(W_out.T.reshape(32, 128, 16, 128).transpose(2, 0, 1, 3))
    gflat = np.ascontiguousarray(
        w_gate.T.reshape(NKT, 128, 64, 128).transpose(2, 0, 1, 3)).reshape(-1)
    uflat = np.ascontiguousarray(
        w_up.T.reshape(NKT, 128, 64, 128).transpose(2, 0, 1, 3)).reshape(-1)
    dflat = np.ascontiguousarray(
        w_down.T.reshape(64, 128, 16, 128).transpose(0, 2, 1, 3)).reshape(-1)
    nsl = gflat.size // 8

    convw = np.ascontiguousarray(conv_w.reshape(34, 128, 4).transpose(1, 0, 2).reshape(128, 136))
    dtbA = np.ascontiguousarray(np.stack([dt_bias, A_log], axis=1))
    Drep = np.ascontiguousarray(
        D[2 * np.arange(32)[None, :] + (np.arange(128)[:, None] // 64)])
    normw = np.ascontiguousarray(norm_w.reshape(32, 128).T)
    lnw = np.ascontiguousarray(ln_w.reshape(64, 128).T)
    cT = np.ascontiguousarray(c.T)

    in_maps = []
    for core in range(8):
        b, h = divmod(core, 2)
        start = h * T
        if start - 3 >= 0:
            pre = x[b, start - 3:start]
        else:
            pre = x[b, 0:3]
        xs = np.concatenate([pre, x[b, start:start + T]], axis=0)
        xT = np.ascontiguousarray(xs.T.reshape(NKT, 128, T + 3))
        nmt = 96 if local else 12
        cols = np.empty(128 * nmt, dtype=np.int64)
        for mt in range(nmt):
            g = mt if local else core * 12 + mt
            v, t = divmod(g, 16)
            cols[mt * 128:(mt + 1) * 128] = v * D_MODEL + t * 128 + np.arange(128)
        wms = wmod[:, cols]
        wmod_s = np.ascontiguousarray(wms.reshape(NKT, 128, nmt, 128).transpose(2, 0, 1, 3))
        nslabs = 64 if local else 8
        def _pm(flat):
            sl = flat if local else flat[core * nsl:(core + 1) * nsl]
            return sl.reshape(nslabs, 16, 128, 128).transpose(2, 0, 1, 3).reshape(128, -1)
        wq_slice = np.ascontiguousarray(np.stack([_pm(gflat), _pm(uflat), _pm(dflat)]))
        masks = np.zeros((128, 8), dtype=np.float32)
        masks[:, 0] = 1.0 if h == 1 else 0.0
        masks[:, 1] = 1.0 if h == 1 else 0.0
        masks[:, 2] = 1.0 if h == 0 else 0.0
        masks[:, 3 + b] = 1.0
        masks[:, 7] = 1.0
        im = {
            "xT": xT, "WinT": WinT, "WoutT": WoutT, "wq_slice": wq_slice,
            "wmod_s": wmod_s, "cT": cT, "convw": convw, "dtbA": dtbA,
            "Drep": Drep, "normw": normw, "lnw": lnw, "masks": masks,
        }
        if local:
            im["SinitT"] = (np.zeros((128, HID), dtype=np.float32)
                            if sinit_list is None else
                            np.ascontiguousarray(sinit_list[core].astype(np.float32)))
        in_maps.append(im)
    return in_maps


def kernel(**inputs):
    nch = 16
    in_maps = stage_inputs(inputs, nch)
    nc, _ = build_program(nch)
    res = run_bass_kernel_spmd(nc, in_maps, core_ids=list(range(8)))
    T = nch * Q
    out = np.empty((4, 4096, 2048), dtype=np.float32)
    for core in range(8):
        b, h = divmod(core, 2)
        o = np.asarray(res.results[core]["out"])
        out[b, h * T:(h + 1) * T, :] = o.reshape(D_MODEL, T).T
    return out


# version 22
# speedup vs baseline: 1.1365x; 1.1365x over previous
import contextlib
import numpy as np
import concourse.bass as bass
import concourse.mybir as mybir
from concourse.bass_utils import run_bass_kernel_spmd
from concourse import library_config

F32 = mybir.dt.float32
BF16 = mybir.dt.bfloat16
ALU = mybir.AluOpType
AF = mybir.ActivationFunctionType
AX = mybir.AxisListType.X

MAGIC = 12582912.0
D_MODEL = 2048
HID = 4096
H = 64
P = 64
NSTATE = 128
KCONV = 4
Q = 128
CONV = 4352
DIN = 8512
I_FFN = 8192
NKT = 16
NOT = 67
LN_EPS = 1e-5
RMS_EPS_M = 1e-5
RMS_EPS_F = 1e-6
QEPS = 1e-5

ENGINES = ("tensor", "vector", "scalar", "gpsimd", "sync")


class _Call:
    __slots__ = ("op", "args", "kwargs", "incs", "auto_sem")

    def __init__(self, op, args, kwargs):
        self.op, self.args, self.kwargs = op, args, kwargs
        self.incs = []
        self.auto_sem = None

    def then_inc(self, sem, k):
        self.incs.append((sem, k))
        return self


_NOAUTO = {"wait_ge", "load_library", "collective_compute", "dma_start",
           "dma_start_transpose", "sem_inc", "engine_nop", "nop"}
_AUTO_ENGINES = {"vector", "scalar", "gpsimd"}


class _EngRec:
    def __init__(self, em, name):
        object.__setattr__(self, "_em", em)
        object.__setattr__(self, "_name", name)

    def __getattr__(self, op):
        em, name = self._em, self._name

        def call(*a, **k):
            auto = name in _AUTO_ENGINES and op not in _NOAUTO
            if auto:
                sem = em.auto_sems.get(name)
                if sem is not None and sem.n > 0:
                    em.streams[name].append(_Call("wait_ge", (sem.h, sem.n), {}))
            c = _Call(op, a, k)
            em.streams[name].append(c)
            if auto and sem is not None:
                c.incs.append((sem, 1))
                sem.n += 1
                c.auto_sem = sem
            return c
        return call


class Sem:
    def __init__(self, handle, queue=False):
        self.h = handle
        self.n = 0
        self.queue = queue

    def inc(self, call, k=1):
        if getattr(call, "auto_sem", None) is self:
            return self.n
        call.then_inc(self, k)
        self.n += k
        return self.n


class Emitter:
    def __init__(self):
        self.streams = {e: [] for e in ENGINES}
        self.auto_sems = {}
        self.t = _EngRec(self, "tensor")
        self.v = _EngRec(self, "vector")
        self.s = _EngRec(self, "scalar")
        self.g = _EngRec(self, "gpsimd")
        self.y = _EngRec(self, "sync")

    def wait(self, eng_rec, sem, val=None):
        v = sem.n if val is None else val
        assert v <= sem.n, "wait on future semaphore value"
        if sem.queue and 0 < v < sem.n:
            v = sem.n
        if v > 0:
            eng_rec.wait_ge(sem.h, v)

    def replay(self, block):
        def make(name):
            def body(eng):
                for idx, c in enumerate(self.streams[name]):
                    try:
                        instr = getattr(eng, c.op)(*c.args, **c.kwargs)
                    except Exception:
                        import sys as _s
                        print(f"REPLAY FAIL {name}[{idx}] op={c.op}", file=_s.stderr)
                        raise
                    for (sem, k) in c.incs:
                        instr.then_inc(sem.h, k)
            return body
        for name in ENGINES:
            if self.streams[name]:
                getattr(block, name)(make(name))


def ceil_div(a, b):
    return (a + b - 1) // b


def col_splits(total, maxn=512):
    out = []
    off = 0
    while off < total:
        n = min(maxn, total - off)
        out.append((off, n))
        off += n
    return out


def build_program(nch=16, dbg=(), mode="full"):
    T = nch * Q
    TC = T + 3
    assert nch % 2 == 0
    npr = nch // 2
    NTB = ceil_div(T, 128)
    LOCAL = (mode == "local")
    NMT = 96 if LOCAL else 12

    nc = bass.Bass("TRN2", target_bir_lowering=False)
    E = Emitter()
    dbg = set(dbg)

    dxT = nc.dram_tensor("xT", [NKT, 128, TC], F32, kind="ExternalInput").ap()
    dWin = nc.dram_tensor("WinT", [NOT, NKT, 128, 128], F32, kind="ExternalInput").ap()
    dWout = nc.dram_tensor("WoutT", [16, 32, 128, 128], F32, kind="ExternalInput").ap()
    nwsl = 64 if LOCAL else 8
    dwq = nc.dram_tensor("wq_slice", [3, 128, nwsl * 2048], F32, kind="ExternalInput").ap()
    dwms = nc.dram_tensor("wmod_s", [NMT, NKT, 128, 128], F32, kind="ExternalInput").ap()
    dSin = nc.dram_tensor("SinitT", [128, HID], F32, kind="ExternalInput").ap() if LOCAL else None
    dcT = nc.dram_tensor("cT", [D_MODEL, 4], F32, kind="ExternalInput").ap()
    dconvw = nc.dram_tensor("convw", [128, 136], F32, kind="ExternalInput").ap()
    ddtbA = nc.dram_tensor("dtbA", [64, 2], F32, kind="ExternalInput").ap()
    dDrep = nc.dram_tensor("Drep", [128, 32], F32, kind="ExternalInput").ap()
    dnormw = nc.dram_tensor("normw", [128, 32], F32, kind="ExternalInput").ap()
    dlnw = nc.dram_tensor("lnw", [128, 64], F32, kind="ExternalInput").ap()
    dmasks = nc.dram_tensor("masks", [128, 8], F32, kind="ExternalInput").ap()
    dout = nc.dram_tensor("out", [NKT, 128, T], F32, kind="ExternalOutput").ap()

    d_spill = nc.dram_tensor("spill_zx", [NOT, 128, TC], BF16).ap()
    d_y1 = nc.dram_tensor("spill_y1", [nch, 32, 128, Q], BF16).ap()
    d_yn = nc.dram_tensor("spill_yn", [32, 128, T], BF16).ap()
    d_x1 = nc.dram_tensor("spill_x1", [NKT, 128, T], F32).ap()
    d_hmid = nc.dram_tensor("spill_hmid", [64, 128, T], BF16).ap()
    shr = "Shared" if not LOCAL else "Local"
    d_armi = nc.dram_tensor("ag_mod_in", [128, 48], F32)
    d_armo = nc.dram_tensor("ag_mod_out", [8, 128, 48], F32, addr_space=shr)
    d_arwi = nc.dram_tensor("ar_w_in", [1, 128], F32)
    d_arwo = nc.dram_tensor("ar_w_out", [1, 128], F32, addr_space=shr)
    d_agqi = nc.dram_tensor("ag_wq_in", [3, 128, 16384], BF16)
    d_agqo = nc.dram_tensor("ag_wq_out", [8, 3, 128, 16384], BF16, addr_space=shr)
    d_arsi = nc.dram_tensor("ar_s_in", [128, HID], F32)
    d_arso = nc.dram_tensor("ar_s_out", [128, HID], F32)
    d_rowb = nc.dram_tensor("row_bounce", [32, 128], F32).ap()
    d_cum = nc.dram_tensor("cum_spill", [64, 16 * Q], F32).ap()
    d_ep = nc.dram_tensor("ep_spill", [64, 16 * Q], BF16).ap()

    dbg_outs = {}

    def dbg_tensor(name, shape, dtype=F32):
        if name in dbg:
            dbg_outs[name] = nc.dram_tensor("dbg_" + name, shape, dtype,
                                            kind="ExternalOutput").ap()
            return dbg_outs[name]
        return None

    root = contextlib.ExitStack()

    def mk(stk):
        def sb(name, shape, dt=F32):
            return stk.enter_context(nc.sbuf_tensor(name, shape, dt))

        def ps(name, shape, dt=F32):
            return stk.enter_context(nc.psum_tensor(name, shape, dt))
        return sb, ps

    sbr, psr = mk(root)

    def seg(side="left"):
        stk = contextlib.ExitStack()

        def sb(name, shape, dt=F32):
            return stk.enter_context(nc.sbuf_tensor(name, shape, dt, side=side))
        return stk, sb

    sems = {}

    def sem(name, queue=False):
        sems[name] = Sem(root.enter_context(nc.semaphore(name)), queue)
        return sems[name]

    sy = sem("q_sync", queue=True)
    sg = sem("q_gp", queue=True)
    szx = [sem("q_zx0", queue=True), sem("q_zx1", queue=True)]
    swi = [sem("q_wi0", queue=True), sem("q_wi1", queue=True)]
    sr3 = [sem("q_r30", queue=True), sem("q_r31", queue=True)]
    sr4 = [sem("q_r40", queue=True), sem("q_r41", queue=True)]
    sy1 = [sem("q_y10", queue=True), sem("q_y11", queue=True)]
    swo = [sem("q_wo0", queue=True), sem("q_wo1", queue=True)]
    sg7 = [sem("q_g70", queue=True), sem("q_g71", queue=True)]
    shm = [sem("q_hm0", queue=True), sem("q_hm1", queue=True)]
    swd = [sem("q_wd0", queue=True), sem("q_wd1", queue=True)]
    sxr = sem("q_xr", queue=True)
    scm = sem("q_cm", queue=True)
    sy1s = sem("q_y1s", queue=True)
    syns = sem("q_yns", queue=True)
    sx1s = sem("q_x1s", queue=True)
    smod = sem("q_mod", queue=True)
    srow = sem("q_row", queue=True)
    sx8 = sem("q_x8", queue=True)
    sout = sem("q_out", queue=True)
    ALLQ = ([sy, sg, sxr, sx8, sout, scm, sy1s, syns, sx1s, smod, srow] + szx +
            swi + sr3 + sr4 + sy1 + swo + sg7 + shm + swd)
    sT = sem("c_T")
    sV = sem("c_V")
    sA = sem("c_A")
    sG = sem("c_G")
    sCC = sem("c_CC")
    E.auto_sems = {"vector": sV, "scalar": sA, "gpsimd": sG}

    def phase_barrier():
        for rec in (E.t, E.v, E.s, E.g, E.y):
            for s_ in [sT, sV, sA, sG] + ALLQ:
                E.wait(rec, s_, s_.n)

    def bounce_bcast(rec_consumers, row_ap, rep_ap, nfree, doff=0):
        E.wait(E.y, sy, sy.n)
        E.y.dma_start(out=d_rowb.rearrange("a b -> (a b)")[doff:doff + nfree].unsqueeze(0),
                      in_=row_ap).then_inc(srow, 16)
        srow.n += 16
        E.wait(E.y, srow, srow.n)
        E.y.dma_start(out=rep_ap,
                      in_=d_rowb.rearrange("a b -> (a b)")[doff:doff + nfree]
                      .unsqueeze(0).broadcast_to([128, nfree])).then_inc(sy, 16)
        sy.n += 16
        return sy.n

    def dmay(out, in_, sem_=None):
        s_ = sem_ or sy
        E.y.dma_start(out=out, in_=in_).then_inc(s_, 16)
        s_.n += 16
        return s_.n

    def dmag(out, in_, sem_=None):
        s_ = sem_ or sg
        E.g.dma_start(out=out, in_=in_).then_inc(s_, 16)
        s_.n += 16
        return s_.n

    ones_col = sbr("ones_col", [128, 1])
    tri = sbr("tri", [128, 128])
    identf = sbr("identf", [128, 128])
    ident = sbr("ident", [128, 128], BF16)
    masks_sb = sbr("masks_s", [128, 8])
    convw_sb = sbr("convw_s", [128, 136])
    dtbA_sb = sbr("dtbA_s", [64, 2])
    Drep_sb = sbr("Drep_s", [128, 32])
    normw_sb = sbr("normw_s", [128, 32])
    lnw_sb = sbr("lnw_s", [128, 64])
    vecs = sbr("vecs", [128, 96])
    swB = sbr("swB", [128, 128])
    rowT = sbr("rowT", [1, max(T, 16 * Q)])

    sa_c = lambda k: vecs[:, 0 + k:1 + k]
    ca1_c = lambda k: vecs[:, 16 + k:17 + k]
    ga_c = lambda k: vecs[:, 32 + k:33 + k]
    sb_c = lambda k: vecs[:, 48 + k:49 + k]
    cb1_c = lambda k: vecs[:, 64 + k:65 + k]
    gb_c = lambda k: vecs[:, 80 + k:81 + k]

    dmay(masks_sb[:], dmasks[:])
    dmay(convw_sb[:], dconvw[:])
    dmay(dtbA_sb[:], ddtbA[:])
    dmay(Drep_sb[:], dDrep[:])
    dmay(normw_sb[:], dnormw[:])
    dmay(lnw_sb[:], dlnw[:])
    ld_small = sy.n

    sV.inc(E.v.memset(ones_col[:], 1.0))
    v0 = sV.inc(E.v.memset(identf[:], 1.0))
    E.wait(E.g, sV, v0)
    g0 = sG.inc(E.g.affine_select(identf[:], identf[:], [[1, 128]], ALU.is_equal, 0.0,
                                  base=0, channel_multiplier=-1))
    E.wait(E.v, sG, g0)
    sV.inc(E.v.tensor_copy(ident[:], identf[:]))
    v0 = sV.inc(E.v.memset(tri[:], 1.0))
    E.wait(E.g, sV, v0)
    g0 = sG.inc(E.g.affine_select(tri[:], tri[:], [[1, 128]], ALU.is_ge, 0.0,
                                  base=0, channel_multiplier=-1))
    E.wait(E.v, sG, g0)

    with contextlib.ExitStack() as stk:
        sb, ps = mk(stk)
        csb = sb("csb", [128, 64])
        dmay(csb[:].rearrange("p (kt b) -> p kt b", b=4),
             dcT.rearrange("(kt p) b -> p kt b", p=128))
        ld_c = sy.n
        siluc = sb("siluc", [128, 64])
        E.wait(E.s, sy, ld_c)
        a_sc = sA.inc(E.s.activation(siluc[:], csb[:], AF.Silu))
        wm_sb = sb("wm_sb", [128, 2 * NKT * 128])
        wm_sl = lambda p_: wm_sb[:, p_ * NKT * 128:(p_ + 1) * NKT * 128]
        modps = ps("modps", [128, 4])
        modpre = sb("modpre", [128, 48]) if not LOCAL else None
        ag4 = sb("ag4", [128, 4 * 96])
        ev_V = 0
        mm_Tm = [0, 0]
        for mt in range(NMT):
            mpar = mt % 2
            wm_sem = smod if mpar == 0 else srow
            E.wait(E.y, sT, mm_Tm[mpar])
            g_l = dmay(wm_sl(mpar).rearrange("p (a c) -> p a c", c=128),
                       dwms[mt].rearrange("a p c -> p a c"), wm_sem)
            E.wait(E.t, wm_sem, g_l)
            if mt == 0:
                E.wait(E.t, sA, a_sc)
            else:
                E.wait(E.t, sV, ev_V)
            for k in range(NKT):
                mm = E.t.matmul(modps[:], wm_sl(mpar)[:, k * 128:(k + 1) * 128],
                                siluc[:, k * 4:k * 4 + 4],
                                start=(k == 0), stop=(k == NKT - 1))
            t_mm = sT.inc(mm)
            mm_Tm[mpar] = t_mm
            E.wait(E.v, sT, t_mm)
            if LOCAL:
                ev_V = sV.inc(E.v.tensor_copy(
                    ag4[:].rearrange("p (b g) -> p b g", g=96)[:, :, mt], modps[:]))
            else:
                ev_V = sV.inc(E.v.tensor_copy(
                    modpre[:].rearrange("p (b mtx) -> p b mtx", mtx=12)[:, :, mt], modps[:]))
        if not LOCAL:
            E.wait(E.y, sV, ev_V)
            dmay(d_armi.ap()[:], modpre[:])
            E.wait(E.g, sy, sy.n)
            cc1 = sCC.inc(E.g.collective_compute(
                "AllGather", ALU.bypass, replica_groups=[list(range(8))],
                ins=[d_armi.ap().opt()], outs=[d_armo.ap().opt()]))
            E.wait(E.y, sCC, cc1)
            E.wait(E.y, sy, ld_small)
            for b in range(4):
                dmay(ag4[:, b * 96:(b + 1) * 96].rearrange("p (r mt) -> p r mt", mt=12),
                     d_armo.ap()[:, :, b * 12:(b + 1) * 12].rearrange("r p mt -> p r mt"))
        modsel = sb("modsel", [128, 96])
        E.wait(E.v, sy, max(sy.n if not LOCAL else 0, ld_small))
        E.wait(E.v, sV, ev_V)
        sV.inc(E.v.tensor_scalar_mul(modsel[:], ag4[:, 0:96], masks_sb[:, 3:4]))
        for b in range(1, 4):
            E.wait(E.v, sV, sV.n)
            sV.inc(E.v.scalar_tensor_tensor(
                modsel[:], ag4[:, b * 96:(b + 1) * 96], masks_sb[:, 3 + b:4 + b],
                modsel[:], ALU.mult, ALU.add))
        v_ms = sV.n
        E.wait(E.v, sV, v_ms)
        sV.inc(E.v.tensor_copy(vecs[:, 0:16], modsel[:, 0:16]))
        sV.inc(E.v.tensor_scalar_add(vecs[:, 16:32], modsel[:, 16:32], 1.0))
        sV.inc(E.v.tensor_copy(vecs[:, 48:64], modsel[:, 48:64]))
        sV.inc(E.v.tensor_scalar_add(vecs[:, 64:80], modsel[:, 64:80], 1.0))
        gtmp = sb("gtmp", [128, 32])
        gsig = sb("gsig", [128, 32])
        sV.inc(E.v.tensor_scalar(gsig[:, 0:16], modsel[:, 32:48], 0.0, 2.0,
                                 ALU.is_ge, ALU.mult))
        sV.inc(E.v.tensor_scalar(gsig[:, 16:32], modsel[:, 80:96], 0.0, 2.0,
                                 ALU.is_ge, ALU.mult))
        v_g1 = sV.inc(E.v.tensor_scalar_sub(gsig[:], gsig[:], 1.0))
        E.wait(E.s, sV, v_ms)
        sA.inc(E.s.activation(gtmp[:, 0:16], modsel[:, 32:48], AF.Abs))
        a_abs = sA.inc(E.s.activation(gtmp[:, 16:32], modsel[:, 80:96], AF.Abs))
        E.wait(E.v, sA, a_abs)
        v_g2 = sV.inc(E.v.tensor_scalar_max(gtmp[:], gtmp[:], 0.25))
        E.wait(E.v, sV, max(v_g1, v_g2))
        sV.inc(E.v.tensor_mul(vecs[:, 32:48], gsig[:, 0:16], gtmp[:, 0:16]))
        v_vecs = sV.inc(E.v.tensor_mul(vecs[:, 80:96], gsig[:, 16:32], gtmp[:, 16:32]))
        if dbg_tensor("vecs", [128, 96]):
            E.wait(E.y, sV, v_vecs)
            dmay(dbg_outs["vecs"][:], vecs[:])

    phase_barrier()
    seg0c = contextlib.ExitStack()

    def sb0c(name, shape, dt=F32):
        return seg0c.enter_context(nc.sbuf_tensor(name, shape, dt, side="left"))
    wqs = sb0c("wqs", [128, 2048])
    wqs2 = sb0c("wqs2", [128, 2048])
    wqo = sb0c("wqo", [128, 2 * 2048], BF16)
    wqs_h = lambda h: (wqs, wqs2)[h][:]
    wqo_h = lambda h: wqo[:, h * 2048:(h + 1) * 2048]
    wabs = sb0c("wabs", [128, 4])
    wtmp = sb0c("wtmp", [128, 2])
    wrow = sb0c("wrow", [1, 128])
    swrow = sb0c("swrow", [1, 128])
    wsum_ps = seg0c.enter_context(nc.psum_tensor("wsum_ps", [1, 4], F32))
    NSLAB = 3 * nwsl
    p1_red = [0]
    p1_ld = {}
    p1_vred = [0]
    sV.inc(E.v.memset(wabs[:, 3:4], 0.0))

    def p1_issue(i):
        ti, r = divmod(i, nwsl)
        E.wait(E.y, sV, p1_red[0])
        p1_ld[i] = dmay(wqs[:], dwq[ti, :, r * 2048:(r + 1) * 2048], smod)

    def p1_reduce(i):
        ti, r = divmod(i, nwsl)
        E.wait(E.v, smod, p1_ld.pop(i))
        if r == 0:
            p1_vred[0] = sV.inc(E.v.tensor_reduce(wabs[:, ti:ti + 1], wqs[:], AX,
                                                  ALU.add, apply_absolute_value=True))
        else:
            sV.inc(E.v.tensor_reduce(wtmp[:, 0:1], wqs[:], AX, ALU.add,
                                     apply_absolute_value=True))
            p1_vred[0] = sV.inc(E.v.tensor_add(wabs[:, ti:ti + 1], wabs[:, ti:ti + 1],
                                               wtmp[:, 0:1]))
        p1_red[0] = p1_vred[0]

    q2_v = [0, 0]
    q2_out = [0, 0]
    q2_next = [0]
    y_swb_r = [0]

    def q2_emit(i):
        par = i % 2
        ti, r = divmod(i, nwsl)
        E.wait(E.y, sV, q2_v[par] or p1_vred[0])
        g_l = dmay(wqs_h(par), dwq[ti, :, r * 2048:(r + 1) * 2048], smod)
        E.wait(E.v, smod, g_l)
        E.wait(E.v, sy, y_swb_r[0])
        sV.inc(E.v.tensor_scalar(wqs_h(par), wqs_h(par), swB[:, 8 + ti:9 + ti],
                                 MAGIC, ALU.mult, ALU.add))
        sV.inc(E.v.tensor_scalar(wqs_h(par), wqs_h(par), MAGIC, -1.0,
                                 ALU.subtract, ALU.max))
        E.wait(E.v, sy1[par], q2_out[par])
        q2_v[par] = sV.inc(E.v.tensor_scalar_min(wqo_h(par), wqs_h(par), 1.0))
        E.wait(E.y, sV, q2_v[par])
        if LOCAL:
            q2_out[par] = dmay(
                d_agqo.ap()[r // 8, ti][:, (r % 8) * 2048:(r % 8 + 1) * 2048],
                wqo_h(par), sy1[par])
        else:
            q2_out[par] = dmay(d_agqi.ap()[ti][:, r * 2048:(r + 1) * 2048],
                               wqo_h(par), sy1[par])

    def q2_pump(k):
        while q2_next[0] < NSLAB and k > 0:
            q2_emit(q2_next[0])
            q2_next[0] += 1
            k -= 1

    seg_dt, sb_dt = seg("left")
    dt_sb = sb_dt("dt_sb", [64, TC])
    with contextlib.ExitStack() as stk:
        sb, ps = mk(stk)
        h1T = sb("h1T", [128, NKT * TC], BF16)
        h1t = lambda k: h1T[:, k * TC:(k + 1) * TC]
        xst = sb("xst", [128, NKT * 512])
        xsq = sb("xsq", [128, NKT * 512])
        stats_stk = contextlib.ExitStack()
        st_ps = stats_stk.enter_context(nc.psum_tensor("st_ps", [33, TC], F32))
        blocks = col_splits(TC, 512)
        a_done = 0
        for (off, nb) in blocks:
            E.wait(E.y, sA, a_done)
            for k in range(NKT):
                dmay(xst[:, k * 512:k * 512 + nb], dxT[k, :, off:off + nb])
            E.wait(E.s, sy, sy.n)
            for k in range(NKT):
                a_done = sA.inc(E.s.activation(xsq[:, k * 512:k * 512 + nb],
                                               xst[:, k * 512:k * 512 + nb], AF.Square))
            E.wait(E.t, sA, a_done)
            for k in range(NKT):
                E.t.matmul(st_ps[0:1, off:off + nb], ones_col[:],
                           xst[:, k * 512:k * 512 + nb],
                           start=(k == 0), stop=(k == NKT - 1), skip_group_check=True)
                mm2 = E.t.matmul(st_ps[32:33, off:off + nb], ones_col[:],
                                 xsq[:, k * 512:k * 512 + nb],
                                 start=(k == 0), stop=(k == NKT - 1), skip_group_check=True)
            t_st = sT.inc(mm2)
        rows = stats_stk.enter_context(
            nc.sbuf_tensor("rows_ln1", [1, 2 * TC], F32, side="right"))
        mrow = rows[:, 0:TC]
        rrow = rows[:, TC:2 * TC]
        E.wait(E.v, sT, sT.n)
        sV.inc(E.v.tensor_scalar_mul(mrow, st_ps[0:1, :], 1.0 / D_MODEL))
        E.wait(E.v, sV, sV.n)
        sV.inc(E.v.tensor_mul(rrow, mrow, mrow))
        E.wait(E.v, sV, sV.n)
        sV.inc(E.v.scalar_tensor_tensor(rrow, st_ps[32:33, :], 1.0 / D_MODEL, rrow,
                                        ALU.mult, ALU.subtract))
        E.wait(E.v, sV, sV.n)
        sV.inc(E.v.tensor_scalar_add(rrow, rrow, LN_EPS))
        E.wait(E.v, sV, sV.n)
        v_r = sV.inc(E.v.reciprocal(rrow, rrow))
        E.wait(E.s, sV, v_r)
        a_r = sA.inc(E.s.activation(rrow, rrow, AF.Sqrt))
        mrep = sb("mrep", [128, TC])
        rrep = sb("rrep", [128, TC])
        E.wait(E.y, sA, a_r)
        E.wait(E.y, sV, v_r)
        bounce_bcast(None, mrow, mrep[:], TC)
        g_m = bounce_bcast(None, rrow, rrep[:], TC)
        g_r = g_m
        stats_stk.close()
        if dbg_tensor("ln1rows", [1, 2 * TC]):
            E.wait(E.y, sA, a_r)
            dmay(dbg_outs["ln1rows"][:], rows[:])
        for bi, (off, nb) in enumerate(blocks):
            E.wait(E.y, sA, sA.n)
            for k in range(NKT):
                dmay(xst[:, k * 512:k * 512 + nb], dxT[k, :, off:off + nb])
            yld = sy.n
            E.wait(E.v, sy, yld)
            sV.inc(E.v.tensor_tensor(
                xsq[:].rearrange("p (k c) -> p k c", c=512)[:, :, 0:nb],
                xst[:].rearrange("p (k c) -> p k c", c=512)[:, :, 0:nb],
                mrep[:, off:off + nb].unsqueeze(1).broadcast_to([128, NKT, nb]),
                ALU.subtract))
            vv = sV.inc(E.v.tensor_tensor(
                xsq[:].rearrange("p (k c) -> p k c", c=512)[:, :, 0:nb],
                xsq[:].rearrange("p (k c) -> p k c", c=512)[:, :, 0:nb],
                rrep[:, off:off + nb].unsqueeze(1).broadcast_to([128, NKT, nb]),
                ALU.mult))
            E.wait(E.s, sV, vv)
            for k in range(NKT):
                sA.inc(E.s.activation(h1t(k)[:, off:off + nb], xsq[:, k * 512:k * 512 + nb],
                                      AF.Identity, bias=sa_c(k), scale=ca1_c(k)))
        E.wait(E.v, sA, sA.n)
        for k in range(NKT):
            v_bm = sV.inc(E.v.tensor_scalar_mul(h1t(k)[:, 0:3], h1t(k)[:, 0:3],
                                                masks_sb[:, 0:1]))
        h1_done_V = sV.n
        if dbg_tensor("h1T", [NKT, 128, TC], BF16):
            E.wait(E.y, sV, h1_done_V)
            for k in range(NKT):
                dmay(dbg_outs["h1T"][k], h1t(k))

        win_sb = sb("win_sb", [128, 2 * NKT * 128], BF16)
        wv_sl = lambda par, k: win_sb[:, par * NKT * 128 + k * 128:par * NKT * 128 + (k + 1) * 128]
        zx_st = sb("zx_st", [128, 2 * TC], BF16)
        zx_sl = lambda par: zx_st[:, par * TC:(par + 1) * TC]
        pj = [ps(f"pj{i}", [128, 512]) for i in range(4)]
        pieces = col_splits(TC, 512)
        slab_V = [h1_done_V, h1_done_V]
        spill_Y = [0, 0]
        psum_V = [h1_done_V] * 4
        E.wait(E.g, sV, h1_done_V)
        for o in range(NOT):
            par = o % 2
            if not LOCAL:
                if o < NSLAB:
                    if o >= 1:
                        p1_reduce(o - 1)
                    p1_issue(o)
                elif o == NSLAB:
                    p1_reduce(NSLAB - 1)
                    E.wait(E.t, sV, p1_vred[0])
                    t_ws = sT.inc(E.t.matmul(wsum_ps[:], ones_col[:], wabs[:],
                                             start=True, stop=True))
                    E.wait(E.v, sT, t_ws)
                    sV.inc(E.v.memset(wrow[:], 0.0))
                    v_wr = sV.inc(E.v.tensor_copy(wrow[:, 0:4], wsum_ps[:]))
                    E.wait(E.y, sV, v_wr)
                    dmay(d_arwi.ap()[:], wrow[:], srow)
                    E.wait(E.g, srow, srow.n)
                    cc2 = sCC.inc(E.g.collective_compute(
                        "AllReduce", ALU.add, replica_groups=[list(range(8))],
                        ins=[d_arwi.ap().opt()], outs=[d_arwo.ap().opt()]))
                elif o == NSLAB + 4:
                    E.wait(E.s, sCC, cc2)
                    E.s.dma_start(out=swrow[:],
                                  in_=d_arwo.ap()[:]).then_inc(srow, 16)
                    srow.n += 16
                    E.wait(E.v, srow, srow.n)
                    nel = float(I_FFN * D_MODEL)
                    sV.inc(E.v.tensor_scalar(swrow[:, 0:3], swrow[:, 0:3], 1.0 / nel,
                                             QEPS, ALU.mult, ALU.max))
                    v_cm = sV.inc(E.v.reciprocal(swrow[:, 8:11], swrow[:, 0:3]))
                    E.wait(E.y, sV, v_cm)
                    y_swb_r[0] = bounce_bcast(None, swrow[:], swB[:], 128)
                elif NSLAB + 6 <= o < NSLAB + 6 + NSLAB:
                    q2_emit(o - NSLAB - 6)
                    q2_next[0] = o - NSLAB - 5
                elif o == NSLAB + 6 + NSLAB + 1:
                    E.wait(E.g, sy1[0], sy1[0].n)
                    E.wait(E.g, sy1[1], sy1[1].n)
                    cc3 = sCC.inc(E.g.collective_compute(
                        "AllGather", ALU.bypass, replica_groups=[list(range(8))],
                        ins=[d_agqi.ap().opt()], outs=[d_agqo.ap().opt()]))
            E.wait(E.g, sV, slab_V[par])
            g_l = dmag(win_sb[:, par * NKT * 128:(par + 1) * NKT * 128]
                       .rearrange("p (a c) -> p a c", c=128),
                       dWin[o].rearrange("a p c -> p a c"), swi[par])
            E.wait(E.t, swi[par], g_l)
            first_evac = True
            for pi, (poff, pnb) in enumerate(pieces):
                pb = pi % 4
                E.wait(E.t, sV, psum_V[pb])
                for k in range(NKT):
                    mm = E.t.matmul(pj[pb][:, 0:pnb], wv_sl(par, k),
                                    h1t(k)[:, poff:poff + pnb],
                                    start=(k == 0), stop=(k == NKT - 1))
                t_mm = sT.inc(mm)
                E.wait(E.v, sT, t_mm)
                if first_evac:
                    E.wait(E.v, szx[par], szx[par].n)
                    first_evac = False
                vv = sV.inc(E.v.tensor_copy(zx_sl(par)[:, poff:poff + pnb],
                                            pj[pb][:, 0:pnb]))
                if o == NOT - 1:
                    vv = sV.inc(E.v.tensor_copy(dt_sb[:, poff:poff + pnb],
                                                pj[pb][0:64, 0:pnb]))
                psum_V[pb] = vv
            E.wait(E.y, sV, psum_V[(len(pieces) - 1) % 4])
            spill_Y[par] = dmay(d_spill[o], zx_sl(par), szx[par])
            slab_V[par] = sV.n
        proj_spill_done = (szx[0].n, szx[1].n)
        if dbg_tensor("zx", [4, 128, TC], BF16):
            E.wait(E.y, szx[0], proj_spill_done[0])
            E.wait(E.y, szx[1], proj_spill_done[1])
            for i, o in enumerate((0, 32 + 1, 64, 65)):
                dmay(dbg_outs["zx"][i], d_spill[o])


    seg24, sb24 = seg("right")
    E1t = sb24("E1t", [128, nch * 64], BF16)
    cumT = sb24("cumT", [128, nch * 64])
    E2t = sb24("E2t", [128, nch * 64], BF16)
    decB = sb24("decB", [128, nch * 64])
    DprB = sb24("DprB", [128, nch * 64])

    if LOCAL:
        for i in range(NSLAB):
            p1_issue(i)
            p1_reduce(i)
        E.wait(E.t, sV, p1_vred[0])
        t_ws = sT.inc(E.t.matmul(wsum_ps[:], ones_col[:], wabs[:], start=True, stop=True))
        E.wait(E.v, sT, t_ws)
        sV.inc(E.v.memset(wrow[:], 0.0))
        v_wr = sV.inc(E.v.tensor_copy(wrow[:, 0:4], wsum_ps[:]))
        E.wait(E.v, sV, v_wr)
        sV.inc(E.v.tensor_copy(swrow[:], wrow[:]))
        cc2 = 0

    phase_barrier()
    with contextlib.ExitStack() as stk:
        sb, ps = mk(stk)
        epfm = sb("epfm", [64, T])
        epbf = sb("epbf", [64, T], BF16)
        dt2 = sb("dt2", [64, T])
        acum = sb("acum", [64, T])
        E2fm = sb("E2fm", [64, T])
        zer64 = sb("zer64", [64, T])
        avt = sb("avt", [64, T])
        dcol = sb("dcol", [64, 34])
        negA = sb("negA", [64, 1])
        dcT = sb("dcT", [32, 64])
        tp2 = [ps("tp_ps0", [128, 128]), ps("tp_ps1", [128, 128])]
        E.wait(E.v, sV, sV.n)
        v_z = sV.inc(E.v.memset(zer64[:], 0.0))
        E.wait(E.s, sV, sV.n)
        E.wait(E.s, sy, ld_small)
        spx = sb("spx", [64, T])
        spr = sb("spr", [64, T])
        sA.inc(E.s.activation(spx[:], dt_sb[:, 3:TC], AF.Identity, bias=dtbA_sb[:, 0:1]))
        sA.inc(E.s.activation(spr[:], spx[:], AF.Relu))
        sA.inc(E.s.activation(dt2[:], spx[:], AF.Abs))
        sA.inc(E.s.activation(dt2[:], dt2[:], AF.Exp, scale=-1.0))
        a_sp0 = sA.inc(E.s.activation(dt2[:], dt2[:], AF.Ln, bias=1.0))
        E.wait(E.v, sA, a_sp0)
        v_sp = sV.inc(E.v.tensor_add(dt2[:], dt2[:], spr[:]))
        a_sp = a_sp0
        a_A = sA.inc(E.s.activation(negA[:], dtbA_sb[:, 1:2], AF.Exp))
        E.wait(E.v, sA, sA.n)
        v_negA = sV.inc(E.v.tensor_scalar_mul(negA[:], negA[:], -1.0))
        E.wait(E.v, sV, v_negA)
        v_av = sV.inc(E.v.tensor_scalar_mul(avt[:], dt2[:], negA[:]))
        E.wait(E.v, sV, max(v_z, v_av))
        for ci in range(nch):
            v_cum = sV.inc(E.v.tensor_tensor_scan(
                acum[:, ci * Q:(ci + 1) * Q], zer64[:, ci * Q:(ci + 1) * Q],
                avt[:, ci * Q:(ci + 1) * Q], 0.0, ALU.add, ALU.add))
        E.wait(E.s, sV, sV.n)
        a_ep = sA.inc(E.s.activation(epfm[:], acum[:], AF.Exp))
        E.wait(E.y, sV, sV.n)
        dmay(d_cum[:, 0:T], acum[:], scm)
        clcol = sb("clcol", [64, 16])
        for ci in range(nch):
            sV.inc(E.v.tensor_copy(clcol[:, ci:ci + 1],
                                   acum[:, ci * Q + Q - 1:ci * Q + Q]))
        E.wait(E.v, sA, a_ep)
        for ci in range(nch):
            sV.inc(E.v.tensor_copy(dcol[:, ci:ci + 1],
                                   epfm[:, ci * Q + Q - 1:ci * Q + Q]))
        v_dc = sV.inc(E.v.memset(dcol[:, nch:nch + 1], 1.0))
        for ci in range(1, nch):
            v_dc = sV.inc(E.v.tensor_mul(dcol[:, nch + ci:nch + ci + 1],
                                         dcol[:, nch + ci - 1:nch + ci],
                                         dcol[:, ci - 1:ci]))
        E.wait(E.s, sV, sV.n)
        for ci in range(nch):
            a_e2 = sA.inc(E.s.activation(E2fm[:, ci * Q:(ci + 1) * Q],
                                         acum[:, ci * Q:(ci + 1) * Q], AF.Exp,
                                         scale=-1.0, bias=clcol[:, ci:ci + 1]))
        E.wait(E.v, sA, a_e2)
        v_E2 = sV.inc(E.v.tensor_mul(E2fm[:], E2fm[:], dt2[:]))
        v_pre = sV.n

        if LOCAL:
            nel = float(I_FFN * D_MODEL)
            sV.inc(E.v.tensor_scalar(swrow[:, 0:3], swrow[:, 0:3], 1.0 / nel, QEPS,
                                     ALU.mult, ALU.max))
            v_cm = sV.inc(E.v.reciprocal(swrow[:, 8:11], swrow[:, 0:3]))
            E.wait(E.y, sV, v_cm)
            y_swb_r[0] = bounce_bcast(None, swrow[:], swB[:], 128)

        E.wait(E.t, sV, v_pre)
        t_dc = sT.inc(E.t.transpose(tp2[0][0:2 * nch, 0:64], dcol[:, 0:2 * nch],
                                    identf[0:64, 0:64]))
        E.wait(E.v, sT, t_dc)
        v_dct = sV.inc(E.v.tensor_copy(dcT[0:2 * nch, :], tp2[0][0:2 * nch, 0:64]))
        E.wait(E.y, sV, v_dct)
        E.wait(E.y, sy, y_swb_r[0])
        y_dc = dmay(d_rowb[0:2 * nch, 0:64], dcT[0:2 * nch, :], srow)
        E.wait(E.y, srow, y_dc)
        dmay(decB[:].rearrange("p (c k) -> p c k", k=64),
             d_rowb[0:nch, 0:64].unsqueeze(0).broadcast_to([128, nch, 64]))
        dmay(DprB[:].rearrange("p (c k) -> p c k", k=64),
             d_rowb[nch:2 * nch, 0:64].unsqueeze(0).broadcast_to([128, nch, 64]))

        tpV = [v_dct, v_pre]
        for cp in range(npr):
            c0 = 2 * cp
            for j, (src, dst) in enumerate(((dt2, E1t), (E2fm, E2t), (acum, cumT))):
                pb = (cp * 3 + j) % 2
                E.wait(E.t, sV, tpV[pb])
                E.t.transpose(tp2[pb][:, 0:64], src[:, c0 * Q:(c0 + 1) * Q],
                              identf[0:64, 0:64])
                tr = sT.inc(E.t.transpose(tp2[pb][:, 64:128],
                                          src[:, (c0 + 1) * Q:(c0 + 2) * Q],
                                          identf[0:64, 0:64]))
                E.wait(E.v, sT, tr)
                tpV[pb] = sV.inc(E.v.tensor_copy(dst[:, c0 * 64:(c0 + 2) * 64],
                                                 tp2[pb][:]))
            if LOCAL:
                q2_pump(3)
        if LOCAL:
            q2_pump(NSLAB)
            cc3 = 0
        if dbg_tensor("negacheck", [64, 2 * T + 2]):
            E.wait(E.y, sV, sV.n)
            E.wait(E.y, sA, sA.n)
            E.y.dma_start(out=dbg_outs["negacheck"][:, 0:1], in_=negA[:],
                          allow_slow_non_contiguous=True).then_inc(sy, 16)
            sy.n += 16
            E.y.dma_start(out=dbg_outs["negacheck"][:, 1:2], in_=dtbA_sb[:, 1:2],
                          allow_slow_non_contiguous=True).then_inc(sy, 16)
            sy.n += 16
            dmay(dbg_outs["negacheck"][:, 2:T + 2], avt[:])
            dmay(dbg_outs["negacheck"][:, T + 2:2 * T + 2], dt2[:])
        if dbg_tensor("dtprep", [64, 4 * T]):
            E.wait(E.y, sA, sA.n)
            E.wait(E.y, sV, sV.n)
            dmay(dbg_outs["dtprep"][:, 0:T], dt2[:])
            dmay(dbg_outs["dtprep"][:, T:2 * T], acum[:])
            dmay(dbg_outs["dtprep"][:, 2 * T:3 * T], E2fm[:])
            dmay(dbg_outs["dtprep"][:, 3 * T:4 * T], epfm[:])
        E.wait(E.v, sA, sA.n)
        E.wait(E.v, sV, sV.n)
        v_epbf = sV.inc(E.v.tensor_copy(epbf[:], epfm[:]))
        E.wait(E.y, sV, v_epbf)
        dmay(d_ep[:, 0:T], epbf[:], scm)
    dtprep_G = sG.n
    dtprep_A = sA.n
    seg_dt.close()
    seg0c.close()

    phase_barrier()
    with contextlib.ExitStack() as stk:
        sb, ps = mk(stk)
        XBW = 34 * 131
        xbc_st = sb("xbc_st", [128, 2 * XBW], BF16)
        xb_sl = lambda par: xbc_st[:, par * XBW:(par + 1) * XBW]
        convoX = sb("convoX", [128, 32 * 128], BF16)
        convoBC = sb("convoBC", [128, 2 * 256], BF16)
        bc_sl = lambda par: convoBC[:, par * 256:(par + 1) * 256]
        cacc = sb("cacc", [128, CONV])
        csc = sb("csc", [128, CONV], BF16)
        Mt = sb("Mt", [128, 128], BF16)
        CROW2 = sb("CROW2", [128, 2 * 32 * 128])
        crow_sl = lambda b: CROW2[:, b * 4096:(b + 1) * 4096]
        Mtb2 = sb("Mtb2", [128, 2 * 32 * 128], BF16)
        mtb_sl = lambda b: Mtb2[:, b * 32 * 128:(b + 1) * 32 * 128]
        xs1 = sb("xs1", [128, HID], BF16)
        xs1_sl = lambda par: xs1[:]
        xs2 = sb("xs2", [128, HID], BF16)
        Btm = sb("Btm", [128, 128], BF16)
        S = sb("S", [128, HID])
        Sbf = sb("Sbf", [128, 2 * HID], BF16)
        sbf_sl = lambda par: Sbf[:, par * HID:(par + 1) * HID]
        Erep = sb("Erep", [128, 32 * 128], BF16)
        y1sb = sb("y1sb", [128, HID], BF16)
        ysc = sb("ysc", [128, 512])
        ps_y = [ps(f"ps_y{i}", [128, 512]) for i in range(2)]
        ps_s = [ps(f"ps_s{i}", [128, 512]) for i in range(2)]
        ps_cb = ps("ps_cb", [128, 128])
        ps_tp = ps("ps_tp", [128, 128], BF16)
        ps_t4 = ps("ps_t4", [128, 512], BF16)

        E.wait(E.v, sG, dtprep_G)
        sV.inc(E.v.memset(S[:], 0.0))
        v_S0 = sV.inc(E.v.tensor_copy(sbf_sl(1), S[:]))

        def reload3(ci):
            par = ci % 2
            E.wait(E.g, sV, v_conv[par])
            if ci < 2:
                E.wait(E.g, szx[0], proj_spill_done[0])
                E.wait(E.g, szx[1], proj_spill_done[1])
            for piece in range(4):
                t0 = piece * 9
                t1 = min(34, t0 + 9)
                dmag(xb_sl(par).rearrange("p (t c) -> p t c", c=131)[:, t0:t1, :],
                     d_spill[32 + t0:32 + t1, :, ci * Q:ci * Q + 131]
                     .rearrange("t p c -> p t c"), sr3[par])
            reload_G[par] = sr3[par].n

        reload_G = [0, 0]
        v_conv = [v_S0, v_S0]
        a_cv_prev = 0
        conv_T = [0, 0]
        conv_TX = 0
        v_drep = 0
        mm_btm_T = 0
        v_mt_prev = 0
        tB_prev = 0
        xs1_T = [0, 0]
        sbf_T = [0, 0]
        a_sbf = [0, 0]
        mtb_T = [0, 0]
        v_sub_b = [v_S0, v_S0]
        er_evV = 0
        hg_evV = [v_S0, v_S0]
        st_rdV = [v_S0, v_S0]
        reload3(0)

        for ci in range(nch):
            par = ci % 2
            if ci + 1 < nch:
                reload3(ci + 1)
            E.wait(E.v, sr3[par], reload_G[par])
            E.wait(E.v, sA, a_cv_prev)
            sl_in = xb_sl(par).rearrange("p (t c) -> p t c", c=131)
            wv3 = convw_sb[:].rearrange("p (t k) -> p t k", k=4)
            v_c = sV.inc(E.v.tensor_tensor(
                cacc[:].rearrange("p (t c) -> p t c", c=128),
                sl_in[:, :, 0:Q],
                wv3[:, 0:34, 0:1].broadcast_to([128, 34, Q]), ALU.mult))
            for kk in range(1, KCONV):
                E.wait(E.v, sV, v_c)
                sV.inc(E.v.tensor_tensor(
                    csc[:].rearrange("p (t c) -> p t c", c=128),
                    sl_in[:, :, kk:kk + Q],
                    wv3[:, 0:34, kk:kk + 1].broadcast_to([128, 34, Q]), ALU.mult))
                E.wait(E.v, sV, sV.n)
                v_c = sV.inc(E.v.tensor_add(cacc[:], cacc[:], csc[:]))
            v_conv[par] = v_c
            E.wait(E.s, sV, v_c)
            E.wait(E.s, sT, conv_TX)
            E.wait(E.s, sV, v_drep)
            a_cvx = sA.inc(E.s.activation(convoX[:], cacc[:, 0:32 * 128], AF.Silu))
            E.wait(E.s, sT, conv_T[par])
            a_cv = sA.inc(E.s.activation(bc_sl(par), cacc[:, 32 * 128:CONV], AF.Silu))
            a_cv_prev = a_cv
            E.wait(E.v, sy1s, sy1s.n)
            E.wait(E.v, sA, a_cvx)
            v_drep = sV.inc(E.v.tensor_tensor(
                y1sb[:].rearrange("p (h c) -> p h c", c=128),
                convoX[:].rearrange("p (h c) -> p h c", c=128),
                Drep_sb[:, 0:32].rearrange("p (h one) -> p h one", one=1)
                .broadcast_to([128, 32, 128]), ALU.mult))
            E.wait(E.t, sA, a_cv)
            E.wait(E.t, sV, tB_prev)
            tB = sT.inc(E.t.transpose(ps_tp[:], bc_sl(par)[:, 0:128], ident[:]))
            E.wait(E.v, sT, tB)
            E.wait(E.v, sT, mm_btm_T)
            tB_prev = sV.inc(E.v.tensor_copy(Btm[:], ps_tp[:]))
            E.wait(E.t, sV, max(tB_prev, v_mt_prev))
            t_cb = sT.inc(E.t.matmul(ps_cb[:], bc_sl(par)[:, 0:128],
                                     bc_sl(par)[:, 128:256], start=True, stop=True))
            E.wait(E.v, sT, t_cb)
            v_mt_prev = sV.inc(E.v.tensor_mul(Mt[:], ps_cb[:], tri[:]))
            vx = sV.n
            for g4 in range(8):
                E.wait(E.t, sV, vx)
                for q4 in range(4):
                    xt = g4 * 4 + q4
                    tx = sT.inc(E.t.transpose(ps_t4[:, q4 * 128:(q4 + 1) * 128],
                                              convoX[:, xt * 128:(xt + 1) * 128], ident[:]))
                E.wait(E.v, sT, tx)
                if g4 == 0:
                    E.wait(E.v, sT, mm_btm_T)
                    E.wait(E.v, sT, xs1_T[0])
                e1sl = E1t[:, ci * 64 + 8 * g4:ci * 64 + 8 * g4 + 8]
                e2sl = E2t[:, ci * 64 + 8 * g4:ci * 64 + 8 * g4 + 8]
                sV.inc(E.v.tensor_tensor(
                    xs1_sl(par)[:, g4 * 512:(g4 + 1) * 512].rearrange("p (h q) -> p h q", h=8),
                    ps_t4[:].rearrange("p (h q) -> p h q", h=8),
                    e1sl.rearrange("p (h one) -> p h one", one=1).broadcast_to([128, 8, 64]),
                    ALU.mult))
                vx = sV.inc(E.v.tensor_tensor(
                    xs2[:, g4 * 512:(g4 + 1) * 512].rearrange("p (h q) -> p h q", h=8),
                    ps_t4[:].rearrange("p (h q) -> p h q", h=8),
                    e2sl.rearrange("p (h one) -> p h one", one=1).broadcast_to([128, 8, 64]),
                    ALU.mult))
            conv_TX = tx
            xs_V = sV.n
            st_V = sV.n
            for sgi in range(8):
                pb2 = sgi % 2
                E.wait(E.t, sV, xs_V if sgi < 2 else st_V)
                E.wait(E.t, sV, hg_evV[pb2])
                mm_s = E.t.matmul(ps_s[pb2][:], Btm[:], xs2[:, sgi * 512:(sgi + 1) * 512],
                                  start=True, stop=True)
                t_s = sT.inc(mm_s)
                if sgi == 7:
                    mm_btm_T = t_s
                E.wait(E.v, sT, t_s)
                sV.inc(E.v.tensor_tensor(
                    S[:, sgi * 512:(sgi + 1) * 512].rearrange("p (h q) -> p h q", q=64),
                    S[:, sgi * 512:(sgi + 1) * 512].rearrange("p (h q) -> p h q", q=64),
                    decB[:, ci * 64 + sgi * 8:ci * 64 + sgi * 8 + 8]
                    .rearrange("p (h one) -> p h one", one=1).broadcast_to([128, 8, 64]),
                    ALU.mult))
                st_V = sV.inc(E.v.tensor_add(S[:, sgi * 512:(sgi + 1) * 512],
                                             S[:, sgi * 512:(sgi + 1) * 512],
                                             ps_s[pb2][:]))
                st_rdV[pb2] = st_V
            E.wait(E.s, sV, st_V)
            E.wait(E.s, sT, sbf_T[par])
            a_sbf[par] = sA.inc(E.s.activation(sbf_sl(par), S[:], AF.Identity))
            E.wait(E.g, sV, er_evV)
            dmag(Erep[0:64, :].rearrange("p (j2 i) -> p j2 i", i=Q),
                 d_ep[0:64:2, ci * Q:(ci + 1) * Q]
                 .unsqueeze(0).broadcast_to([64, 32, Q]))
            er_Y = dmag(Erep[64:128, :].rearrange("p (j2 i) -> p j2 i", i=Q),
                        d_ep[1:64:2, ci * Q:(ci + 1) * Q]
                        .unsqueeze(0).broadcast_to([64, 32, Q]))
            ev_V = sV.n
            sbprev = sbf_sl(1 - par)
            for hh in range(2):
                b = hh
                cr_sem = scm if b == 0 else sxr
                E.wait(E.y, sV, v_sub_b[b])
                g_pb = dmay(crow_sl(b).rearrange("p (h i) -> p h i", i=Q),
                            d_cum[hh * 32:(hh + 1) * 32, ci * Q:(ci + 1) * Q]
                            .unsqueeze(0).broadcast_to([128, 32, Q]), cr_sem)
                E.wait(E.v, cr_sem, g_pb)
                E.wait(E.v, sT, mtb_T[b])
                v_sub_b[b] = sV.inc(E.v.tensor_tensor(
                    mtb_sl(b).rearrange("p (h i) -> p h i", i=Q),
                    crow_sl(b).rearrange("p (h i) -> p h i", i=Q),
                    cumT[:, ci * 64 + hh * 32:ci * 64 + hh * 32 + 32]
                    .rearrange("p (h one) -> p h one", one=1).broadcast_to([128, 32, Q]),
                    ALU.subtract))
                v_sub_last = v_sub_b[b]
                E.wait(E.s, sV, v_sub_last)
                sA.inc(E.s.activation(mtb_sl(b), mtb_sl(b), AF.Relu, scale=-1.0))
                a_E = sA.inc(E.s.activation(mtb_sl(b), mtb_sl(b), AF.Exp, scale=-1.0))
                E.wait(E.v, sA, a_E)
                v_mtb = sV.inc(E.v.tensor_tensor(
                    mtb_sl(b).rearrange("p (h i) -> p h i", i=Q),
                    mtb_sl(b).rearrange("p (h i) -> p h i", i=Q),
                    Mt[:].unsqueeze(1).broadcast_to([128, 32, Q]), ALU.mult))
                for hg in range(4):
                    pb = hg % 2
                    E.wait(E.t, sV, max(v_mtb, hg_evV[pb]))
                    if hh == 0 and hg < 2:
                        E.wait(E.t, sA, a_sbf[1 - par])
                        E.wait(E.t, sV, st_rdV[pb])
                    for j in range(4):
                        hpair = (hh * 4 + hg) * 4 + j
                        hl = hg * 8 + 2 * j
                        he, ho = 2 * hpair, 2 * hpair + 1
                        E.t.matmul(ps_y[pb][0:64, j * 128:(j + 1) * 128],
                                   xs1_sl(par)[:, he * 64:(he + 1) * 64],
                                   mtb_sl(b)[:, (hg * 8 + 2 * j) * 128:(hg * 8 + 2 * j + 1) * 128],
                                   start=True, stop=True)
                        E.t.matmul(ps_s[pb][0:64, j * 128:(j + 1) * 128],
                                   sbprev[:, he * 64:(he + 1) * 64],
                                   bc_sl(par)[:, 128:256], start=True, stop=True)
                        E.t.matmul(ps_y[pb][64:128, j * 128:(j + 1) * 128],
                                   xs1_sl(par)[:, ho * 64:(ho + 1) * 64],
                                   mtb_sl(b)[:, (hg * 8 + 2 * j + 1) * 128:(hg * 8 + 2 * j + 2) * 128],
                                   start=True, stop=True)
                        mmy = E.t.matmul(ps_s[pb][64:128, j * 128:(j + 1) * 128],
                                         sbprev[:, ho * 64:(ho + 1) * 64],
                                         bc_sl(par)[:, 128:256], start=True,
                                         stop=True)
                    t_y = sT.inc(mmy)
                    mtb_T[b] = t_y
                    xs1_T[0] = t_y
                    sbf_T[1 - par] = t_y
                    conv_T[par] = t_y
                    hq = (hh * 4 + hg) * 4
                    E.wait(E.v, sT, t_y)
                    if hh == 0 and hg == 0:
                        E.wait(E.v, sg, er_Y)
                    sV.inc(E.v.tensor_tensor(ysc[:], ps_s[pb][:],
                                             Erep[:, hq * 128:hq * 128 + 512], ALU.mult))
                    sV.inc(E.v.tensor_add(ysc[:], ysc[:], ps_y[pb][:]))
                    ev_V = sV.inc(E.v.tensor_add(y1sb[:, hq * 128:hq * 128 + 512],
                                                 y1sb[:, hq * 128:hq * 128 + 512],
                                                 ysc[:]))
                    hg_evV[pb] = ev_V
                    er_evV = ev_V
            if ci == 0 and dbg_tensor("ssm0", [4, 128, HID], BF16):
                E.wait(E.y, sV, sV.n)
                E.wait(E.y, sA, sA.n)
                dmay(dbg_outs["ssm0"][0, :, 0:128], Mt[:])
                dmay(dbg_outs["ssm0"][1], xs1_sl(par))
                dmay(dbg_outs["ssm0"][2], Erep[:])
                dmay(dbg_outs["ssm0"][3, :, 0:32 * 128], mtb_sl(1))
                E.wait(E.v, sy, sy.n)
                E.wait(E.s, sy, sy.n)
                E.wait(E.g, sy, sy.n)
            E.wait(E.g, sV, ev_V)
            dmag(d_y1[ci].rearrange("t p c -> p t c"),
                 y1sb[:].rearrange("p (t c) -> p t c", c=Q), sy1s)
        if dbg_tensor("Stot", [128, HID]):
            E.wait(E.y, sV, sV.n)
            dmay(dbg_outs["Stot"][:], S[:])
        if dbg_tensor("y1", [nch, 32, 128, Q], BF16):
            E.wait(E.y, sy1s, sy1s.n)
            for ci2 in range(nch):
                dmay(dbg_outs["y1"][ci2], d_y1[ci2])
        if not LOCAL:
            E.wait(E.v, sV, sV.n)
            E.wait(E.v, sA, sA.n)
            v_ari = sV.inc(E.v.tensor_scalar_mul(cacc[:, 0:HID], S[:], masks_sb[:, 2:3]))
            E.wait(E.y, sV, v_ari)
            dmay(d_arsi.ap()[:], cacc[:, 0:HID])
            E.wait(E.g, sy, sy.n)
            cc4 = sCC.inc(E.g.collective_compute(
                "AllReduce", ALU.add, replica_groups=[[0, 1], [2, 3], [4, 5], [6, 7]],
                ins=[d_arsi.ap().opt()], outs=[d_arso.ap().opt()]))

    phase_barrier()
    seg45, sb45 = seg("left")
    rstd_rep = sb45("rstd_rep", [128, T])
    with contextlib.ExitStack() as stk:
        sb, ps = mk(stk)
        z_st = sb("z_st", [128, 2 * 32 * 256], BF16)
        z_sl = lambda par: z_st[:, par * 32 * 256:(par + 1) * 32 * 256]
        c_st = sb("c_st", [128, 2 * 259], BF16)
        c_sl = lambda par: c_st[:, par * 259:(par + 1) * 259]
        y1_st = sb("y1_st", [128, 2 * HID], BF16)
        y1_sl = lambda loc: y1_st[:, loc * HID:(loc + 1) * HID]
        siluz2 = sb("siluz2", [128, 2 * HID], BF16)
        sz_sl = lambda par: siluz2[:, par * HID:(par + 1) * HID]
        Sini_c = sb("Sini_c", [128, HID], BF16)
        ysum = sb("ysum", [128, HID])
        yzsq = sb("yzsq", [128, HID])
        ynbf = sb("ynbf", [128, HID], BF16)
        ssqrow = sb("ssqrow", [1, T])
        cacc2 = sb("cacc2", [128, Q])
        csc2 = sb("csc2", [128, Q])
        convC2 = sb("convC2", [128, 2 * Q], BF16)
        cC_sl = lambda par: convC2[:, par * Q:(par + 1) * Q]
        Sinit = sb("Sinit", [128, HID])
        Erep42 = sb("Erep42", [128, 2 * 32 * 128], BF16)
        er_sl = lambda par: Erep42[:, par * 4096:(par + 1) * 4096]
        ysc2 = sb("ysc2", [128, 2048])
        ps_y = ps("p4_y", [128, 2048])
        ps_r = ps("p4_r", [1, 256])

        def reload4(pp):
            par = pp % 2
            E.wait(E.g, sA, a_z_pair[par])
            for piece in range(2):
                t0, t1 = piece * 16, piece * 16 + 16
                dmag(z_sl(par).rearrange("p (t c) -> p t c", c=256)[:, t0:t1, :],
                     d_spill[t0:t1, :, 3 + pp * 256:3 + pp * 256 + 256]
                     .rearrange("t p c -> p t c"), sr4[par])
            dmag(c_sl(par), d_spill[65, :, pp * 256:pp * 256 + 259], sr4[par])
            reload_G[par] = sr4[par].n

        reload_G = [0, 0]
        y1_Y = [0, 0]
        y1_rdV = [0, 0]
        a_z_pair = [0, 0]
        a_c_prev = 0
        conv_T4 = [0, 0]
        silz_V = [0, 0]
        sin_T = 0
        a_sq_prev = 0
        t_r_prev = 0
        row_cpV = [0, 0]
        er_evV4 = [0, 0]
        if LOCAL:
            dmay(Sinit[:], dSin[:])
        else:
            E.wait(E.y, sCC, cc4)
            dmay(Sinit[:], d_arso.ap()[:])
        E.wait(E.v, sy, sy.n)
        sV.inc(E.v.tensor_scalar_mul(Sinit[:], Sinit[:], masks_sb[:, 1:2]))
        reload4(0)
        if npr > 1:
            reload4(1)
        for ci in range(nch):
            pp, local = divmod(ci, 2)
            par = pp % 2
            E.wait(E.y, sV, y1_rdV[local])
            y1_Y[local] = dmay(y1_sl(local).rearrange("p (t c) -> p t c", c=Q),
                               d_y1[ci].rearrange("t p c -> p t c"), sy1[local])
            base = local * Q
            E.wait(E.v, sr4[par], reload_G[par])
            E.wait(E.v, sA, a_c_prev)
            v_c = sV.inc(E.v.tensor_tensor(
                cacc2[:], c_sl(par)[:, base:base + Q],
                convw_sb[:, 33 * 4:33 * 4 + 1].broadcast_to([128, Q]), ALU.mult))
            for kk in range(1, KCONV):
                E.wait(E.v, sV, v_c)
                sV.inc(E.v.tensor_tensor(
                    csc2[:], c_sl(par)[:, base + kk:base + kk + Q],
                    convw_sb[:, 33 * 4 + kk:33 * 4 + kk + 1].broadcast_to([128, Q]),
                    ALU.mult))
                E.wait(E.v, sV, sV.n)
                v_c = sV.inc(E.v.tensor_add(cacc2[:], cacc2[:], csc2[:]))
            cpar = ci % 2
            E.wait(E.s, sV, v_c)
            E.wait(E.s, sT, conv_T4[cpar])
            a_c_prev = sA.inc(E.s.activation(cC_sl(cpar), cacc2[:], AF.Silu))
            a_c = a_c_prev
            E.wait(E.s, sV, silz_V[cpar])
            a_z = sA.inc(E.s.activation(
                sz_sl(cpar).rearrange("p (t c) -> p t c", c=Q),
                z_sl(par).rearrange("p (t c) -> p t c", c=256)[:, :, base:base + Q],
                AF.Silu))
            if local == 1:
                a_z_pair[par] = a_z
                if pp + 2 < npr:
                    reload4(pp + 2)
            E.wait(E.v, sT, sin_T)
            v_sc = sV.inc(E.v.tensor_tensor(
                Sini_c[:].rearrange("p (h q) -> p h q", q=64),
                Sinit[:].rearrange("p (h q) -> p h q", q=64),
                DprB[:, ci * 64:(ci + 1) * 64].rearrange("p (h one) -> p h one", one=1)
                .broadcast_to([128, 64, 64]), ALU.mult))
            epar = ci % 2
            er_sem = scm if epar == 0 else sxr
            E.wait(E.y, sV, er_evV4[epar])
            dmay(er_sl(epar)[0:64, :].rearrange("p (j2 i) -> p j2 i", i=Q),
                 d_ep[0:64:2, ci * Q:(ci + 1) * Q]
                 .unsqueeze(0).broadcast_to([64, 32, Q]), er_sem)
            er_Y = dmay(er_sl(epar)[64:128, :].rearrange("p (j2 i) -> p j2 i", i=Q),
                        d_ep[1:64:2, ci * Q:(ci + 1) * Q]
                        .unsqueeze(0).broadcast_to([64, 32, Q]), er_sem)
            E.wait(E.t, sA, a_c)
            E.wait(E.t, sV, v_sc)
            E.wait(E.v, syns, syns.n)
            ev_V = sV.n
            ev_q = [ev_V, ev_V]
            for qr in range(4):
                rp = qr % 2
                ro = rp * 1024
                E.wait(E.t, sV, ev_q[rp])
                for hg4 in range(2):
                    for j in range(4):
                        hpair = qr * 8 + hg4 * 4 + j
                        he, ho = 2 * hpair, 2 * hpair + 1
                        co = ro + hg4 * 512 + j * 128
                        E.t.matmul(ps_y[0:64, co:co + 128],
                                   Sini_c[:, he * 64:(he + 1) * 64], cC_sl(cpar),
                                   start=True, stop=True)
                        mmy = E.t.matmul(ps_y[64:128, co:co + 128],
                                         Sini_c[:, ho * 64:(ho + 1) * 64], cC_sl(cpar),
                                         start=True, stop=True)
                t_y = sT.inc(mmy)
                sin_T = t_y
                conv_T4[cpar] = t_y
                sl4 = slice(qr * 1024, qr * 1024 + 1024)
                E.wait(E.v, sT, t_y)
                if qr == 0:
                    E.wait(E.v, er_sem, er_Y)
                    E.wait(E.v, sy1[local], y1_Y[local])
                    E.wait(E.v, sA, max(a_z, a_sq_prev))
                sV.inc(E.v.tensor_tensor(ysc2[:, ro:ro + 1024], ps_y[:, ro:ro + 1024],
                                         er_sl(epar)[:, sl4], ALU.mult))
                sV.inc(E.v.tensor_add(ysum[:, sl4], ysc2[:, ro:ro + 1024],
                                      y1_sl(local)[:, sl4]))
                sV.inc(E.v.tensor_mul(ysum[:, sl4], ysum[:, sl4], sz_sl(cpar)[:, sl4]))
                ev_V = sV.inc(E.v.tensor_tensor(
                    ynbf[:, sl4].rearrange("p (j c) -> p j c", c=128),
                    ysum[:, sl4].rearrange("p (j c) -> p j c", c=128),
                    normw_sb[:, qr * 8:qr * 8 + 8]
                    .rearrange("p (j one) -> p j one", one=1)
                    .broadcast_to([128, 8, 128]), ALU.mult))
                ev_q[rp] = ev_V
                er_evV4[epar] = ev_V
            y1_rdV[local] = ev_V
            silz_V[cpar] = ev_V
            E.wait(E.s, sV, ev_V)
            E.wait(E.s, sT, t_r_prev)
            a_sq = sA.inc(E.s.activation(yzsq[:], ysum[:], AF.Square))
            a_sq_prev = a_sq
            E.wait(E.g, sV, ev_V)
            dmag(d_yn[:, :, ci * Q:(ci + 1) * Q].rearrange("t p c -> p t c"),
                 ynbf[:].rearrange("p (t c) -> p t c", c=Q), syns)
            E.wait(E.t, sA, a_sq)
            E.wait(E.t, sV, row_cpV[ci % 2])
            for xt in range(32):
                mm_r = E.t.matmul(ps_r[:, (ci % 2) * 128:(ci % 2 + 1) * 128], ones_col[:],
                                  yzsq[:, xt * 128:(xt + 1) * 128],
                                  start=(xt == 0), stop=(xt == 31))
            t_r = sT.inc(mm_r)
            t_r_prev = t_r
            E.wait(E.v, sT, t_r)
            row_cpV[ci % 2] = sV.inc(E.v.tensor_copy(
                ssqrow[:, ci * Q:(ci + 1) * Q],
                ps_r[:, (ci % 2) * 128:(ci % 2 + 1) * 128]))
        if dbg_tensor("yn", [32, 128, T], BF16):
            E.wait(E.y, syns, syns.n)
            for t0 in range(32):
                dmay(dbg_outs["yn"][t0], d_yn[t0])
        E.wait(E.v, sV, sV.n)
        sV.inc(E.v.tensor_scalar(ssqrow[:], ssqrow[:], 1.0 / HID, RMS_EPS_M,
                                 ALU.mult, ALU.add))
        E.wait(E.v, sV, sV.n)
        v_rs = sV.inc(E.v.reciprocal(ssqrow[:], ssqrow[:]))
        E.wait(E.s, sV, v_rs)
        a_rs = sA.inc(E.s.activation(ssqrow[:], ssqrow[:], AF.Sqrt))
        E.wait(E.y, sA, a_rs)
        g_rs = bounce_bcast(None, ssqrow[:], rstd_rep[:], T)
        yn_spill_Y = sy.n

    seg24.close()

    phase_barrier()
    seg56, sb56 = seg("right")
    rows2 = sb56("rows_ln2", [1, 2 * T])
    with contextlib.ExitStack() as stk:
        sb, ps = mk(stk)
        ynres = sb("ynres", [128, 32 * T], BF16)
        yn_sl = lambda k: ynres[:, k * T:(k + 1) * T]
        wo_sb = sb("wo_sb", [128, 2 * 32 * 128], BF16)
        wo_sl = lambda par, k: wo_sb[:, par * 32 * 128 + k * 128:par * 32 * 128 + (k + 1) * 128]
        x1sb = sb("x1sb", [128, T])
        x1sq = sb("x1sq", [128, T])
        xres = sb("xres", [128, 2 * 512])
        xr_sl = lambda xb: xres[:, xb * 512:(xb + 1) * 512]
        ps_m2 = [ps(f"ps_m{i}", [128, 512]) for i in range(2)]
        ps_l2 = ps("ps_l2", [33, T])
        E.wait(E.y, syns, syns.n)
        for t0 in range(32):
            dmay(yn_sl(t0), d_yn[t0])
        yn_loaded = sy.n
        mm_T5 = [0, 0]
        xr_V = [0, 0]
        ev_V5 = [0, 0]
        v_x1 = 0
        for m in range(16):
            par = m % 2
            E.wait(E.g, sT, mm_T5[par])
            g_l = dmag(wo_sb[:, par * 32 * 128:(par + 1) * 32 * 128]
                       .rearrange("p (a c) -> p a c", c=128),
                       dWout[m].rearrange("a p c -> p a c"), swo[par])
            E.wait(E.t, swo[par], g_l)
            if m == 0:
                E.wait(E.t, sy, yn_loaded)
            for pi, (coff, cnb) in enumerate(col_splits(T, 512)):
                pb = pi % 2
                E.wait(E.t, sV, ev_V5[pb])
                if pi == 0:
                    E.wait(E.t, sV, v_x1)
                    E.wait(E.t, sx1s, sx1s.n)
                for k in range(32):
                    mm = E.t.matmul(ps_m2[pb][:, 0:cnb], wo_sl(par, k),
                                    yn_sl(k)[:, coff:coff + cnb],
                                    start=(k == 0), stop=(k == 31), skip_group_check=True)
                t_m = sT.inc(mm)
                mm_T5[par] = t_m
                E.wait(E.y, sV, xr_V[pb])
                y_x = dmay(xr_sl(pb)[:, 0:cnb], dxT[m, :, 3 + coff:3 + coff + cnb], sxr)
                E.wait(E.v, sT, t_m)
                E.wait(E.v, sxr, y_x)
                E.wait(E.v, sy, g_rs)
                sV.inc(E.v.tensor_tensor(x1sq[:, coff:coff + cnb], ps_m2[pb][:, 0:cnb],
                                         rstd_rep[:, coff:coff + cnb], ALU.mult))
                E.wait(E.v, sV, sV.n)
                v_x1 = sV.inc(E.v.scalar_tensor_tensor(
                    x1sb[:, coff:coff + cnb], x1sq[:, coff:coff + cnb], ga_c(m),
                    xr_sl(pb)[:, 0:cnb], ALU.mult, ALU.add))
                ev_V5[pb] = v_x1
                xr_V[pb] = v_x1
            E.wait(E.y, sV, v_x1)
            dmay(d_x1[m], x1sb[:], sx1s)
            E.wait(E.s, sV, v_x1)
            a_sq = sA.inc(E.s.activation(x1sq[:], x1sb[:], AF.Square))
            E.wait(E.t, sA, a_sq)
            for (coff, cnb) in col_splits(T, 512):
                E.t.matmul(ps_l2[0:1, coff:coff + cnb], ones_col[:],
                           x1sb[:, coff:coff + cnb], start=(m == 0), stop=(m == 15),
                           skip_group_check=True)
                mm2 = E.t.matmul(ps_l2[32:33, coff:coff + cnb], ones_col[:],
                                 x1sq[:, coff:coff + cnb], start=(m == 0), stop=(m == 15),
                                 skip_group_check=True)
            t_l2 = sT.inc(mm2)
            slab_V[par] = sV.n
        x1_spill_Y = sx1s.n
        if dbg_tensor("x1", [NKT, 128, T]):
            E.wait(E.y, sx1s, x1_spill_Y)
            for m in range(16):
                dmay(dbg_outs["x1"][m], d_x1[m])
        m2row = rows2[:, 0:T]
        r2row = rows2[:, T:2 * T]
        E.wait(E.v, sT, sT.n)
        sV.inc(E.v.tensor_scalar_mul(m2row, ps_l2[0:1, 0:T], 1.0 / D_MODEL))
        E.wait(E.v, sV, sV.n)
        sV.inc(E.v.tensor_mul(r2row, m2row, m2row))
        E.wait(E.v, sV, sV.n)
        sV.inc(E.v.scalar_tensor_tensor(r2row, ps_l2[32:33, 0:T], 1.0 / D_MODEL, r2row,
                                        ALU.mult, ALU.subtract))
        E.wait(E.v, sV, sV.n)
        sV.inc(E.v.tensor_scalar_add(r2row, r2row, LN_EPS))
        E.wait(E.v, sV, sV.n)
        v_r2 = sV.inc(E.v.reciprocal(r2row, r2row))
        E.wait(E.s, sV, v_r2)
        a_r2 = sA.inc(E.s.activation(r2row, r2row, AF.Sqrt))

    seg45.close()

    phase_barrier()
    seg67, sb67 = seg("left")
    qx2 = sb67("qx2", [128, NKT * T], BF16)
    qx_sl = lambda k: qx2[:, k * T:(k + 1) * T]
    with contextlib.ExitStack() as stk:
        sb, ps = mk(stk)
        m2rep = sb("m2rep", [128, T])
        r2rep = sb("r2rep", [128, T])
        s2rep = sb("s2rep", [128, T])
        i2rep = sb("i2rep", [128, T])
        h2t = sb("h2t", [128, T])
        h2s = sb("h2s", [128, T])
        rmax2 = sb("rmax2", [128, T])
        s2tm = sb("s2tm", [128, NTB])
        i2tm = sb("i2tm", [128, NTB])
        xpose = sb("xpose", [128, 128])
        ps_tp = ps("p6_tp", [128, 128])
        E.wait(E.y, sA, sA.n)
        E.wait(E.y, sV, sV.n)
        bounce_bcast(None, rows2[:, 0:T], m2rep[:], T)
        g_r2 = bounce_bcast(None, rows2[:, T:2 * T], r2rep[:], T)
        rmin2 = sb("rmin2", [128, T])
        mn2tm = sb("mn2tm", [128, NTB])
        E.wait(E.v, sy, g_r2)
        sV.inc(E.v.memset(rmax2[:], 0.0))
        v_rm = sV.inc(E.v.memset(rmin2[:], 0.0))

        def ln2_tile(m, dest, v_guard):
            E.wait(E.y, sV, sV.n)
            y_x = dmay(h2s[:], d_x1[m])
            E.wait(E.v, sy, y_x)
            sV.inc(E.v.tensor_sub(h2t[:], h2s[:], m2rep[:]))
            vv = sV.inc(E.v.tensor_mul(h2t[:], h2t[:], r2rep[:]))
            E.wait(E.s, sV, vv)
            aa = sA.inc(E.s.activation(h2t[:], h2t[:], AF.Identity,
                                       bias=sb_c(m), scale=cb1_c(m)))
            return aa

        for m in range(16):
            aa = ln2_tile(m, None, v_rm)
            E.wait(E.v, sA, aa)
            sV.inc(E.v.tensor_tensor(rmax2[:], rmax2[:], h2t[:], ALU.max))
            v_rm = sV.inc(E.v.tensor_tensor(rmin2[:], rmin2[:], h2t[:], ALU.min))
        v_tp = v_rm
        for tb in range(NTB):
            E.wait(E.t, sV, v_tp)
            t_t = sT.inc(E.t.transpose(ps_tp[:], rmax2[:, tb * 128:(tb + 1) * 128], identf[:]))
            E.wait(E.v, sT, t_t)
            v_tp = sV.inc(E.v.reduce_max(s2tm[:, tb:tb + 1], ps_tp[:], AX))
            E.wait(E.t, sV, v_tp)
            t_m2 = sT.inc(E.t.transpose(ps_tp[:], rmin2[:, tb * 128:(tb + 1) * 128], identf[:]))
            E.wait(E.v, sT, t_m2)
            v_tp = sV.inc(E.v.tensor_reduce(mn2tm[:, tb:tb + 1], ps_tp[:], AX, ALU.min, negate=True))
        E.wait(E.v, sV, v_tp)
        sV.inc(E.v.tensor_tensor(s2tm[:], s2tm[:], mn2tm[:], ALU.max))
        E.wait(E.v, sV, sV.n)
        sV.inc(E.v.tensor_scalar_max(s2tm[:], s2tm[:], QEPS))
        E.wait(E.v, sV, sV.n)
        sV.inc(E.v.tensor_scalar_mul(i2tm[:], s2tm[:], 1.0 / 127.0))
        E.wait(E.v, sV, sV.n)
        v_s2 = sV.inc(E.v.reciprocal(s2tm[:], i2tm[:]))
        for (tmt, rep) in ((s2tm, s2rep), (i2tm, i2rep)):
            E.wait(E.t, sV, sV.n)
            t_b = sT.inc(E.t.transpose(ps_tp[0:NTB, 0:128], tmt[:, 0:NTB], identf[:]))
            E.wait(E.v, sT, t_b)
            E.wait(E.v, sy, sy.n)
            v_b = sV.inc(E.v.tensor_copy(xpose[0:NTB, 0:128], ps_tp[0:NTB, 0:128]))
            E.wait(E.y, sV, v_b)
            E.wait(E.y, sy, sy.n)
            dmay(d_rowb[0:NTB], xpose[0:NTB, 0:128], srow)
            E.wait(E.y, sy, sy.n)
            E.wait(E.y, srow, srow.n)
            g_b = dmay(rep[:], d_rowb.rearrange("a b -> (a b)")[0:T]
                       .unsqueeze(0).broadcast_to([128, T]))
        rep_G = sy.n
        v_q = sV.n
        for m in range(16):
            aa = ln2_tile(m, None, v_q)
            E.wait(E.v, sA, aa)
            E.wait(E.v, sy, rep_G)
            sV.inc(E.v.tensor_mul(h2t[:], h2t[:], s2rep[:]))
            E.wait(E.v, sV, sV.n)
            sV.inc(E.v.tensor_scalar(h2t[:], h2t[:], MAGIC, MAGIC, ALU.add, ALU.subtract))
            E.wait(E.v, sV, sV.n)
            sV.inc(E.v.tensor_scalar(h2t[:], h2t[:], -128.0, 127.0, ALU.max, ALU.min))
            E.wait(E.v, sV, sV.n)
            v_q = sV.inc(E.v.tensor_mul(qx_sl(m), h2t[:], i2rep[:]))
        qx2_V = sV.n
        if dbg_tensor("qx2", [NKT, 128, T], BF16):
            E.wait(E.y, sV, qx2_V)
            for m in range(16):
                dmay(dbg_outs["qx2"][m], qx_sl(m))

    seg56.close()

    phase_barrier()
    seg7b, sb7b = seg("right")
    rmax3 = sb7b("rmax3", [128, T])
    rmin3 = sb7b("rmin3", [128, T])
    ssq3 = sb7b("ssq3", [128, T])
    with contextlib.ExitStack() as stk:
        sb, ps = mk(stk)
        wg_sb = sb("wg_sb", [128, 2 * NKT * 128], BF16)
        wu_sb = sb("wu_sb", [128, 2 * NKT * 128], BF16)
        wg_sl = lambda par, k: wg_sb[:, par * NKT * 128 + k * 128:par * NKT * 128 + (k + 1) * 128]
        wu_sl = lambda par, k: wu_sb[:, par * NKT * 128 + k * 128:par * NKT * 128 + (k + 1) * 128]
        hmid = sb("hmid", [128, T], BF16)
        gsc = sb("gsc", [128, T])
        silug = sb("silug", [128, T])
        hsq = sb("hsq", [128, T])
        ps_g2 = [ps(f"ps_g{i}", [128, 1024]) for i in range(2)]
        ps_u2 = [ps(f"ps_u{i}", [128, 1024]) for i in range(2)]
        E.wait(E.v, sV, qx2_V)
        sV.inc(E.v.memset(rmax3[:], 0.0))
        sV.inc(E.v.memset(rmin3[:], 0.0))
        v_f0 = sV.inc(E.v.memset(ssq3[:], 0.0))
        E.wait(E.s, sV, v_f0)
        E.wait(E.g, sCC, cc3)
        pieces = col_splits(T, 512)
        groups = [pieces[i:i + 2] for i in range(0, len(pieces), 2)]
        mm_T = [0, 0]
        ev_Vp = [v_f0, v_f0]
        ev_Ap = [0, 0]
        hm_y = 0
        for it in range(64):
            par = it % 2
            E.wait(E.y, sT, mm_T[par])
            r, loc = divmod(it, 8)
            g_lg = dmay(wg_sb[:, par * NKT * 128:(par + 1) * NKT * 128],
                        d_agqo.ap()[r, 0][:, loc * 2048:(loc + 1) * 2048], sg7[par])
            g_lu = dmay(wu_sb[:, par * NKT * 128:(par + 1) * NKT * 128],
                        d_agqo.ap()[r, 1][:, loc * 2048:(loc + 1) * 2048], shm[par])
            E.wait(E.t, sg7[par], g_lg)
            E.wait(E.t, shm[par], g_lu)
            v_h = 0
            for gi, grp in enumerate(groups):
                pb = gi % 2
                g0 = grp[0][0]
                gw = sum(n for _, n in grp)
                E.wait(E.t, sV, ev_Vp[pb])
                E.wait(E.t, sA, ev_Ap[pb])
                for (coff, cnb) in grp:
                    po = coff - g0
                    for k in range(NKT):
                        E.t.matmul(ps_g2[pb][:, po:po + cnb], wg_sl(par, k),
                                   qx_sl(k)[:, coff:coff + cnb],
                                   start=(k == 0), stop=(k == NKT - 1),
                                   skip_group_check=True)
                    for k in range(NKT):
                        mmu = E.t.matmul(ps_u2[pb][:, po:po + cnb], wu_sl(par, k),
                                         qx_sl(k)[:, coff:coff + cnb],
                                         start=(k == 0), stop=(k == NKT - 1),
                                         skip_group_check=True)
                t_gu = sT.inc(mmu)
                mm_T[par] = t_gu
                E.wait(E.s, sT, t_gu)
                a_g = sA.inc(E.s.activation(silug[:, g0:g0 + gw], ps_g2[pb][:, 0:gw],
                                            AF.Silu, scale=swB[:, 0:1]))
                E.wait(E.v, sA, a_g)
                if gi == 0:
                    E.wait(E.v, sy1s, hm_y)
                v_h = sV.inc(E.v.scalar_tensor_tensor(
                    hmid[:, g0:g0 + gw], ps_u2[pb][:, 0:gw], swB[:, 1:2],
                    silug[:, g0:g0 + gw], ALU.mult, ALU.mult))
                E.wait(E.s, sV, v_h)
                sA.inc(E.s.activation(gsc[:, g0:g0 + gw], hmid[:, g0:g0 + gw],
                                      AF.Identity, scale=lnw_sb[:, it:it + 1]))
                a_sq2 = sA.inc(E.s.activation(hsq[:, g0:g0 + gw], hmid[:, g0:g0 + gw],
                                              AF.Square))
                ev_Ap[pb] = a_g
                E.wait(E.v, sA, a_sq2)
                sV.inc(E.v.tensor_add(ssq3[:, g0:g0 + gw], ssq3[:, g0:g0 + gw],
                                      hsq[:, g0:g0 + gw]))
                sV.inc(E.v.tensor_tensor(rmax3[:, g0:g0 + gw], rmax3[:, g0:g0 + gw],
                                         gsc[:, g0:g0 + gw], ALU.max))
                ev_Vp[pb] = sV.inc(E.v.tensor_tensor(rmin3[:, g0:g0 + gw],
                                                     rmin3[:, g0:g0 + gw],
                                                     gsc[:, g0:g0 + gw], ALU.min))
            E.wait(E.g, sV, v_h)
            hm_y = dmag(d_hmid[it], hmid[:], sy1s)
        ffn1_G = sG.n
        ffn1_V = sV.n
        hmid_Y = sy.n

    seg67.close()

    phase_barrier()
    seg78, sb78 = seg("left")
    rs3rep = sb78("rs3rep", [128, T])
    cmb_rep = sb78("cmb_rep", [128, T])
    with contextlib.ExitStack() as stk:
        sb, ps = mk(stk)
        s3tm = sb("s3tm", [128, NTB])
        r3tm = sb("r3tm", [128, NTB])
        xpose = sb("xpose7", [128, 128])
        ps_tp = ps("p7_tp", [128, 128])
        mn3tm = sb("mn3tm", [128, NTB])
        v_tp = ffn1_V
        for tb in range(NTB):
            E.wait(E.t, sV, v_tp)
            E.wait(E.t, sG, ffn1_G)
            t_t = sT.inc(E.t.transpose(ps_tp[:], rmax3[:, tb * 128:(tb + 1) * 128], identf[:]))
            E.wait(E.v, sT, t_t)
            v_tp = sV.inc(E.v.reduce_max(s3tm[:, tb:tb + 1], ps_tp[:], AX))
            E.wait(E.t, sV, v_tp)
            t_m2 = sT.inc(E.t.transpose(ps_tp[:], rmin3[:, tb * 128:(tb + 1) * 128], identf[:]))
            E.wait(E.v, sT, t_m2)
            v_tp = sV.inc(E.v.tensor_reduce(mn3tm[:, tb:tb + 1], ps_tp[:], AX, ALU.min, negate=True))
            E.wait(E.t, sV, v_tp)
            t_t2 = sT.inc(E.t.transpose(ps_tp[:], ssq3[:, tb * 128:(tb + 1) * 128], identf[:]))
            E.wait(E.v, sT, t_t2)
            v_tp = sV.inc(E.v.tensor_reduce(r3tm[:, tb:tb + 1], ps_tp[:], AX, ALU.add))
        E.wait(E.v, sV, v_tp)
        sV.inc(E.v.tensor_tensor(s3tm[:], s3tm[:], mn3tm[:], ALU.max))
        E.wait(E.v, sV, sV.n)
        sV.inc(E.v.tensor_scalar(r3tm[:], r3tm[:], 1.0 / I_FFN, RMS_EPS_F, ALU.mult, ALU.add))
        E.wait(E.v, sV, sV.n)
        v_r3 = sV.inc(E.v.reciprocal(r3tm[:], r3tm[:]))
        E.wait(E.s, sV, v_r3)
        a_r3 = sA.inc(E.s.activation(r3tm[:], r3tm[:], AF.Sqrt))
        i3tm = sb("i3tm", [128, NTB])
        E.wait(E.v, sA, a_r3)
        sV.inc(E.v.tensor_mul(s3tm[:], s3tm[:], r3tm[:]))
        sV.inc(E.v.tensor_scalar_max(s3tm[:], s3tm[:], QEPS))
        sV.inc(E.v.tensor_scalar_mul(i3tm[:], s3tm[:], 1.0 / 127.0))
        sV.inc(E.v.reciprocal(s3tm[:], i3tm[:]))
        v_rs3 = sV.inc(E.v.tensor_mul(s3tm[:], s3tm[:], r3tm[:]))
        for (tmt, rep) in ((s3tm, rs3rep), (i3tm, cmb_rep)):
            E.wait(E.t, sV, sV.n)
            t_b = sT.inc(E.t.transpose(ps_tp[0:NTB, 0:128], tmt[:, 0:NTB], identf[:]))
            E.wait(E.v, sT, t_b)
            E.wait(E.v, sy, sy.n)
            v_b = sV.inc(E.v.tensor_copy(xpose[0:NTB, 0:128], ps_tp[0:NTB, 0:128]))
            E.wait(E.y, sV, v_b)
            E.wait(E.y, sy, sy.n)
            dmay(d_rowb[0:NTB], xpose[0:NTB, 0:128], srow)
            E.wait(E.y, sy, sy.n)
            E.wait(E.y, srow, srow.n)
            g_b = dmay(rep[:], d_rowb.rearrange("a b -> (a b)")[0:T]
                       .unsqueeze(0).broadcast_to([128, T]))
        E.wait(E.v, sy, sy.n)
        v_cmb = sV.inc(E.v.tensor_scalar_mul(cmb_rep[:], cmb_rep[:], swB[:, 2:3]))
        if dbg_tensor("ffnrows", [2, 128, T]):
            E.wait(E.y, sV, sV.n)
            dmay(dbg_outs["ffnrows"][0], rs3rep[:])
            dmay(dbg_outs["ffnrows"][1], cmb_rep[:])

    seg7b.close()

    phase_barrier()
    with contextlib.ExitStack() as stk:
        sb, ps = mk(stk)
        n_tq = ceil_div(T, 512)
        n_tqp = ceil_div(n_tq, 2)
        qh2 = sb("qh2", [128, min(2, n_tq) * 64 * 512], BF16)
        qh_sl = lambda ti, k: qh2[:, (ti * 64 + k) * 512:(ti * 64 + k + 1) * 512]
        wd_sb = sb("wd_sb", [128, 2 * 512], BF16)
        wd_sl = lambda par, m: wd_sb[:, par * 512 + m * 128:par * 512 + (m + 1) * 128]
        hm_st = sb("hm_st", [128, 4 * 1024], BF16)
        hm_sl = lambda s: hm_st[:, s * 1024:(s + 1) * 1024]
        x1_st = sb("x1_st", [128, 512])
        outsb = sb("outsb", [128, 2 * 512])
        o_sl = lambda par: outsb[:, par * 512:(par + 1) * 512]
        qsc = sb("qsc", [128, 4 * 1024])
        qs_sl = lambda s: qsc[:, s * 1024:(s + 1) * 1024]
        ps_d = [ps(f"ps_d{i}", [128, 512]) for i in range(8)]
        aq_s = [0, 0, 0, 0]
        qv_s = [0, 0, 0, 0]
        ev_ps = [0] * 8
        wd_T = [0, 0]
        dn_V = sV.n
        dn_T = sT.n
        qh_T = sT.n
        for tqp in range(n_tqp):
            tqs = [t for t in (2 * tqp, 2 * tqp + 1) if t < n_tq]
            for mq in range(4):
                for k in range(64):
                    par = k % 2
                    if mq == 0 and par == 0:
                        for ti, tq in enumerate(tqs):
                            s = ti * 2 + (k // 2) % 2
                            coff = tq * 512
                            cnb = min(512, T - coff)
                            hm3 = hm_sl(s).rearrange("p (j c) -> p j c", c=512)
                            qs3 = qs_sl(s).rearrange("p (j c) -> p j c", c=512)
                            E.wait(E.y, sA, aq_s[s])
                            E.wait(E.y, sT, qh_T)
                            y_h = dmay(hm3[:, :, 0:cnb],
                                       d_hmid[k:k + 2, :, coff:coff + cnb]
                                       .rearrange("j p c -> p j c"), shm[(k // 2) % 2])
                            E.wait(E.s, shm[(k // 2) % 2], y_h)
                            E.wait(E.s, sV, qv_s[s])
                            for dk in range(2):
                                aq = sA.inc(E.s.activation(
                                    qs_sl(s)[:, dk * 512:dk * 512 + cnb],
                                    hm_sl(s)[:, dk * 512:dk * 512 + cnb],
                                    AF.Identity, scale=lnw_sb[:, k + dk:k + dk + 1]))
                            aq_s[s] = aq
                            E.wait(E.v, sA, aq)
                            E.wait(E.v, sT, qh_T)
                            sV.inc(E.v.tensor_tensor(
                                qs3[:, :, 0:cnb], qs3[:, :, 0:cnb],
                                rs3rep[:, coff:coff + cnb].unsqueeze(1)
                                .broadcast_to([128, 2, cnb]), ALU.mult))
                            sV.inc(E.v.tensor_scalar(
                                qs3[:, :, 0:cnb], qs3[:, :, 0:cnb],
                                MAGIC, MAGIC, ALU.add, ALU.subtract))
                            qv_s[s] = sV.inc(E.v.tensor_scalar(
                                qh2[:, (ti * 64 + k) * 512:(ti * 64 + k + 2) * 512]
                                .rearrange("p (j c) -> p j c", c=512)[:, :, 0:cnb],
                                qs3[:, :, 0:cnb],
                                -128.0, 127.0, ALU.max, ALU.min))
                        qv_k = qv_s[(len(tqs) - 1) * 2 + (k // 2) % 2]
                    r, loc = divmod(k, 8)
                    wsrc = d_agqo.ap()[r, 2][:, loc * 2048 + mq * 512:
                                             loc * 2048 + mq * 512 + 512]
                    if par == 0:
                        E.wait(E.g, sT, wd_T[0])
                        g_l = dmag(wd_sb[:, 0:512], wsrc, swd[0])
                        E.wait(E.t, swd[0], g_l)
                    else:
                        E.wait(E.y, sT, wd_T[1])
                        g_l = dmay(wd_sb[:, 512:1024], wsrc, swd[1])
                        E.wait(E.t, swd[1], g_l)
                    if mq == 0:
                        E.wait(E.t, sV, qv_k)
                    for m4 in range(4):
                        for ti, tq in enumerate(tqs):
                            cnb = min(512, T - tq * 512)
                            if k == 0 and (mq or tqp):
                                E.wait(E.t, sV, ev_ps[ti * 4 + m4])
                            mm = E.t.matmul(ps_d[ti * 4 + m4][:, 0:cnb],
                                            wd_sl(par, m4), qh_sl(ti, k)[:, 0:cnb],
                                            start=(k == 0), stop=(k == 63),
                                            skip_group_check=True)
                    dn_T = sT.inc(mm)
                    wd_T[par] = dn_T
                for m4 in range(4):
                    gm = mq * 4 + m4
                    for ti, tq in enumerate(tqs):
                        coff = tq * 512
                        cnb = min(512, T - coff)
                        opar = (m4 + ti) % 2
                        E.wait(E.s, sV, dn_V)
                        E.s.dma_start(out=x1_st[:, 0:cnb],
                                      in_=d_x1[gm, :, coff:coff + cnb]).then_inc(sx8, 16)
                        sx8.n += 16
                        y_x = sx8.n
                        E.wait(E.v, sT, dn_T)
                        E.wait(E.v, sx8, y_x)
                        E.wait(E.v, sout, sout.n)
                        sV.inc(E.v.tensor_tensor(o_sl(opar)[:, 0:cnb],
                                                 ps_d[ti * 4 + m4][:, 0:cnb],
                                                 cmb_rep[:, coff:coff + cnb], ALU.mult))
                        v_o = sV.inc(E.v.scalar_tensor_tensor(
                            o_sl(opar)[:, 0:cnb], o_sl(opar)[:, 0:cnb], gb_c(gm),
                            x1_st[:, 0:cnb], ALU.mult, ALU.add))
                        ev_ps[ti * 4 + m4] = v_o
                        E.wait(E.s, sV, v_o)
                        E.s.dma_start(out=dout[gm, :, coff:coff + cnb],
                                      in_=o_sl(opar)[:, 0:cnb]).then_inc(sout, 16)
                        sout.n += 16
                        dn_V = v_o
            qh_T = dn_T
        for s_ in ALLQ:
            E.wait(E.y, s_, s_.n)
            E.wait(E.g, s_, s_.n)

    seg78.close()
    with nc.Block() as block:
        E.replay(block)
    root.close()
    return nc, sorted(dbg_outs.keys())



def stage_inputs(inputs, nch=16, mode="full", sinit_list=None):
    T = nch * Q
    local = (mode == "local")
    f32 = lambda k: np.ascontiguousarray(np.asarray(inputs[k], dtype=np.float32))
    x = f32("L0_attn_x")
    W_in = f32("L0_attn_W_in")
    conv_w = f32("L0_attn_conv_w")
    dt_bias = f32("L0_attn_dt_bias")
    A_log = f32("L0_attn_A_log")
    D = f32("L0_attn_D")
    norm_w = f32("L0_attn_norm_w")
    W_out = f32("L0_attn_W_out")
    w_gate = f32("L0_ffn_w_gate")
    w_up = f32("L0_ffn_w_up")
    w_down = f32("L0_ffn_w_down")
    ln_w = f32("L0_ffn_ln_weight")
    c = f32("L0_c")
    wmod = f32("L0_wmod")

    WTp = np.zeros((D_MODEL, NOT * 128), dtype=np.float32)
    WTp[:, :DIN] = W_in.T
    WinT = np.ascontiguousarray(WTp.reshape(NKT, 128, NOT, 128).transpose(2, 0, 1, 3))
    WoutT = np.ascontiguousarray(W_out.T.reshape(32, 128, 16, 128).transpose(2, 0, 1, 3))
    gflat = np.ascontiguousarray(
        w_gate.T.reshape(NKT, 128, 64, 128).transpose(2, 0, 1, 3)).reshape(-1)
    uflat = np.ascontiguousarray(
        w_up.T.reshape(NKT, 128, 64, 128).transpose(2, 0, 1, 3)).reshape(-1)
    dflat = np.ascontiguousarray(
        w_down.T.reshape(64, 128, 16, 128).transpose(0, 2, 1, 3)).reshape(-1)
    nsl = gflat.size // 8

    convw = np.ascontiguousarray(conv_w.reshape(34, 128, 4).transpose(1, 0, 2).reshape(128, 136))
    dtbA = np.ascontiguousarray(np.stack([dt_bias, A_log], axis=1))
    Drep = np.ascontiguousarray(
        D[2 * np.arange(32)[None, :] + (np.arange(128)[:, None] // 64)])
    normw = np.ascontiguousarray(norm_w.reshape(32, 128).T)
    lnw = np.ascontiguousarray(ln_w.reshape(64, 128).T)
    cT = np.ascontiguousarray(c.T)

    in_maps = []
    for core in range(8):
        b, h = divmod(core, 2)
        start = h * T
        if start - 3 >= 0:
            pre = x[b, start - 3:start]
        else:
            pre = x[b, 0:3]
        xs = np.concatenate([pre, x[b, start:start + T]], axis=0)
        xT = np.ascontiguousarray(xs.T.reshape(NKT, 128, T + 3))
        nmt = 96 if local else 12
        cols = np.empty(128 * nmt, dtype=np.int64)
        for mt in range(nmt):
            g = mt if local else core * 12 + mt
            v, t = divmod(g, 16)
            cols[mt * 128:(mt + 1) * 128] = v * D_MODEL + t * 128 + np.arange(128)
        wms = wmod[:, cols]
        wmod_s = np.ascontiguousarray(wms.reshape(NKT, 128, nmt, 128).transpose(2, 0, 1, 3))
        nslabs = 64 if local else 8
        def _pm(flat):
            sl = flat if local else flat[core * nsl:(core + 1) * nsl]
            return sl.reshape(nslabs, 16, 128, 128).transpose(2, 0, 1, 3).reshape(128, -1)
        wq_slice = np.ascontiguousarray(np.stack([_pm(gflat), _pm(uflat), _pm(dflat)]))
        masks = np.zeros((128, 8), dtype=np.float32)
        masks[:, 0] = 1.0 if h == 1 else 0.0
        masks[:, 1] = 1.0 if h == 1 else 0.0
        masks[:, 2] = 1.0 if h == 0 else 0.0
        masks[:, 3 + b] = 1.0
        masks[:, 7] = 1.0
        im = {
            "xT": xT, "WinT": WinT, "WoutT": WoutT, "wq_slice": wq_slice,
            "wmod_s": wmod_s, "cT": cT, "convw": convw, "dtbA": dtbA,
            "Drep": Drep, "normw": normw, "lnw": lnw, "masks": masks,
        }
        if local:
            im["SinitT"] = (np.zeros((128, HID), dtype=np.float32)
                            if sinit_list is None else
                            np.ascontiguousarray(sinit_list[core].astype(np.float32)))
        in_maps.append(im)
    return in_maps


def kernel(**inputs):
    nch = 16
    in_maps = stage_inputs(inputs, nch)
    nc, _ = build_program(nch)
    res = run_bass_kernel_spmd(nc, in_maps, core_ids=list(range(8)))
    T = nch * Q
    out = np.empty((4, 4096, 2048), dtype=np.float32)
    for core in range(8):
        b, h = divmod(core, 2)
        o = np.asarray(res.results[core]["out"])
        out[b, h * T:(h + 1) * T, :] = o.reshape(D_MODEL, T).T
    return out


# version 23
# speedup vs baseline: 1.1462x; 1.0086x over previous
import contextlib
import numpy as np
import concourse.bass as bass
import concourse.mybir as mybir
from concourse.bass_utils import run_bass_kernel_spmd
from concourse import library_config

F32 = mybir.dt.float32
BF16 = mybir.dt.bfloat16
ALU = mybir.AluOpType
AF = mybir.ActivationFunctionType
AX = mybir.AxisListType.X

MAGIC = 12582912.0
D_MODEL = 2048
HID = 4096
H = 64
P = 64
NSTATE = 128
KCONV = 4
Q = 128
CONV = 4352
DIN = 8512
I_FFN = 8192
NKT = 16
NOT = 67
LN_EPS = 1e-5
RMS_EPS_M = 1e-5
RMS_EPS_F = 1e-6
QEPS = 1e-5

ENGINES = ("tensor", "vector", "scalar", "gpsimd", "sync")


class _Call:
    __slots__ = ("op", "args", "kwargs", "incs", "auto_sem")

    def __init__(self, op, args, kwargs):
        self.op, self.args, self.kwargs = op, args, kwargs
        self.incs = []
        self.auto_sem = None

    def then_inc(self, sem, k):
        self.incs.append((sem, k))
        return self


_NOAUTO = {"wait_ge", "load_library", "collective_compute", "dma_start",
           "dma_start_transpose", "sem_inc", "engine_nop", "nop"}
_AUTO_ENGINES = {"vector", "scalar", "gpsimd"}


class _EngRec:
    def __init__(self, em, name):
        object.__setattr__(self, "_em", em)
        object.__setattr__(self, "_name", name)

    def __getattr__(self, op):
        em, name = self._em, self._name

        def call(*a, **k):
            auto = name in _AUTO_ENGINES and op not in _NOAUTO
            if auto:
                sem = em.auto_sems.get(name)
                if sem is not None and sem.n > 0:
                    em.streams[name].append(_Call("wait_ge", (sem.h, sem.n), {}))
            c = _Call(op, a, k)
            em.streams[name].append(c)
            if auto and sem is not None:
                c.incs.append((sem, 1))
                sem.n += 1
                c.auto_sem = sem
            return c
        return call


class Sem:
    def __init__(self, handle, queue=False):
        self.h = handle
        self.n = 0
        self.queue = queue

    def inc(self, call, k=1):
        if getattr(call, "auto_sem", None) is self:
            return self.n
        call.then_inc(self, k)
        self.n += k
        return self.n


class Emitter:
    def __init__(self):
        self.streams = {e: [] for e in ENGINES}
        self.auto_sems = {}
        self.t = _EngRec(self, "tensor")
        self.v = _EngRec(self, "vector")
        self.s = _EngRec(self, "scalar")
        self.g = _EngRec(self, "gpsimd")
        self.y = _EngRec(self, "sync")

    def wait(self, eng_rec, sem, val=None):
        v = sem.n if val is None else val
        assert v <= sem.n, "wait on future semaphore value"
        if sem.queue and 0 < v < sem.n:
            v = sem.n
        if v > 0:
            eng_rec.wait_ge(sem.h, v)

    def replay(self, block):
        def make(name):
            def body(eng):
                for idx, c in enumerate(self.streams[name]):
                    try:
                        instr = getattr(eng, c.op)(*c.args, **c.kwargs)
                    except Exception:
                        import sys as _s
                        print(f"REPLAY FAIL {name}[{idx}] op={c.op}", file=_s.stderr)
                        raise
                    for (sem, k) in c.incs:
                        instr.then_inc(sem.h, k)
            return body
        for name in ENGINES:
            if self.streams[name]:
                getattr(block, name)(make(name))


def ceil_div(a, b):
    return (a + b - 1) // b


def col_splits(total, maxn=512):
    out = []
    off = 0
    while off < total:
        n = min(maxn, total - off)
        out.append((off, n))
        off += n
    return out


def build_program(nch=16, dbg=(), mode="full"):
    T = nch * Q
    TC = T + 3
    assert nch % 2 == 0
    npr = nch // 2
    NTB = ceil_div(T, 128)
    LOCAL = (mode == "local")
    NMT = 96 if LOCAL else 12

    nc = bass.Bass("TRN2", target_bir_lowering=False)
    E = Emitter()
    dbg = set(dbg)

    dxT = nc.dram_tensor("xT", [NKT, 128, TC], F32, kind="ExternalInput").ap()
    dWin = nc.dram_tensor("WinT", [NOT, NKT, 128, 128], F32, kind="ExternalInput").ap()
    dWout = nc.dram_tensor("WoutT", [16, 32, 128, 128], F32, kind="ExternalInput").ap()
    nwsl = 64 if LOCAL else 8
    dwq = nc.dram_tensor("wq_slice", [3, 128, nwsl * 2048], F32, kind="ExternalInput").ap()
    dwms = nc.dram_tensor("wmod_s", [NMT, NKT, 128, 128], F32, kind="ExternalInput").ap()
    dSin = nc.dram_tensor("SinitT", [128, HID], F32, kind="ExternalInput").ap() if LOCAL else None
    dcT = nc.dram_tensor("cT", [D_MODEL, 4], F32, kind="ExternalInput").ap()
    dconvw = nc.dram_tensor("convw", [128, 136], F32, kind="ExternalInput").ap()
    ddtbA = nc.dram_tensor("dtbA", [64, 2], F32, kind="ExternalInput").ap()
    dDrep = nc.dram_tensor("Drep", [128, 32], F32, kind="ExternalInput").ap()
    dnormw = nc.dram_tensor("normw", [128, 32], F32, kind="ExternalInput").ap()
    dlnw = nc.dram_tensor("lnw", [128, 64], F32, kind="ExternalInput").ap()
    dmasks = nc.dram_tensor("masks", [128, 8], F32, kind="ExternalInput").ap()
    dout = nc.dram_tensor("out", [NKT, 128, T], F32, kind="ExternalOutput").ap()

    d_spill = nc.dram_tensor("spill_zx", [NOT, 128, TC], BF16).ap()
    d_y1 = nc.dram_tensor("spill_y1", [nch, 32, 128, Q], BF16).ap()
    d_yn = nc.dram_tensor("spill_yn", [32, 128, T], BF16).ap()
    d_x1 = nc.dram_tensor("spill_x1", [NKT, 128, T], F32).ap()
    d_hmid = nc.dram_tensor("spill_hmid", [64, 128, T], BF16).ap()
    shr = "Shared" if not LOCAL else "Local"
    d_armi = nc.dram_tensor("ag_mod_in", [128, 48], F32)
    d_armo = nc.dram_tensor("ag_mod_out", [8, 128, 48], F32, addr_space=shr)
    d_arwi = nc.dram_tensor("ar_w_in", [1, 128], F32)
    d_arwo = nc.dram_tensor("ar_w_out", [1, 128], F32, addr_space=shr)
    d_agqi = nc.dram_tensor("ag_wq_in", [3, 128, 16384], BF16)
    d_agqo = nc.dram_tensor("ag_wq_out", [8, 3, 128, 16384], BF16, addr_space=shr)
    d_arsi = nc.dram_tensor("ar_s_in", [128, HID], F32)
    d_arso = nc.dram_tensor("ar_s_out", [128, HID], F32)
    d_rowb = nc.dram_tensor("row_bounce", [32, 128], F32).ap()
    d_cum = nc.dram_tensor("cum_spill", [64, 16 * Q], F32).ap()
    d_ep = nc.dram_tensor("ep_spill", [64, 16 * Q], BF16).ap()

    dbg_outs = {}

    def dbg_tensor(name, shape, dtype=F32):
        if name in dbg:
            dbg_outs[name] = nc.dram_tensor("dbg_" + name, shape, dtype,
                                            kind="ExternalOutput").ap()
            return dbg_outs[name]
        return None

    root = contextlib.ExitStack()

    def mk(stk):
        def sb(name, shape, dt=F32):
            return stk.enter_context(nc.sbuf_tensor(name, shape, dt))

        def ps(name, shape, dt=F32):
            return stk.enter_context(nc.psum_tensor(name, shape, dt))
        return sb, ps

    sbr, psr = mk(root)

    def seg(side="left"):
        stk = contextlib.ExitStack()

        def sb(name, shape, dt=F32):
            return stk.enter_context(nc.sbuf_tensor(name, shape, dt, side=side))
        return stk, sb

    sems = {}

    def sem(name, queue=False):
        sems[name] = Sem(root.enter_context(nc.semaphore(name)), queue)
        return sems[name]

    sy = sem("q_sync", queue=True)
    sg = sem("q_gp", queue=True)
    szx = [sem("q_zx0", queue=True), sem("q_zx1", queue=True)]
    swi = [sem("q_wi0", queue=True), sem("q_wi1", queue=True)]
    sr3 = [sem("q_r30", queue=True), sem("q_r31", queue=True)]
    sr4 = [sem("q_r40", queue=True), sem("q_r41", queue=True)]
    sy1 = [sem("q_y10", queue=True), sem("q_y11", queue=True)]
    swo = [sem("q_wo0", queue=True), sem("q_wo1", queue=True)]
    sg7 = [sem("q_g70", queue=True), sem("q_g71", queue=True)]
    shm = [sem("q_hm0", queue=True), sem("q_hm1", queue=True)]
    swd = [sem("q_wd0", queue=True), sem("q_wd1", queue=True)]
    sxr = sem("q_xr", queue=True)
    scm = sem("q_cm", queue=True)
    sy1s = sem("q_y1s", queue=True)
    syns = sem("q_yns", queue=True)
    sx1s = sem("q_x1s", queue=True)
    smod = sem("q_mod", queue=True)
    srow = sem("q_row", queue=True)
    sx8 = sem("q_x8", queue=True)
    sout = sem("q_out", queue=True)
    ALLQ = ([sy, sg, sxr, sx8, sout, scm, sy1s, syns, sx1s, smod, srow] + szx +
            swi + sr3 + sr4 + sy1 + swo + sg7 + shm + swd)
    sT = sem("c_T")
    sV = sem("c_V")
    sA = sem("c_A")
    sG = sem("c_G")
    sCC = sem("c_CC")
    E.auto_sems = {"vector": sV, "scalar": sA, "gpsimd": sG}

    def phase_barrier():
        for rec in (E.t, E.v, E.s, E.g, E.y):
            for s_ in [sT, sV, sA, sG] + ALLQ:
                E.wait(rec, s_, s_.n)

    def bounce_bcast(rec_consumers, row_ap, rep_ap, nfree, doff=0):
        E.wait(E.y, sy, sy.n)
        E.y.dma_start(out=d_rowb.rearrange("a b -> (a b)")[doff:doff + nfree].unsqueeze(0),
                      in_=row_ap).then_inc(srow, 16)
        srow.n += 16
        E.wait(E.y, srow, srow.n)
        E.y.dma_start(out=rep_ap,
                      in_=d_rowb.rearrange("a b -> (a b)")[doff:doff + nfree]
                      .unsqueeze(0).broadcast_to([128, nfree])).then_inc(sy, 16)
        sy.n += 16
        return sy.n

    def dmay(out, in_, sem_=None):
        s_ = sem_ or sy
        E.y.dma_start(out=out, in_=in_).then_inc(s_, 16)
        s_.n += 16
        return s_.n

    def dmag(out, in_, sem_=None):
        s_ = sem_ or sg
        E.g.dma_start(out=out, in_=in_).then_inc(s_, 16)
        s_.n += 16
        return s_.n

    ones_col = sbr("ones_col", [128, 1])
    tri = sbr("tri", [128, 128])
    identf = sbr("identf", [128, 128])
    ident = sbr("ident", [128, 128], BF16)
    masks_sb = sbr("masks_s", [128, 8])
    convw_sb = sbr("convw_s", [128, 136])
    dtbA_sb = sbr("dtbA_s", [64, 2])
    Drep_sb = sbr("Drep_s", [128, 32])
    normw_sb = sbr("normw_s", [128, 32])
    lnw_sb = sbr("lnw_s", [128, 64])
    vecs = sbr("vecs", [128, 96])
    swB = sbr("swB", [128, 128])
    rowT = sbr("rowT", [1, max(T, 16 * Q)])

    sa_c = lambda k: vecs[:, 0 + k:1 + k]
    ca1_c = lambda k: vecs[:, 16 + k:17 + k]
    ga_c = lambda k: vecs[:, 32 + k:33 + k]
    sb_c = lambda k: vecs[:, 48 + k:49 + k]
    cb1_c = lambda k: vecs[:, 64 + k:65 + k]
    gb_c = lambda k: vecs[:, 80 + k:81 + k]

    dmay(masks_sb[:], dmasks[:])
    dmay(convw_sb[:], dconvw[:])
    dmay(dtbA_sb[:], ddtbA[:])
    dmay(Drep_sb[:], dDrep[:])
    dmay(normw_sb[:], dnormw[:])
    dmay(lnw_sb[:], dlnw[:])
    ld_small = sy.n

    sV.inc(E.v.memset(ones_col[:], 1.0))
    v0 = sV.inc(E.v.memset(identf[:], 1.0))
    E.wait(E.g, sV, v0)
    g0 = sG.inc(E.g.affine_select(identf[:], identf[:], [[1, 128]], ALU.is_equal, 0.0,
                                  base=0, channel_multiplier=-1))
    E.wait(E.v, sG, g0)
    sV.inc(E.v.tensor_copy(ident[:], identf[:]))
    v0 = sV.inc(E.v.memset(tri[:], 1.0))
    E.wait(E.g, sV, v0)
    g0 = sG.inc(E.g.affine_select(tri[:], tri[:], [[1, 128]], ALU.is_ge, 0.0,
                                  base=0, channel_multiplier=-1))
    E.wait(E.v, sG, g0)

    with contextlib.ExitStack() as stk:
        sb, ps = mk(stk)
        csb = sb("csb", [128, 64])
        dmay(csb[:].rearrange("p (kt b) -> p kt b", b=4),
             dcT.rearrange("(kt p) b -> p kt b", p=128))
        ld_c = sy.n
        siluc = sb("siluc", [128, 64])
        E.wait(E.s, sy, ld_c)
        a_sc = sA.inc(E.s.activation(siluc[:], csb[:], AF.Silu))
        wm_sb = sb("wm_sb", [128, 2 * NKT * 128])
        wm_sl = lambda p_: wm_sb[:, p_ * NKT * 128:(p_ + 1) * NKT * 128]
        modps = ps("modps", [128, 4])
        modpre = sb("modpre", [128, 48]) if not LOCAL else None
        ag4 = sb("ag4", [128, 4 * 96])
        ev_V = 0
        mm_Tm = [0, 0]
        for mt in range(NMT):
            mpar = mt % 2
            wm_sem = smod if mpar == 0 else srow
            E.wait(E.y, sT, mm_Tm[mpar])
            g_l = dmay(wm_sl(mpar).rearrange("p (a c) -> p a c", c=128),
                       dwms[mt].rearrange("a p c -> p a c"), wm_sem)
            E.wait(E.t, wm_sem, g_l)
            if mt == 0:
                E.wait(E.t, sA, a_sc)
            else:
                E.wait(E.t, sV, ev_V)
            for k in range(NKT):
                mm = E.t.matmul(modps[:], wm_sl(mpar)[:, k * 128:(k + 1) * 128],
                                siluc[:, k * 4:k * 4 + 4],
                                start=(k == 0), stop=(k == NKT - 1))
            t_mm = sT.inc(mm)
            mm_Tm[mpar] = t_mm
            E.wait(E.v, sT, t_mm)
            if LOCAL:
                ev_V = sV.inc(E.v.tensor_copy(
                    ag4[:].rearrange("p (b g) -> p b g", g=96)[:, :, mt], modps[:]))
            else:
                ev_V = sV.inc(E.v.tensor_copy(
                    modpre[:].rearrange("p (b mtx) -> p b mtx", mtx=12)[:, :, mt], modps[:]))
        if not LOCAL:
            E.wait(E.y, sV, ev_V)
            dmay(d_armi.ap()[:], modpre[:])
            E.wait(E.g, sy, sy.n)
            cc1 = sCC.inc(E.g.collective_compute(
                "AllGather", ALU.bypass, replica_groups=[list(range(8))],
                ins=[d_armi.ap().opt()], outs=[d_armo.ap().opt()]))
            E.wait(E.y, sCC, cc1)
            E.wait(E.y, sy, ld_small)
            for b in range(4):
                dmay(ag4[:, b * 96:(b + 1) * 96].rearrange("p (r mt) -> p r mt", mt=12),
                     d_armo.ap()[:, :, b * 12:(b + 1) * 12].rearrange("r p mt -> p r mt"))
        modsel = sb("modsel", [128, 96])
        E.wait(E.v, sy, max(sy.n if not LOCAL else 0, ld_small))
        E.wait(E.v, sV, ev_V)
        sV.inc(E.v.tensor_scalar_mul(modsel[:], ag4[:, 0:96], masks_sb[:, 3:4]))
        for b in range(1, 4):
            E.wait(E.v, sV, sV.n)
            sV.inc(E.v.scalar_tensor_tensor(
                modsel[:], ag4[:, b * 96:(b + 1) * 96], masks_sb[:, 3 + b:4 + b],
                modsel[:], ALU.mult, ALU.add))
        v_ms = sV.n
        E.wait(E.v, sV, v_ms)
        sV.inc(E.v.tensor_copy(vecs[:, 0:16], modsel[:, 0:16]))
        sV.inc(E.v.tensor_scalar_add(vecs[:, 16:32], modsel[:, 16:32], 1.0))
        sV.inc(E.v.tensor_copy(vecs[:, 48:64], modsel[:, 48:64]))
        sV.inc(E.v.tensor_scalar_add(vecs[:, 64:80], modsel[:, 64:80], 1.0))
        gtmp = sb("gtmp", [128, 32])
        gsig = sb("gsig", [128, 32])
        sV.inc(E.v.tensor_scalar(gsig[:, 0:16], modsel[:, 32:48], 0.0, 2.0,
                                 ALU.is_ge, ALU.mult))
        sV.inc(E.v.tensor_scalar(gsig[:, 16:32], modsel[:, 80:96], 0.0, 2.0,
                                 ALU.is_ge, ALU.mult))
        v_g1 = sV.inc(E.v.tensor_scalar_sub(gsig[:], gsig[:], 1.0))
        E.wait(E.s, sV, v_ms)
        sA.inc(E.s.activation(gtmp[:, 0:16], modsel[:, 32:48], AF.Abs))
        a_abs = sA.inc(E.s.activation(gtmp[:, 16:32], modsel[:, 80:96], AF.Abs))
        E.wait(E.v, sA, a_abs)
        v_g2 = sV.inc(E.v.tensor_scalar_max(gtmp[:], gtmp[:], 0.25))
        E.wait(E.v, sV, max(v_g1, v_g2))
        sV.inc(E.v.tensor_mul(vecs[:, 32:48], gsig[:, 0:16], gtmp[:, 0:16]))
        v_vecs = sV.inc(E.v.tensor_mul(vecs[:, 80:96], gsig[:, 16:32], gtmp[:, 16:32]))
        if dbg_tensor("vecs", [128, 96]):
            E.wait(E.y, sV, v_vecs)
            dmay(dbg_outs["vecs"][:], vecs[:])

    phase_barrier()
    seg0c = contextlib.ExitStack()

    def sb0c(name, shape, dt=F32):
        return seg0c.enter_context(nc.sbuf_tensor(name, shape, dt, side="left"))
    wqs = sb0c("wqs", [128, 2048])
    wqs2 = sb0c("wqs2", [128, 2048])
    wqo = sb0c("wqo", [128, 2048], BF16)
    wqs_h = lambda h: (wqs, wqs2)[h][:]
    wqo_h = lambda h: wqo[:]
    wabs = sb0c("wabs", [128, 4])
    wtmp = sb0c("wtmp", [128, 2])
    wrow = sb0c("wrow", [1, 128])
    swrow = sb0c("swrow", [1, 128])
    wsum_ps = seg0c.enter_context(nc.psum_tensor("wsum_ps", [1, 4], F32))
    NSLAB = 3 * nwsl
    p1_red = [0]
    p1_ld = {}
    p1_vred = [0]
    sV.inc(E.v.memset(wabs[:, 3:4], 0.0))

    def p1_issue(i):
        ti, r = divmod(i, nwsl)
        E.wait(E.y, sV, p1_red[0])
        p1_ld[i] = dmay(wqs[:], dwq[ti, :, r * 2048:(r + 1) * 2048], smod)

    def p1_reduce(i):
        ti, r = divmod(i, nwsl)
        E.wait(E.v, smod, p1_ld.pop(i))
        if r == 0:
            p1_vred[0] = sV.inc(E.v.tensor_reduce(wabs[:, ti:ti + 1], wqs[:], AX,
                                                  ALU.add, apply_absolute_value=True))
        else:
            sV.inc(E.v.tensor_reduce(wtmp[:, 0:1], wqs[:], AX, ALU.add,
                                     apply_absolute_value=True))
            p1_vred[0] = sV.inc(E.v.tensor_add(wabs[:, ti:ti + 1], wabs[:, ti:ti + 1],
                                               wtmp[:, 0:1]))
        p1_red[0] = p1_vred[0]

    q2_v = [0, 0]
    q2_out = [0, 0]
    q2_next = [0]
    y_swb_r = [0]

    def q2_emit(i):
        par = i % 2
        ti, r = divmod(i, nwsl)
        E.wait(E.y, sV, q2_v[par] or p1_vred[0])
        g_l = dmay(wqs_h(par), dwq[ti, :, r * 2048:(r + 1) * 2048], smod)
        E.wait(E.v, smod, g_l)
        E.wait(E.v, sy, y_swb_r[0])
        sV.inc(E.v.tensor_scalar(wqs_h(par), wqs_h(par), swB[:, 8 + ti:9 + ti],
                                 MAGIC, ALU.mult, ALU.add))
        sV.inc(E.v.tensor_scalar(wqs_h(par), wqs_h(par), MAGIC, -1.0,
                                 ALU.subtract, ALU.max))
        E.wait(E.v, sy1[0], q2_out[0])
        E.wait(E.v, sy1[1], q2_out[1])
        q2_v[par] = sV.inc(E.v.tensor_scalar_min(wqo_h(par), wqs_h(par), 1.0))
        E.wait(E.y, sV, q2_v[par])
        if LOCAL:
            q2_out[par] = dmay(
                d_agqo.ap()[r // 8, ti][:, (r % 8) * 2048:(r % 8 + 1) * 2048],
                wqo_h(par), sy1[par])
        else:
            q2_out[par] = dmay(d_agqi.ap()[ti][:, r * 2048:(r + 1) * 2048],
                               wqo_h(par), sy1[par])

    def q2_pump(k):
        while q2_next[0] < NSLAB and k > 0:
            q2_emit(q2_next[0])
            q2_next[0] += 1
            k -= 1

    seg_dt, sb_dt = seg("left")
    dt_sb = sb_dt("dt_sb", [64, TC])
    with contextlib.ExitStack() as stk:
        sb, ps = mk(stk)
        h1T = sb("h1T", [128, NKT * TC], BF16)
        h1t = lambda k: h1T[:, k * TC:(k + 1) * TC]
        xst = sb("xst", [128, NKT * 512])
        xsq = sb("xsq", [128, NKT * 512])
        stats_stk = contextlib.ExitStack()
        st_ps = stats_stk.enter_context(nc.psum_tensor("st_ps", [33, TC], F32))
        blocks = col_splits(TC, 512)
        a_done = 0
        for (off, nb) in blocks:
            E.wait(E.y, sA, a_done)
            for k in range(NKT):
                dmay(xst[:, k * 512:k * 512 + nb], dxT[k, :, off:off + nb])
            E.wait(E.s, sy, sy.n)
            for k in range(NKT):
                a_done = sA.inc(E.s.activation(xsq[:, k * 512:k * 512 + nb],
                                               xst[:, k * 512:k * 512 + nb], AF.Square))
            E.wait(E.t, sA, a_done)
            for k in range(NKT):
                E.t.matmul(st_ps[0:1, off:off + nb], ones_col[:],
                           xst[:, k * 512:k * 512 + nb],
                           start=(k == 0), stop=(k == NKT - 1), skip_group_check=True)
                mm2 = E.t.matmul(st_ps[32:33, off:off + nb], ones_col[:],
                                 xsq[:, k * 512:k * 512 + nb],
                                 start=(k == 0), stop=(k == NKT - 1), skip_group_check=True)
            t_st = sT.inc(mm2)
        rows = stats_stk.enter_context(
            nc.sbuf_tensor("rows_ln1", [1, 2 * TC], F32, side="right"))
        mrow = rows[:, 0:TC]
        rrow = rows[:, TC:2 * TC]
        E.wait(E.v, sT, sT.n)
        sV.inc(E.v.tensor_scalar_mul(mrow, st_ps[0:1, :], 1.0 / D_MODEL))
        E.wait(E.v, sV, sV.n)
        sV.inc(E.v.tensor_mul(rrow, mrow, mrow))
        E.wait(E.v, sV, sV.n)
        sV.inc(E.v.scalar_tensor_tensor(rrow, st_ps[32:33, :], 1.0 / D_MODEL, rrow,
                                        ALU.mult, ALU.subtract))
        E.wait(E.v, sV, sV.n)
        sV.inc(E.v.tensor_scalar_add(rrow, rrow, LN_EPS))
        E.wait(E.v, sV, sV.n)
        v_r = sV.inc(E.v.reciprocal(rrow, rrow))
        E.wait(E.s, sV, v_r)
        a_r = sA.inc(E.s.activation(rrow, rrow, AF.Sqrt))
        mrep = sb("mrep", [128, TC])
        rrep = sb("rrep", [128, TC])
        E.wait(E.y, sA, a_r)
        E.wait(E.y, sV, v_r)
        bounce_bcast(None, mrow, mrep[:], TC)
        g_m = bounce_bcast(None, rrow, rrep[:], TC)
        g_r = g_m
        stats_stk.close()
        if dbg_tensor("ln1rows", [1, 2 * TC]):
            E.wait(E.y, sA, a_r)
            dmay(dbg_outs["ln1rows"][:], rows[:])
        for bi, (off, nb) in enumerate(blocks):
            E.wait(E.y, sA, sA.n)
            for k in range(NKT):
                dmay(xst[:, k * 512:k * 512 + nb], dxT[k, :, off:off + nb])
            yld = sy.n
            E.wait(E.v, sy, yld)
            sV.inc(E.v.tensor_tensor(
                xsq[:].rearrange("p (k c) -> p k c", c=512)[:, :, 0:nb],
                xst[:].rearrange("p (k c) -> p k c", c=512)[:, :, 0:nb],
                mrep[:, off:off + nb].unsqueeze(1).broadcast_to([128, NKT, nb]),
                ALU.subtract))
            vv = sV.inc(E.v.tensor_tensor(
                xsq[:].rearrange("p (k c) -> p k c", c=512)[:, :, 0:nb],
                xsq[:].rearrange("p (k c) -> p k c", c=512)[:, :, 0:nb],
                rrep[:, off:off + nb].unsqueeze(1).broadcast_to([128, NKT, nb]),
                ALU.mult))
            E.wait(E.s, sV, vv)
            for k in range(NKT):
                sA.inc(E.s.activation(h1t(k)[:, off:off + nb], xsq[:, k * 512:k * 512 + nb],
                                      AF.Identity, bias=sa_c(k), scale=ca1_c(k)))
        E.wait(E.v, sA, sA.n)
        for k in range(NKT):
            v_bm = sV.inc(E.v.tensor_scalar_mul(h1t(k)[:, 0:3], h1t(k)[:, 0:3],
                                                masks_sb[:, 0:1]))
        h1_done_V = sV.n
        if dbg_tensor("h1T", [NKT, 128, TC], BF16):
            E.wait(E.y, sV, h1_done_V)
            for k in range(NKT):
                dmay(dbg_outs["h1T"][k], h1t(k))

        win_sb = sb("win_sb", [128, 3 * NKT * 128], BF16)
        wv_sl = lambda par, k: win_sb[:, par * NKT * 128 + k * 128:par * NKT * 128 + (k + 1) * 128]
        zx_st = sb("zx_st", [128, 2 * TC], BF16)
        zx_sl = lambda par: zx_st[:, par * TC:(par + 1) * TC]
        pj = [ps(f"pj{i}", [128, 512]) for i in range(4)]
        pieces = col_splits(TC, 512)
        win_T = [0, 0, 0]
        wsems = (swi[0], swi[1], sr3[0])
        spill_Y = [0, 0]
        psum_V = [h1_done_V] * 4
        E.wait(E.g, sV, h1_done_V)
        for o in range(NOT):
            par = o % 2
            if not LOCAL:
                if o < NSLAB:
                    if o >= 1:
                        p1_reduce(o - 1)
                    p1_issue(o)
                elif o == NSLAB:
                    p1_reduce(NSLAB - 1)
                    E.wait(E.t, sV, p1_vred[0])
                    t_ws = sT.inc(E.t.matmul(wsum_ps[:], ones_col[:], wabs[:],
                                             start=True, stop=True))
                    E.wait(E.v, sT, t_ws)
                    sV.inc(E.v.memset(wrow[:], 0.0))
                    v_wr = sV.inc(E.v.tensor_copy(wrow[:, 0:4], wsum_ps[:]))
                    E.wait(E.y, sV, v_wr)
                    dmay(d_arwi.ap()[:], wrow[:], srow)
                    E.wait(E.g, srow, srow.n)
                    cc2 = sCC.inc(E.g.collective_compute(
                        "AllReduce", ALU.add, replica_groups=[list(range(8))],
                        ins=[d_arwi.ap().opt()], outs=[d_arwo.ap().opt()]))
                elif o == NSLAB + 4:
                    E.wait(E.s, sCC, cc2)
                    E.s.dma_start(out=swrow[:],
                                  in_=d_arwo.ap()[:]).then_inc(srow, 16)
                    srow.n += 16
                    E.wait(E.v, srow, srow.n)
                    nel = float(I_FFN * D_MODEL)
                    sV.inc(E.v.tensor_scalar(swrow[:, 0:3], swrow[:, 0:3], 1.0 / nel,
                                             QEPS, ALU.mult, ALU.max))
                    v_cm = sV.inc(E.v.reciprocal(swrow[:, 8:11], swrow[:, 0:3]))
                    E.wait(E.y, sV, v_cm)
                    y_swb_r[0] = bounce_bcast(None, swrow[:], swB[:], 128)
                elif NSLAB + 6 <= o < NSLAB + 6 + NSLAB:
                    q2_emit(o - NSLAB - 6)
                    q2_next[0] = o - NSLAB - 5
                elif o == NSLAB + 6 + NSLAB + 1:
                    E.wait(E.g, sy1[0], sy1[0].n)
                    E.wait(E.g, sy1[1], sy1[1].n)
                    cc3 = sCC.inc(E.g.collective_compute(
                        "AllGather", ALU.bypass, replica_groups=[list(range(8))],
                        ins=[d_agqi.ap().opt()], outs=[d_agqo.ap().opt()]))
            par3 = o % 3
            E.wait(E.g, sT, win_T[par3])
            g_l = dmag(win_sb[:, par3 * NKT * 128:(par3 + 1) * NKT * 128]
                       .rearrange("p (a c) -> p a c", c=128),
                       dWin[o].rearrange("a p c -> p a c"), wsems[par3])
            E.wait(E.t, wsems[par3], g_l)
            first_evac = True
            for pi, (poff, pnb) in enumerate(pieces):
                pb = pi % 4
                E.wait(E.t, sV, psum_V[pb])
                for k in range(NKT):
                    mm = E.t.matmul(pj[pb][:, 0:pnb], wv_sl(par3, k),
                                    h1t(k)[:, poff:poff + pnb],
                                    start=(k == 0), stop=(k == NKT - 1))
                t_mm = sT.inc(mm)
                win_T[par3] = t_mm
                E.wait(E.v, sT, t_mm)
                if first_evac:
                    E.wait(E.v, szx[par], szx[par].n)
                    first_evac = False
                vv = sV.inc(E.v.tensor_copy(zx_sl(par)[:, poff:poff + pnb],
                                            pj[pb][:, 0:pnb]))
                if o == NOT - 1:
                    vv = sV.inc(E.v.tensor_copy(dt_sb[:, poff:poff + pnb],
                                                pj[pb][0:64, 0:pnb]))
                psum_V[pb] = vv
            E.wait(E.y, sV, psum_V[(len(pieces) - 1) % 4])
            spill_Y[par] = dmay(d_spill[o], zx_sl(par), szx[par])
        proj_spill_done = (szx[0].n, szx[1].n)
        if dbg_tensor("zx", [4, 128, TC], BF16):
            E.wait(E.y, szx[0], proj_spill_done[0])
            E.wait(E.y, szx[1], proj_spill_done[1])
            for i, o in enumerate((0, 32 + 1, 64, 65)):
                dmay(dbg_outs["zx"][i], d_spill[o])


    seg24, sb24 = seg("right")
    E1t = sb24("E1t", [128, nch * 64], BF16)
    cumT = sb24("cumT", [128, nch * 64])
    E2t = sb24("E2t", [128, nch * 64], BF16)
    decB = sb24("decB", [128, nch * 64])
    DprB = sb24("DprB", [128, nch * 64])

    if LOCAL:
        for i in range(NSLAB):
            p1_issue(i)
            p1_reduce(i)
        E.wait(E.t, sV, p1_vred[0])
        t_ws = sT.inc(E.t.matmul(wsum_ps[:], ones_col[:], wabs[:], start=True, stop=True))
        E.wait(E.v, sT, t_ws)
        sV.inc(E.v.memset(wrow[:], 0.0))
        v_wr = sV.inc(E.v.tensor_copy(wrow[:, 0:4], wsum_ps[:]))
        E.wait(E.v, sV, v_wr)
        sV.inc(E.v.tensor_copy(swrow[:], wrow[:]))
        cc2 = 0

    phase_barrier()
    with contextlib.ExitStack() as stk:
        sb, ps = mk(stk)
        epfm = sb("epfm", [64, T])
        epbf = sb("epbf", [64, T], BF16)
        dt2 = sb("dt2", [64, T])
        acum = sb("acum", [64, T])
        E2fm = sb("E2fm", [64, T])
        zer64 = sb("zer64", [64, T])
        avt = sb("avt", [64, T])
        dcol = sb("dcol", [64, 34])
        negA = sb("negA", [64, 1])
        dcT = sb("dcT", [32, 64])
        tp2 = [ps("tp_ps0", [128, 128]), ps("tp_ps1", [128, 128])]
        E.wait(E.v, sV, sV.n)
        v_z = sV.inc(E.v.memset(zer64[:], 0.0))
        E.wait(E.s, sV, sV.n)
        E.wait(E.s, sy, ld_small)
        spx = sb("spx", [64, T])
        spr = sb("spr", [64, T])
        sA.inc(E.s.activation(spx[:], dt_sb[:, 3:TC], AF.Identity, bias=dtbA_sb[:, 0:1]))
        sA.inc(E.s.activation(spr[:], spx[:], AF.Relu))
        sA.inc(E.s.activation(dt2[:], spx[:], AF.Abs))
        sA.inc(E.s.activation(dt2[:], dt2[:], AF.Exp, scale=-1.0))
        a_sp0 = sA.inc(E.s.activation(dt2[:], dt2[:], AF.Ln, bias=1.0))
        E.wait(E.v, sA, a_sp0)
        v_sp = sV.inc(E.v.tensor_add(dt2[:], dt2[:], spr[:]))
        a_sp = a_sp0
        a_A = sA.inc(E.s.activation(negA[:], dtbA_sb[:, 1:2], AF.Exp))
        E.wait(E.v, sA, sA.n)
        v_negA = sV.inc(E.v.tensor_scalar_mul(negA[:], negA[:], -1.0))
        E.wait(E.v, sV, v_negA)
        v_av = sV.inc(E.v.tensor_scalar_mul(avt[:], dt2[:], negA[:]))
        E.wait(E.v, sV, max(v_z, v_av))
        for ci in range(nch):
            v_cum = sV.inc(E.v.tensor_tensor_scan(
                acum[:, ci * Q:(ci + 1) * Q], zer64[:, ci * Q:(ci + 1) * Q],
                avt[:, ci * Q:(ci + 1) * Q], 0.0, ALU.add, ALU.add))
        E.wait(E.s, sV, sV.n)
        a_ep = sA.inc(E.s.activation(epfm[:], acum[:], AF.Exp))
        E.wait(E.y, sV, sV.n)
        dmay(d_cum[:, 0:T], acum[:], scm)
        clcol = sb("clcol", [64, 16])
        for ci in range(nch):
            sV.inc(E.v.tensor_copy(clcol[:, ci:ci + 1],
                                   acum[:, ci * Q + Q - 1:ci * Q + Q]))
        E.wait(E.v, sA, a_ep)
        for ci in range(nch):
            sV.inc(E.v.tensor_copy(dcol[:, ci:ci + 1],
                                   epfm[:, ci * Q + Q - 1:ci * Q + Q]))
        v_dc = sV.inc(E.v.memset(dcol[:, nch:nch + 1], 1.0))
        for ci in range(1, nch):
            v_dc = sV.inc(E.v.tensor_mul(dcol[:, nch + ci:nch + ci + 1],
                                         dcol[:, nch + ci - 1:nch + ci],
                                         dcol[:, ci - 1:ci]))
        E.wait(E.s, sV, sV.n)
        for ci in range(nch):
            a_e2 = sA.inc(E.s.activation(E2fm[:, ci * Q:(ci + 1) * Q],
                                         acum[:, ci * Q:(ci + 1) * Q], AF.Exp,
                                         scale=-1.0, bias=clcol[:, ci:ci + 1]))
        E.wait(E.v, sA, a_e2)
        v_E2 = sV.inc(E.v.tensor_mul(E2fm[:], E2fm[:], dt2[:]))
        v_pre = sV.n

        if LOCAL:
            nel = float(I_FFN * D_MODEL)
            sV.inc(E.v.tensor_scalar(swrow[:, 0:3], swrow[:, 0:3], 1.0 / nel, QEPS,
                                     ALU.mult, ALU.max))
            v_cm = sV.inc(E.v.reciprocal(swrow[:, 8:11], swrow[:, 0:3]))
            E.wait(E.y, sV, v_cm)
            y_swb_r[0] = bounce_bcast(None, swrow[:], swB[:], 128)

        E.wait(E.t, sV, v_pre)
        t_dc = sT.inc(E.t.transpose(tp2[0][0:2 * nch, 0:64], dcol[:, 0:2 * nch],
                                    identf[0:64, 0:64]))
        E.wait(E.v, sT, t_dc)
        v_dct = sV.inc(E.v.tensor_copy(dcT[0:2 * nch, :], tp2[0][0:2 * nch, 0:64]))
        E.wait(E.y, sV, v_dct)
        E.wait(E.y, sy, y_swb_r[0])
        y_dc = dmay(d_rowb[0:2 * nch, 0:64], dcT[0:2 * nch, :], srow)
        E.wait(E.y, srow, y_dc)
        dmay(decB[:].rearrange("p (c k) -> p c k", k=64),
             d_rowb[0:nch, 0:64].unsqueeze(0).broadcast_to([128, nch, 64]))
        dmay(DprB[:].rearrange("p (c k) -> p c k", k=64),
             d_rowb[nch:2 * nch, 0:64].unsqueeze(0).broadcast_to([128, nch, 64]))

        tpV = [v_dct, v_pre]
        for cp in range(npr):
            c0 = 2 * cp
            for j, (src, dst) in enumerate(((dt2, E1t), (E2fm, E2t), (acum, cumT))):
                pb = (cp * 3 + j) % 2
                E.wait(E.t, sV, tpV[pb])
                E.t.transpose(tp2[pb][:, 0:64], src[:, c0 * Q:(c0 + 1) * Q],
                              identf[0:64, 0:64])
                tr = sT.inc(E.t.transpose(tp2[pb][:, 64:128],
                                          src[:, (c0 + 1) * Q:(c0 + 2) * Q],
                                          identf[0:64, 0:64]))
                E.wait(E.v, sT, tr)
                tpV[pb] = sV.inc(E.v.tensor_copy(dst[:, c0 * 64:(c0 + 2) * 64],
                                                 tp2[pb][:]))
            if LOCAL:
                q2_pump(3)
        if LOCAL:
            q2_pump(NSLAB)
            cc3 = 0
        if dbg_tensor("negacheck", [64, 2 * T + 2]):
            E.wait(E.y, sV, sV.n)
            E.wait(E.y, sA, sA.n)
            E.y.dma_start(out=dbg_outs["negacheck"][:, 0:1], in_=negA[:],
                          allow_slow_non_contiguous=True).then_inc(sy, 16)
            sy.n += 16
            E.y.dma_start(out=dbg_outs["negacheck"][:, 1:2], in_=dtbA_sb[:, 1:2],
                          allow_slow_non_contiguous=True).then_inc(sy, 16)
            sy.n += 16
            dmay(dbg_outs["negacheck"][:, 2:T + 2], avt[:])
            dmay(dbg_outs["negacheck"][:, T + 2:2 * T + 2], dt2[:])
        if dbg_tensor("dtprep", [64, 4 * T]):
            E.wait(E.y, sA, sA.n)
            E.wait(E.y, sV, sV.n)
            dmay(dbg_outs["dtprep"][:, 0:T], dt2[:])
            dmay(dbg_outs["dtprep"][:, T:2 * T], acum[:])
            dmay(dbg_outs["dtprep"][:, 2 * T:3 * T], E2fm[:])
            dmay(dbg_outs["dtprep"][:, 3 * T:4 * T], epfm[:])
        E.wait(E.v, sA, sA.n)
        E.wait(E.v, sV, sV.n)
        v_epbf = sV.inc(E.v.tensor_copy(epbf[:], epfm[:]))
        E.wait(E.y, sV, v_epbf)
        dmay(d_ep[:, 0:T], epbf[:], scm)
    dtprep_G = sG.n
    dtprep_A = sA.n
    seg_dt.close()
    seg0c.close()

    phase_barrier()
    with contextlib.ExitStack() as stk:
        sb, ps = mk(stk)
        XBW = 34 * 131
        xbc_st = sb("xbc_st", [128, 2 * XBW], BF16)
        xb_sl = lambda par: xbc_st[:, par * XBW:(par + 1) * XBW]
        convoX = sb("convoX", [128, 32 * 128], BF16)
        convoBC = sb("convoBC", [128, 2 * 256], BF16)
        bc_sl = lambda par: convoBC[:, par * 256:(par + 1) * 256]
        cacc = sb("cacc", [128, CONV])
        csc = sb("csc", [128, CONV], BF16)
        Mt = sb("Mt", [128, 128], BF16)
        CROW2 = sb("CROW2", [128, 2 * 32 * 128])
        crow_sl = lambda b: CROW2[:, b * 4096:(b + 1) * 4096]
        Mtb2 = sb("Mtb2", [128, 2 * 32 * 128], BF16)
        mtb_sl = lambda b: Mtb2[:, b * 32 * 128:(b + 1) * 32 * 128]
        xs1 = sb("xs1", [128, HID], BF16)
        xs1_sl = lambda par: xs1[:]
        xs2 = sb("xs2", [128, HID], BF16)
        Btm = sb("Btm", [128, 128], BF16)
        S = sb("S", [128, HID])
        Sbf = sb("Sbf", [128, 2 * HID], BF16)
        sbf_sl = lambda par: Sbf[:, par * HID:(par + 1) * HID]
        Erep = sb("Erep", [128, 32 * 128], BF16)
        y1sb = sb("y1sb", [128, HID], BF16)
        ysc = sb("ysc", [128, 512])
        ps_y = [ps(f"ps_y{i}", [128, 512]) for i in range(2)]
        ps_s = [ps(f"ps_s{i}", [128, 512]) for i in range(2)]
        ps_cb = ps("ps_cb", [128, 128])
        ps_tp = ps("ps_tp", [128, 128], BF16)
        ps_t4 = ps("ps_t4", [128, 512], BF16)

        E.wait(E.v, sG, dtprep_G)
        sV.inc(E.v.memset(S[:], 0.0))
        v_S0 = sV.inc(E.v.tensor_copy(sbf_sl(1), S[:]))

        def reload3(ci):
            par = ci % 2
            E.wait(E.g, sV, v_conv[par])
            if ci < 2:
                E.wait(E.g, szx[0], proj_spill_done[0])
                E.wait(E.g, szx[1], proj_spill_done[1])
            for piece in range(4):
                t0 = piece * 9
                t1 = min(34, t0 + 9)
                dmag(xb_sl(par).rearrange("p (t c) -> p t c", c=131)[:, t0:t1, :],
                     d_spill[32 + t0:32 + t1, :, ci * Q:ci * Q + 131]
                     .rearrange("t p c -> p t c"), sr3[par])
            reload_G[par] = sr3[par].n

        reload_G = [0, 0]
        v_conv = [v_S0, v_S0]
        a_cv_prev = 0
        conv_T = [0, 0]
        conv_TX = 0
        v_drep = 0
        mm_btm_T = 0
        v_mt_prev = 0
        tB_prev = 0
        xs1_T = [0, 0]
        sbf_T = [0, 0]
        a_sbf = [0, 0]
        mtb_T = [0, 0]
        v_sub_b = [v_S0, v_S0]
        er_evV = 0
        hg_evV = [v_S0, v_S0]
        st_rdV = [v_S0, v_S0]
        reload3(0)

        for ci in range(nch):
            par = ci % 2
            if ci + 1 < nch:
                reload3(ci + 1)
            E.wait(E.v, sr3[par], reload_G[par])
            E.wait(E.v, sA, a_cv_prev)
            sl_in = xb_sl(par).rearrange("p (t c) -> p t c", c=131)
            wv3 = convw_sb[:].rearrange("p (t k) -> p t k", k=4)
            v_c = sV.inc(E.v.tensor_tensor(
                cacc[:].rearrange("p (t c) -> p t c", c=128),
                sl_in[:, :, 0:Q],
                wv3[:, 0:34, 0:1].broadcast_to([128, 34, Q]), ALU.mult))
            for kk in range(1, KCONV):
                E.wait(E.v, sV, v_c)
                sV.inc(E.v.tensor_tensor(
                    csc[:].rearrange("p (t c) -> p t c", c=128),
                    sl_in[:, :, kk:kk + Q],
                    wv3[:, 0:34, kk:kk + 1].broadcast_to([128, 34, Q]), ALU.mult))
                E.wait(E.v, sV, sV.n)
                v_c = sV.inc(E.v.tensor_add(cacc[:], cacc[:], csc[:]))
            v_conv[par] = v_c
            E.wait(E.s, sV, v_c)
            E.wait(E.s, sT, conv_TX)
            E.wait(E.s, sV, v_drep)
            a_cvx = sA.inc(E.s.activation(convoX[:], cacc[:, 0:32 * 128], AF.Silu))
            E.wait(E.s, sT, conv_T[par])
            a_cv = sA.inc(E.s.activation(bc_sl(par), cacc[:, 32 * 128:CONV], AF.Silu))
            a_cv_prev = a_cv
            E.wait(E.v, sy1s, sy1s.n)
            E.wait(E.v, sA, a_cvx)
            v_drep = sV.inc(E.v.tensor_tensor(
                y1sb[:].rearrange("p (h c) -> p h c", c=128),
                convoX[:].rearrange("p (h c) -> p h c", c=128),
                Drep_sb[:, 0:32].rearrange("p (h one) -> p h one", one=1)
                .broadcast_to([128, 32, 128]), ALU.mult))
            E.wait(E.t, sA, a_cv)
            E.wait(E.t, sV, tB_prev)
            tB = sT.inc(E.t.transpose(ps_tp[:], bc_sl(par)[:, 0:128], ident[:]))
            E.wait(E.v, sT, tB)
            E.wait(E.v, sT, mm_btm_T)
            tB_prev = sV.inc(E.v.tensor_copy(Btm[:], ps_tp[:]))
            E.wait(E.t, sV, max(tB_prev, v_mt_prev))
            t_cb = sT.inc(E.t.matmul(ps_cb[:], bc_sl(par)[:, 0:128],
                                     bc_sl(par)[:, 128:256], start=True, stop=True))
            E.wait(E.v, sT, t_cb)
            v_mt_prev = sV.inc(E.v.tensor_mul(Mt[:], ps_cb[:], tri[:]))
            vx = sV.n
            for g4 in range(8):
                E.wait(E.t, sV, vx)
                for q4 in range(4):
                    xt = g4 * 4 + q4
                    tx = sT.inc(E.t.transpose(ps_t4[:, q4 * 128:(q4 + 1) * 128],
                                              convoX[:, xt * 128:(xt + 1) * 128], ident[:]))
                E.wait(E.v, sT, tx)
                if g4 == 0:
                    E.wait(E.v, sT, mm_btm_T)
                    E.wait(E.v, sT, xs1_T[0])
                e1sl = E1t[:, ci * 64 + 8 * g4:ci * 64 + 8 * g4 + 8]
                e2sl = E2t[:, ci * 64 + 8 * g4:ci * 64 + 8 * g4 + 8]
                sV.inc(E.v.tensor_tensor(
                    xs1_sl(par)[:, g4 * 512:(g4 + 1) * 512].rearrange("p (h q) -> p h q", h=8),
                    ps_t4[:].rearrange("p (h q) -> p h q", h=8),
                    e1sl.rearrange("p (h one) -> p h one", one=1).broadcast_to([128, 8, 64]),
                    ALU.mult))
                vx = sV.inc(E.v.tensor_tensor(
                    xs2[:, g4 * 512:(g4 + 1) * 512].rearrange("p (h q) -> p h q", h=8),
                    ps_t4[:].rearrange("p (h q) -> p h q", h=8),
                    e2sl.rearrange("p (h one) -> p h one", one=1).broadcast_to([128, 8, 64]),
                    ALU.mult))
            conv_TX = tx
            xs_V = sV.n
            st_V = sV.n
            for sgi in range(8):
                pb2 = sgi % 2
                E.wait(E.t, sV, xs_V if sgi < 2 else st_V)
                E.wait(E.t, sV, hg_evV[pb2])
                mm_s = E.t.matmul(ps_s[pb2][:], Btm[:], xs2[:, sgi * 512:(sgi + 1) * 512],
                                  start=True, stop=True)
                t_s = sT.inc(mm_s)
                if sgi == 7:
                    mm_btm_T = t_s
                E.wait(E.v, sT, t_s)
                sV.inc(E.v.tensor_tensor(
                    S[:, sgi * 512:(sgi + 1) * 512].rearrange("p (h q) -> p h q", q=64),
                    S[:, sgi * 512:(sgi + 1) * 512].rearrange("p (h q) -> p h q", q=64),
                    decB[:, ci * 64 + sgi * 8:ci * 64 + sgi * 8 + 8]
                    .rearrange("p (h one) -> p h one", one=1).broadcast_to([128, 8, 64]),
                    ALU.mult))
                st_V = sV.inc(E.v.tensor_add(S[:, sgi * 512:(sgi + 1) * 512],
                                             S[:, sgi * 512:(sgi + 1) * 512],
                                             ps_s[pb2][:]))
                st_rdV[pb2] = st_V
            E.wait(E.s, sV, st_V)
            E.wait(E.s, sT, sbf_T[par])
            a_sbf[par] = sA.inc(E.s.activation(sbf_sl(par), S[:], AF.Identity))
            E.wait(E.g, sV, er_evV)
            dmag(Erep[0:64, :].rearrange("p (j2 i) -> p j2 i", i=Q),
                 d_ep[0:64:2, ci * Q:(ci + 1) * Q]
                 .unsqueeze(0).broadcast_to([64, 32, Q]))
            er_Y = dmag(Erep[64:128, :].rearrange("p (j2 i) -> p j2 i", i=Q),
                        d_ep[1:64:2, ci * Q:(ci + 1) * Q]
                        .unsqueeze(0).broadcast_to([64, 32, Q]))
            ev_V = sV.n
            sbprev = sbf_sl(1 - par)
            for hh in range(2):
                b = hh
                cr_sem = scm if b == 0 else sxr
                E.wait(E.y, sV, v_sub_b[b])
                g_pb = dmay(crow_sl(b).rearrange("p (h i) -> p h i", i=Q),
                            d_cum[hh * 32:(hh + 1) * 32, ci * Q:(ci + 1) * Q]
                            .unsqueeze(0).broadcast_to([128, 32, Q]), cr_sem)
                E.wait(E.v, cr_sem, g_pb)
                E.wait(E.v, sT, mtb_T[b])
                v_sub_b[b] = sV.inc(E.v.tensor_tensor(
                    mtb_sl(b).rearrange("p (h i) -> p h i", i=Q),
                    crow_sl(b).rearrange("p (h i) -> p h i", i=Q),
                    cumT[:, ci * 64 + hh * 32:ci * 64 + hh * 32 + 32]
                    .rearrange("p (h one) -> p h one", one=1).broadcast_to([128, 32, Q]),
                    ALU.subtract))
                v_sub_last = v_sub_b[b]
                E.wait(E.s, sV, v_sub_last)
                sA.inc(E.s.activation(mtb_sl(b), mtb_sl(b), AF.Relu, scale=-1.0))
                a_E = sA.inc(E.s.activation(mtb_sl(b), mtb_sl(b), AF.Exp, scale=-1.0))
                E.wait(E.v, sA, a_E)
                v_mtb = sV.inc(E.v.tensor_tensor(
                    mtb_sl(b).rearrange("p (h i) -> p h i", i=Q),
                    mtb_sl(b).rearrange("p (h i) -> p h i", i=Q),
                    Mt[:].unsqueeze(1).broadcast_to([128, 32, Q]), ALU.mult))
                for hg in range(4):
                    pb = hg % 2
                    E.wait(E.t, sV, max(v_mtb, hg_evV[pb]))
                    if hh == 0 and hg < 2:
                        E.wait(E.t, sA, a_sbf[1 - par])
                        E.wait(E.t, sV, st_rdV[pb])
                    for j in range(4):
                        hpair = (hh * 4 + hg) * 4 + j
                        hl = hg * 8 + 2 * j
                        he, ho = 2 * hpair, 2 * hpair + 1
                        E.t.matmul(ps_y[pb][0:64, j * 128:(j + 1) * 128],
                                   xs1_sl(par)[:, he * 64:(he + 1) * 64],
                                   mtb_sl(b)[:, (hg * 8 + 2 * j) * 128:(hg * 8 + 2 * j + 1) * 128],
                                   start=True, stop=True)
                        E.t.matmul(ps_s[pb][0:64, j * 128:(j + 1) * 128],
                                   sbprev[:, he * 64:(he + 1) * 64],
                                   bc_sl(par)[:, 128:256], start=True, stop=True)
                        E.t.matmul(ps_y[pb][64:128, j * 128:(j + 1) * 128],
                                   xs1_sl(par)[:, ho * 64:(ho + 1) * 64],
                                   mtb_sl(b)[:, (hg * 8 + 2 * j + 1) * 128:(hg * 8 + 2 * j + 2) * 128],
                                   start=True, stop=True)
                        mmy = E.t.matmul(ps_s[pb][64:128, j * 128:(j + 1) * 128],
                                         sbprev[:, ho * 64:(ho + 1) * 64],
                                         bc_sl(par)[:, 128:256], start=True,
                                         stop=True)
                    t_y = sT.inc(mmy)
                    mtb_T[b] = t_y
                    xs1_T[0] = t_y
                    sbf_T[1 - par] = t_y
                    conv_T[par] = t_y
                    hq = (hh * 4 + hg) * 4
                    E.wait(E.v, sT, t_y)
                    if hh == 0 and hg == 0:
                        E.wait(E.v, sg, er_Y)
                    sV.inc(E.v.tensor_tensor(ysc[:], ps_s[pb][:],
                                             Erep[:, hq * 128:hq * 128 + 512], ALU.mult))
                    sV.inc(E.v.tensor_add(ysc[:], ysc[:], ps_y[pb][:]))
                    ev_V = sV.inc(E.v.tensor_add(y1sb[:, hq * 128:hq * 128 + 512],
                                                 y1sb[:, hq * 128:hq * 128 + 512],
                                                 ysc[:]))
                    hg_evV[pb] = ev_V
                    er_evV = ev_V
            if ci == 0 and dbg_tensor("ssm0", [4, 128, HID], BF16):
                E.wait(E.y, sV, sV.n)
                E.wait(E.y, sA, sA.n)
                dmay(dbg_outs["ssm0"][0, :, 0:128], Mt[:])
                dmay(dbg_outs["ssm0"][1], xs1_sl(par))
                dmay(dbg_outs["ssm0"][2], Erep[:])
                dmay(dbg_outs["ssm0"][3, :, 0:32 * 128], mtb_sl(1))
                E.wait(E.v, sy, sy.n)
                E.wait(E.s, sy, sy.n)
                E.wait(E.g, sy, sy.n)
            E.wait(E.g, sV, ev_V)
            dmag(d_y1[ci].rearrange("t p c -> p t c"),
                 y1sb[:].rearrange("p (t c) -> p t c", c=Q), sy1s)
        if dbg_tensor("Stot", [128, HID]):
            E.wait(E.y, sV, sV.n)
            dmay(dbg_outs["Stot"][:], S[:])
        if dbg_tensor("y1", [nch, 32, 128, Q], BF16):
            E.wait(E.y, sy1s, sy1s.n)
            for ci2 in range(nch):
                dmay(dbg_outs["y1"][ci2], d_y1[ci2])
        if not LOCAL:
            E.wait(E.v, sV, sV.n)
            E.wait(E.v, sA, sA.n)
            v_ari = sV.inc(E.v.tensor_scalar_mul(cacc[:, 0:HID], S[:], masks_sb[:, 2:3]))
            E.wait(E.y, sV, v_ari)
            dmay(d_arsi.ap()[:], cacc[:, 0:HID])
            E.wait(E.g, sy, sy.n)
            cc4 = sCC.inc(E.g.collective_compute(
                "AllReduce", ALU.add, replica_groups=[[0, 1], [2, 3], [4, 5], [6, 7]],
                ins=[d_arsi.ap().opt()], outs=[d_arso.ap().opt()]))

    phase_barrier()
    seg45, sb45 = seg("left")
    rstd_rep = sb45("rstd_rep", [128, T])
    with contextlib.ExitStack() as stk:
        sb, ps = mk(stk)
        z_st = sb("z_st", [128, 2 * 32 * 256], BF16)
        z_sl = lambda par: z_st[:, par * 32 * 256:(par + 1) * 32 * 256]
        c_st = sb("c_st", [128, 2 * 259], BF16)
        c_sl = lambda par: c_st[:, par * 259:(par + 1) * 259]
        y1_st = sb("y1_st", [128, 2 * HID], BF16)
        y1_sl = lambda loc: y1_st[:, loc * HID:(loc + 1) * HID]
        siluz2 = sb("siluz2", [128, 2 * HID], BF16)
        sz_sl = lambda par: siluz2[:, par * HID:(par + 1) * HID]
        Sini_c = sb("Sini_c", [128, HID], BF16)
        ysum = sb("ysum", [128, HID])
        yzsq = sb("yzsq", [128, HID])
        ynbf = sb("ynbf", [128, HID], BF16)
        ssqrow = sb("ssqrow", [1, T])
        cacc2 = sb("cacc2", [128, Q])
        csc2 = sb("csc2", [128, Q])
        convC2 = sb("convC2", [128, 2 * Q], BF16)
        cC_sl = lambda par: convC2[:, par * Q:(par + 1) * Q]
        Sinit = sb("Sinit", [128, HID])
        Erep42 = sb("Erep42", [128, 2 * 32 * 128], BF16)
        er_sl = lambda par: Erep42[:, par * 4096:(par + 1) * 4096]
        ysc2 = sb("ysc2", [128, 2048])
        ps_y = ps("p4_y", [128, 2048])
        ps_r = ps("p4_r", [1, 256])

        def reload4(pp):
            par = pp % 2
            E.wait(E.g, sA, a_z_pair[par])
            for piece in range(2):
                t0, t1 = piece * 16, piece * 16 + 16
                dmag(z_sl(par).rearrange("p (t c) -> p t c", c=256)[:, t0:t1, :],
                     d_spill[t0:t1, :, 3 + pp * 256:3 + pp * 256 + 256]
                     .rearrange("t p c -> p t c"), sr4[par])
            dmag(c_sl(par), d_spill[65, :, pp * 256:pp * 256 + 259], sr4[par])
            reload_G[par] = sr4[par].n

        reload_G = [0, 0]
        y1_Y = [0, 0]
        y1_rdV = [0, 0]
        a_z_pair = [0, 0]
        a_c_prev = 0
        conv_T4 = [0, 0]
        silz_V = [0, 0]
        sin_T = 0
        a_sq_prev = 0
        t_r_prev = 0
        row_cpV = [0, 0]
        er_evV4 = [0, 0]
        if LOCAL:
            dmay(Sinit[:], dSin[:])
        else:
            E.wait(E.y, sCC, cc4)
            dmay(Sinit[:], d_arso.ap()[:])
        E.wait(E.v, sy, sy.n)
        sV.inc(E.v.tensor_scalar_mul(Sinit[:], Sinit[:], masks_sb[:, 1:2]))
        reload4(0)
        if npr > 1:
            reload4(1)
        for ci in range(nch):
            pp, local = divmod(ci, 2)
            par = pp % 2
            E.wait(E.y, sV, y1_rdV[local])
            y1_Y[local] = dmay(y1_sl(local).rearrange("p (t c) -> p t c", c=Q),
                               d_y1[ci].rearrange("t p c -> p t c"), sy1[local])
            base = local * Q
            E.wait(E.v, sr4[par], reload_G[par])
            E.wait(E.v, sA, a_c_prev)
            v_c = sV.inc(E.v.tensor_tensor(
                cacc2[:], c_sl(par)[:, base:base + Q],
                convw_sb[:, 33 * 4:33 * 4 + 1].broadcast_to([128, Q]), ALU.mult))
            for kk in range(1, KCONV):
                E.wait(E.v, sV, v_c)
                sV.inc(E.v.tensor_tensor(
                    csc2[:], c_sl(par)[:, base + kk:base + kk + Q],
                    convw_sb[:, 33 * 4 + kk:33 * 4 + kk + 1].broadcast_to([128, Q]),
                    ALU.mult))
                E.wait(E.v, sV, sV.n)
                v_c = sV.inc(E.v.tensor_add(cacc2[:], cacc2[:], csc2[:]))
            cpar = ci % 2
            E.wait(E.s, sV, v_c)
            E.wait(E.s, sT, conv_T4[cpar])
            a_c_prev = sA.inc(E.s.activation(cC_sl(cpar), cacc2[:], AF.Silu))
            a_c = a_c_prev
            E.wait(E.s, sV, silz_V[cpar])
            a_z = sA.inc(E.s.activation(
                sz_sl(cpar).rearrange("p (t c) -> p t c", c=Q),
                z_sl(par).rearrange("p (t c) -> p t c", c=256)[:, :, base:base + Q],
                AF.Silu))
            if local == 1:
                a_z_pair[par] = a_z
                if pp + 2 < npr:
                    reload4(pp + 2)
            E.wait(E.v, sT, sin_T)
            v_sc = sV.inc(E.v.tensor_tensor(
                Sini_c[:].rearrange("p (h q) -> p h q", q=64),
                Sinit[:].rearrange("p (h q) -> p h q", q=64),
                DprB[:, ci * 64:(ci + 1) * 64].rearrange("p (h one) -> p h one", one=1)
                .broadcast_to([128, 64, 64]), ALU.mult))
            epar = ci % 2
            er_sem = scm if epar == 0 else sxr
            E.wait(E.y, sV, er_evV4[epar])
            dmay(er_sl(epar)[0:64, :].rearrange("p (j2 i) -> p j2 i", i=Q),
                 d_ep[0:64:2, ci * Q:(ci + 1) * Q]
                 .unsqueeze(0).broadcast_to([64, 32, Q]), er_sem)
            er_Y = dmay(er_sl(epar)[64:128, :].rearrange("p (j2 i) -> p j2 i", i=Q),
                        d_ep[1:64:2, ci * Q:(ci + 1) * Q]
                        .unsqueeze(0).broadcast_to([64, 32, Q]), er_sem)
            E.wait(E.t, sA, a_c)
            E.wait(E.t, sV, v_sc)
            E.wait(E.v, syns, syns.n)
            ev_V = sV.n
            ev_q = [ev_V, ev_V]
            for qr in range(4):
                rp = qr % 2
                ro = rp * 1024
                E.wait(E.t, sV, ev_q[rp])
                for hg4 in range(2):
                    for j in range(4):
                        hpair = qr * 8 + hg4 * 4 + j
                        he, ho = 2 * hpair, 2 * hpair + 1
                        co = ro + hg4 * 512 + j * 128
                        E.t.matmul(ps_y[0:64, co:co + 128],
                                   Sini_c[:, he * 64:(he + 1) * 64], cC_sl(cpar),
                                   start=True, stop=True)
                        mmy = E.t.matmul(ps_y[64:128, co:co + 128],
                                         Sini_c[:, ho * 64:(ho + 1) * 64], cC_sl(cpar),
                                         start=True, stop=True)
                t_y = sT.inc(mmy)
                sin_T = t_y
                conv_T4[cpar] = t_y
                sl4 = slice(qr * 1024, qr * 1024 + 1024)
                E.wait(E.v, sT, t_y)
                if qr == 0:
                    E.wait(E.v, er_sem, er_Y)
                    E.wait(E.v, sy1[local], y1_Y[local])
                    E.wait(E.v, sA, max(a_z, a_sq_prev))
                sV.inc(E.v.tensor_tensor(ysc2[:, ro:ro + 1024], ps_y[:, ro:ro + 1024],
                                         er_sl(epar)[:, sl4], ALU.mult))
                sV.inc(E.v.tensor_add(ysum[:, sl4], ysc2[:, ro:ro + 1024],
                                      y1_sl(local)[:, sl4]))
                sV.inc(E.v.tensor_mul(ysum[:, sl4], ysum[:, sl4], sz_sl(cpar)[:, sl4]))
                ev_V = sV.inc(E.v.tensor_tensor(
                    ynbf[:, sl4].rearrange("p (j c) -> p j c", c=128),
                    ysum[:, sl4].rearrange("p (j c) -> p j c", c=128),
                    normw_sb[:, qr * 8:qr * 8 + 8]
                    .rearrange("p (j one) -> p j one", one=1)
                    .broadcast_to([128, 8, 128]), ALU.mult))
                ev_q[rp] = ev_V
                er_evV4[epar] = ev_V
            y1_rdV[local] = ev_V
            silz_V[cpar] = ev_V
            E.wait(E.s, sV, ev_V)
            E.wait(E.s, sT, t_r_prev)
            a_sq = sA.inc(E.s.activation(yzsq[:], ysum[:], AF.Square))
            a_sq_prev = a_sq
            E.wait(E.g, sV, ev_V)
            dmag(d_yn[:, :, ci * Q:(ci + 1) * Q].rearrange("t p c -> p t c"),
                 ynbf[:].rearrange("p (t c) -> p t c", c=Q), syns)
            E.wait(E.t, sA, a_sq)
            E.wait(E.t, sV, row_cpV[ci % 2])
            for xt in range(32):
                mm_r = E.t.matmul(ps_r[:, (ci % 2) * 128:(ci % 2 + 1) * 128], ones_col[:],
                                  yzsq[:, xt * 128:(xt + 1) * 128],
                                  start=(xt == 0), stop=(xt == 31))
            t_r = sT.inc(mm_r)
            t_r_prev = t_r
            E.wait(E.v, sT, t_r)
            row_cpV[ci % 2] = sV.inc(E.v.tensor_copy(
                ssqrow[:, ci * Q:(ci + 1) * Q],
                ps_r[:, (ci % 2) * 128:(ci % 2 + 1) * 128]))
        if dbg_tensor("yn", [32, 128, T], BF16):
            E.wait(E.y, syns, syns.n)
            for t0 in range(32):
                dmay(dbg_outs["yn"][t0], d_yn[t0])
        E.wait(E.v, sV, sV.n)
        sV.inc(E.v.tensor_scalar(ssqrow[:], ssqrow[:], 1.0 / HID, RMS_EPS_M,
                                 ALU.mult, ALU.add))
        E.wait(E.v, sV, sV.n)
        v_rs = sV.inc(E.v.reciprocal(ssqrow[:], ssqrow[:]))
        E.wait(E.s, sV, v_rs)
        a_rs = sA.inc(E.s.activation(ssqrow[:], ssqrow[:], AF.Sqrt))
        E.wait(E.y, sA, a_rs)
        g_rs = bounce_bcast(None, ssqrow[:], rstd_rep[:], T)
        yn_spill_Y = sy.n

    seg24.close()

    phase_barrier()
    seg56, sb56 = seg("right")
    rows2 = sb56("rows_ln2", [1, 2 * T])
    with contextlib.ExitStack() as stk:
        sb, ps = mk(stk)
        ynres = sb("ynres", [128, 32 * T], BF16)
        yn_sl = lambda k: ynres[:, k * T:(k + 1) * T]
        wo_sb = sb("wo_sb", [128, 2 * 32 * 128], BF16)
        wo_sl = lambda par, k: wo_sb[:, par * 32 * 128 + k * 128:par * 32 * 128 + (k + 1) * 128]
        x1sb = sb("x1sb", [128, T])
        x1sq = sb("x1sq", [128, T])
        xres = sb("xres", [128, 2 * 512])
        xr_sl = lambda xb: xres[:, xb * 512:(xb + 1) * 512]
        ps_m2 = [ps(f"ps_m{i}", [128, 512]) for i in range(2)]
        ps_l2 = ps("ps_l2", [33, T])
        E.wait(E.y, syns, syns.n)
        for t0 in range(32):
            dmay(yn_sl(t0), d_yn[t0])
        yn_loaded = sy.n
        mm_T5 = [0, 0]
        xr_V = [0, 0]
        ev_V5 = [0, 0]
        v_x1 = 0
        for m in range(16):
            par = m % 2
            E.wait(E.g, sT, mm_T5[par])
            g_l = dmag(wo_sb[:, par * 32 * 128:(par + 1) * 32 * 128]
                       .rearrange("p (a c) -> p a c", c=128),
                       dWout[m].rearrange("a p c -> p a c"), swo[par])
            E.wait(E.t, swo[par], g_l)
            if m == 0:
                E.wait(E.t, sy, yn_loaded)
            for pi, (coff, cnb) in enumerate(col_splits(T, 512)):
                pb = pi % 2
                E.wait(E.t, sV, ev_V5[pb])
                if pi == 0:
                    E.wait(E.t, sV, v_x1)
                    E.wait(E.t, sx1s, sx1s.n)
                for k in range(32):
                    mm = E.t.matmul(ps_m2[pb][:, 0:cnb], wo_sl(par, k),
                                    yn_sl(k)[:, coff:coff + cnb],
                                    start=(k == 0), stop=(k == 31), skip_group_check=True)
                t_m = sT.inc(mm)
                mm_T5[par] = t_m
                E.wait(E.y, sV, xr_V[pb])
                y_x = dmay(xr_sl(pb)[:, 0:cnb], dxT[m, :, 3 + coff:3 + coff + cnb], sxr)
                E.wait(E.v, sT, t_m)
                E.wait(E.v, sxr, y_x)
                E.wait(E.v, sy, g_rs)
                sV.inc(E.v.tensor_tensor(x1sq[:, coff:coff + cnb], ps_m2[pb][:, 0:cnb],
                                         rstd_rep[:, coff:coff + cnb], ALU.mult))
                E.wait(E.v, sV, sV.n)
                v_x1 = sV.inc(E.v.scalar_tensor_tensor(
                    x1sb[:, coff:coff + cnb], x1sq[:, coff:coff + cnb], ga_c(m),
                    xr_sl(pb)[:, 0:cnb], ALU.mult, ALU.add))
                ev_V5[pb] = v_x1
                xr_V[pb] = v_x1
            E.wait(E.y, sV, v_x1)
            dmay(d_x1[m], x1sb[:], sx1s)
            E.wait(E.s, sV, v_x1)
            a_sq = sA.inc(E.s.activation(x1sq[:], x1sb[:], AF.Square))
            E.wait(E.t, sA, a_sq)
            for (coff, cnb) in col_splits(T, 512):
                E.t.matmul(ps_l2[0:1, coff:coff + cnb], ones_col[:],
                           x1sb[:, coff:coff + cnb], start=(m == 0), stop=(m == 15),
                           skip_group_check=True)
                mm2 = E.t.matmul(ps_l2[32:33, coff:coff + cnb], ones_col[:],
                                 x1sq[:, coff:coff + cnb], start=(m == 0), stop=(m == 15),
                                 skip_group_check=True)
            t_l2 = sT.inc(mm2)
        x1_spill_Y = sx1s.n
        if dbg_tensor("x1", [NKT, 128, T]):
            E.wait(E.y, sx1s, x1_spill_Y)
            for m in range(16):
                dmay(dbg_outs["x1"][m], d_x1[m])
        m2row = rows2[:, 0:T]
        r2row = rows2[:, T:2 * T]
        E.wait(E.v, sT, sT.n)
        sV.inc(E.v.tensor_scalar_mul(m2row, ps_l2[0:1, 0:T], 1.0 / D_MODEL))
        E.wait(E.v, sV, sV.n)
        sV.inc(E.v.tensor_mul(r2row, m2row, m2row))
        E.wait(E.v, sV, sV.n)
        sV.inc(E.v.scalar_tensor_tensor(r2row, ps_l2[32:33, 0:T], 1.0 / D_MODEL, r2row,
                                        ALU.mult, ALU.subtract))
        E.wait(E.v, sV, sV.n)
        sV.inc(E.v.tensor_scalar_add(r2row, r2row, LN_EPS))
        E.wait(E.v, sV, sV.n)
        v_r2 = sV.inc(E.v.reciprocal(r2row, r2row))
        E.wait(E.s, sV, v_r2)
        a_r2 = sA.inc(E.s.activation(r2row, r2row, AF.Sqrt))

    seg45.close()

    phase_barrier()
    seg67, sb67 = seg("left")
    qx2 = sb67("qx2", [128, NKT * T], BF16)
    qx_sl = lambda k: qx2[:, k * T:(k + 1) * T]
    with contextlib.ExitStack() as stk:
        sb, ps = mk(stk)
        m2rep = sb("m2rep", [128, T])
        r2rep = sb("r2rep", [128, T])
        s2rep = sb("s2rep", [128, T])
        i2rep = sb("i2rep", [128, T])
        h2t = sb("h2t", [128, T])
        h2s = sb("h2s", [128, T])
        rmax2 = sb("rmax2", [128, T])
        s2tm = sb("s2tm", [128, NTB])
        i2tm = sb("i2tm", [128, NTB])
        xpose = sb("xpose", [128, 128])
        ps_tp = ps("p6_tp", [128, 128])
        E.wait(E.y, sA, sA.n)
        E.wait(E.y, sV, sV.n)
        bounce_bcast(None, rows2[:, 0:T], m2rep[:], T)
        g_r2 = bounce_bcast(None, rows2[:, T:2 * T], r2rep[:], T)
        rmin2 = sb("rmin2", [128, T])
        mn2tm = sb("mn2tm", [128, NTB])
        E.wait(E.v, sy, g_r2)
        sV.inc(E.v.memset(rmax2[:], 0.0))
        v_rm = sV.inc(E.v.memset(rmin2[:], 0.0))

        def ln2_tile(m, dest, v_guard):
            E.wait(E.y, sV, sV.n)
            y_x = dmay(h2s[:], d_x1[m])
            E.wait(E.v, sy, y_x)
            sV.inc(E.v.tensor_sub(h2t[:], h2s[:], m2rep[:]))
            vv = sV.inc(E.v.tensor_mul(h2t[:], h2t[:], r2rep[:]))
            E.wait(E.s, sV, vv)
            aa = sA.inc(E.s.activation(h2t[:], h2t[:], AF.Identity,
                                       bias=sb_c(m), scale=cb1_c(m)))
            return aa

        for m in range(16):
            aa = ln2_tile(m, None, v_rm)
            E.wait(E.v, sA, aa)
            sV.inc(E.v.tensor_tensor(rmax2[:], rmax2[:], h2t[:], ALU.max))
            v_rm = sV.inc(E.v.tensor_tensor(rmin2[:], rmin2[:], h2t[:], ALU.min))
        v_tp = v_rm
        for tb in range(NTB):
            E.wait(E.t, sV, v_tp)
            t_t = sT.inc(E.t.transpose(ps_tp[:], rmax2[:, tb * 128:(tb + 1) * 128], identf[:]))
            E.wait(E.v, sT, t_t)
            v_tp = sV.inc(E.v.reduce_max(s2tm[:, tb:tb + 1], ps_tp[:], AX))
            E.wait(E.t, sV, v_tp)
            t_m2 = sT.inc(E.t.transpose(ps_tp[:], rmin2[:, tb * 128:(tb + 1) * 128], identf[:]))
            E.wait(E.v, sT, t_m2)
            v_tp = sV.inc(E.v.tensor_reduce(mn2tm[:, tb:tb + 1], ps_tp[:], AX, ALU.min, negate=True))
        E.wait(E.v, sV, v_tp)
        sV.inc(E.v.tensor_tensor(s2tm[:], s2tm[:], mn2tm[:], ALU.max))
        E.wait(E.v, sV, sV.n)
        sV.inc(E.v.tensor_scalar_max(s2tm[:], s2tm[:], QEPS))
        E.wait(E.v, sV, sV.n)
        sV.inc(E.v.tensor_scalar_mul(i2tm[:], s2tm[:], 1.0 / 127.0))
        E.wait(E.v, sV, sV.n)
        v_s2 = sV.inc(E.v.reciprocal(s2tm[:], i2tm[:]))
        for (tmt, rep) in ((s2tm, s2rep), (i2tm, i2rep)):
            E.wait(E.t, sV, sV.n)
            t_b = sT.inc(E.t.transpose(ps_tp[0:NTB, 0:128], tmt[:, 0:NTB], identf[:]))
            E.wait(E.v, sT, t_b)
            E.wait(E.v, sy, sy.n)
            v_b = sV.inc(E.v.tensor_copy(xpose[0:NTB, 0:128], ps_tp[0:NTB, 0:128]))
            E.wait(E.y, sV, v_b)
            E.wait(E.y, sy, sy.n)
            dmay(d_rowb[0:NTB], xpose[0:NTB, 0:128], srow)
            E.wait(E.y, sy, sy.n)
            E.wait(E.y, srow, srow.n)
            g_b = dmay(rep[:], d_rowb.rearrange("a b -> (a b)")[0:T]
                       .unsqueeze(0).broadcast_to([128, T]))
        rep_G = sy.n
        v_q = sV.n
        for m in range(16):
            aa = ln2_tile(m, None, v_q)
            E.wait(E.v, sA, aa)
            E.wait(E.v, sy, rep_G)
            sV.inc(E.v.tensor_mul(h2t[:], h2t[:], s2rep[:]))
            E.wait(E.v, sV, sV.n)
            sV.inc(E.v.tensor_scalar(h2t[:], h2t[:], MAGIC, MAGIC, ALU.add, ALU.subtract))
            E.wait(E.v, sV, sV.n)
            sV.inc(E.v.tensor_scalar(h2t[:], h2t[:], -128.0, 127.0, ALU.max, ALU.min))
            E.wait(E.v, sV, sV.n)
            v_q = sV.inc(E.v.tensor_mul(qx_sl(m), h2t[:], i2rep[:]))
        qx2_V = sV.n
        if dbg_tensor("qx2", [NKT, 128, T], BF16):
            E.wait(E.y, sV, qx2_V)
            for m in range(16):
                dmay(dbg_outs["qx2"][m], qx_sl(m))

    seg56.close()

    phase_barrier()
    seg7b, sb7b = seg("right")
    rmax3 = sb7b("rmax3", [128, T])
    rmin3 = sb7b("rmin3", [128, T])
    ssq3 = sb7b("ssq3", [128, T])
    with contextlib.ExitStack() as stk:
        sb, ps = mk(stk)
        wg_sb = sb("wg_sb", [128, 2 * NKT * 128], BF16)
        wu_sb = sb("wu_sb", [128, 2 * NKT * 128], BF16)
        wg_sl = lambda par, k: wg_sb[:, par * NKT * 128 + k * 128:par * NKT * 128 + (k + 1) * 128]
        wu_sl = lambda par, k: wu_sb[:, par * NKT * 128 + k * 128:par * NKT * 128 + (k + 1) * 128]
        hmid = sb("hmid", [128, T], BF16)
        gsc = sb("gsc", [128, T])
        silug = sb("silug", [128, T])
        hsq = sb("hsq", [128, T])
        ps_g2 = [ps(f"ps_g{i}", [128, 1024]) for i in range(2)]
        ps_u2 = [ps(f"ps_u{i}", [128, 1024]) for i in range(2)]
        E.wait(E.v, sV, qx2_V)
        sV.inc(E.v.memset(rmax3[:], 0.0))
        sV.inc(E.v.memset(rmin3[:], 0.0))
        v_f0 = sV.inc(E.v.memset(ssq3[:], 0.0))
        E.wait(E.s, sV, v_f0)
        E.wait(E.g, sCC, cc3)
        pieces = col_splits(T, 512)
        groups = [pieces[i:i + 2] for i in range(0, len(pieces), 2)]
        mm_T = [0, 0]
        ev_Vp = [v_f0, v_f0]
        ev_Ap = [0, 0]
        hm_y = 0
        for it in range(64):
            par = it % 2
            E.wait(E.y, sT, mm_T[par])
            r, loc = divmod(it, 8)
            g_lg = dmay(wg_sb[:, par * NKT * 128:(par + 1) * NKT * 128],
                        d_agqo.ap()[r, 0][:, loc * 2048:(loc + 1) * 2048], sg7[par])
            g_lu = dmay(wu_sb[:, par * NKT * 128:(par + 1) * NKT * 128],
                        d_agqo.ap()[r, 1][:, loc * 2048:(loc + 1) * 2048], shm[par])
            E.wait(E.t, sg7[par], g_lg)
            E.wait(E.t, shm[par], g_lu)
            v_h = 0
            for gi, grp in enumerate(groups):
                pb = gi % 2
                g0 = grp[0][0]
                gw = sum(n for _, n in grp)
                E.wait(E.t, sV, ev_Vp[pb])
                E.wait(E.t, sA, ev_Ap[pb])
                for (coff, cnb) in grp:
                    po = coff - g0
                    for k in range(NKT):
                        E.t.matmul(ps_g2[pb][:, po:po + cnb], wg_sl(par, k),
                                   qx_sl(k)[:, coff:coff + cnb],
                                   start=(k == 0), stop=(k == NKT - 1),
                                   skip_group_check=True)
                    for k in range(NKT):
                        mmu = E.t.matmul(ps_u2[pb][:, po:po + cnb], wu_sl(par, k),
                                         qx_sl(k)[:, coff:coff + cnb],
                                         start=(k == 0), stop=(k == NKT - 1),
                                         skip_group_check=True)
                t_gu = sT.inc(mmu)
                mm_T[par] = t_gu
                E.wait(E.s, sT, t_gu)
                a_g = sA.inc(E.s.activation(silug[:, g0:g0 + gw], ps_g2[pb][:, 0:gw],
                                            AF.Silu, scale=swB[:, 0:1]))
                E.wait(E.v, sA, a_g)
                if gi == 0:
                    E.wait(E.v, sy1s, hm_y)
                v_h = sV.inc(E.v.scalar_tensor_tensor(
                    hmid[:, g0:g0 + gw], ps_u2[pb][:, 0:gw], swB[:, 1:2],
                    silug[:, g0:g0 + gw], ALU.mult, ALU.mult))
                E.wait(E.s, sV, v_h)
                sA.inc(E.s.activation(gsc[:, g0:g0 + gw], hmid[:, g0:g0 + gw],
                                      AF.Identity, scale=lnw_sb[:, it:it + 1]))
                a_sq2 = sA.inc(E.s.activation(hsq[:, g0:g0 + gw], hmid[:, g0:g0 + gw],
                                              AF.Square))
                ev_Ap[pb] = a_g
                E.wait(E.v, sA, a_sq2)
                sV.inc(E.v.tensor_add(ssq3[:, g0:g0 + gw], ssq3[:, g0:g0 + gw],
                                      hsq[:, g0:g0 + gw]))
                sV.inc(E.v.tensor_tensor(rmax3[:, g0:g0 + gw], rmax3[:, g0:g0 + gw],
                                         gsc[:, g0:g0 + gw], ALU.max))
                ev_Vp[pb] = sV.inc(E.v.tensor_tensor(rmin3[:, g0:g0 + gw],
                                                     rmin3[:, g0:g0 + gw],
                                                     gsc[:, g0:g0 + gw], ALU.min))
            E.wait(E.g, sV, v_h)
            hm_y = dmag(d_hmid[it], hmid[:], sy1s)
        ffn1_G = sG.n
        ffn1_V = sV.n
        hmid_Y = sy.n

    seg67.close()

    phase_barrier()
    seg78, sb78 = seg("left")
    rs3rep = sb78("rs3rep", [128, T])
    cmb_rep = sb78("cmb_rep", [128, T])
    with contextlib.ExitStack() as stk:
        sb, ps = mk(stk)
        s3tm = sb("s3tm", [128, NTB])
        r3tm = sb("r3tm", [128, NTB])
        xpose = sb("xpose7", [128, 128])
        ps_tp = ps("p7_tp", [128, 128])
        mn3tm = sb("mn3tm", [128, NTB])
        v_tp = ffn1_V
        for tb in range(NTB):
            E.wait(E.t, sV, v_tp)
            E.wait(E.t, sG, ffn1_G)
            t_t = sT.inc(E.t.transpose(ps_tp[:], rmax3[:, tb * 128:(tb + 1) * 128], identf[:]))
            E.wait(E.v, sT, t_t)
            v_tp = sV.inc(E.v.reduce_max(s3tm[:, tb:tb + 1], ps_tp[:], AX))
            E.wait(E.t, sV, v_tp)
            t_m2 = sT.inc(E.t.transpose(ps_tp[:], rmin3[:, tb * 128:(tb + 1) * 128], identf[:]))
            E.wait(E.v, sT, t_m2)
            v_tp = sV.inc(E.v.tensor_reduce(mn3tm[:, tb:tb + 1], ps_tp[:], AX, ALU.min, negate=True))
            E.wait(E.t, sV, v_tp)
            t_t2 = sT.inc(E.t.transpose(ps_tp[:], ssq3[:, tb * 128:(tb + 1) * 128], identf[:]))
            E.wait(E.v, sT, t_t2)
            v_tp = sV.inc(E.v.tensor_reduce(r3tm[:, tb:tb + 1], ps_tp[:], AX, ALU.add))
        E.wait(E.v, sV, v_tp)
        sV.inc(E.v.tensor_tensor(s3tm[:], s3tm[:], mn3tm[:], ALU.max))
        E.wait(E.v, sV, sV.n)
        sV.inc(E.v.tensor_scalar(r3tm[:], r3tm[:], 1.0 / I_FFN, RMS_EPS_F, ALU.mult, ALU.add))
        E.wait(E.v, sV, sV.n)
        v_r3 = sV.inc(E.v.reciprocal(r3tm[:], r3tm[:]))
        E.wait(E.s, sV, v_r3)
        a_r3 = sA.inc(E.s.activation(r3tm[:], r3tm[:], AF.Sqrt))
        i3tm = sb("i3tm", [128, NTB])
        E.wait(E.v, sA, a_r3)
        sV.inc(E.v.tensor_mul(s3tm[:], s3tm[:], r3tm[:]))
        sV.inc(E.v.tensor_scalar_max(s3tm[:], s3tm[:], QEPS))
        sV.inc(E.v.tensor_scalar_mul(i3tm[:], s3tm[:], 1.0 / 127.0))
        sV.inc(E.v.reciprocal(s3tm[:], i3tm[:]))
        v_rs3 = sV.inc(E.v.tensor_mul(s3tm[:], s3tm[:], r3tm[:]))
        for (tmt, rep) in ((s3tm, rs3rep), (i3tm, cmb_rep)):
            E.wait(E.t, sV, sV.n)
            t_b = sT.inc(E.t.transpose(ps_tp[0:NTB, 0:128], tmt[:, 0:NTB], identf[:]))
            E.wait(E.v, sT, t_b)
            E.wait(E.v, sy, sy.n)
            v_b = sV.inc(E.v.tensor_copy(xpose[0:NTB, 0:128], ps_tp[0:NTB, 0:128]))
            E.wait(E.y, sV, v_b)
            E.wait(E.y, sy, sy.n)
            dmay(d_rowb[0:NTB], xpose[0:NTB, 0:128], srow)
            E.wait(E.y, sy, sy.n)
            E.wait(E.y, srow, srow.n)
            g_b = dmay(rep[:], d_rowb.rearrange("a b -> (a b)")[0:T]
                       .unsqueeze(0).broadcast_to([128, T]))
        E.wait(E.v, sy, sy.n)
        v_cmb = sV.inc(E.v.tensor_scalar_mul(cmb_rep[:], cmb_rep[:], swB[:, 2:3]))
        if dbg_tensor("ffnrows", [2, 128, T]):
            E.wait(E.y, sV, sV.n)
            dmay(dbg_outs["ffnrows"][0], rs3rep[:])
            dmay(dbg_outs["ffnrows"][1], cmb_rep[:])

    seg7b.close()

    phase_barrier()
    with contextlib.ExitStack() as stk:
        sb, ps = mk(stk)
        n_tq = ceil_div(T, 512)
        n_tqp = ceil_div(n_tq, 2)
        qh2 = sb("qh2", [128, min(2, n_tq) * 64 * 512], BF16)
        qh_sl = lambda ti, k: qh2[:, (ti * 64 + k) * 512:(ti * 64 + k + 1) * 512]
        wd_sb = sb("wd_sb", [128, 2 * 512], BF16)
        wd_sl = lambda par, m: wd_sb[:, par * 512 + m * 128:par * 512 + (m + 1) * 128]
        hm_st = sb("hm_st", [128, 4 * 1024], BF16)
        hm_sl = lambda s: hm_st[:, s * 1024:(s + 1) * 1024]
        x1_st = sb("x1_st", [128, 512])
        outsb = sb("outsb", [128, 2 * 512])
        o_sl = lambda par: outsb[:, par * 512:(par + 1) * 512]
        qsc = sb("qsc", [128, 4 * 1024])
        qs_sl = lambda s: qsc[:, s * 1024:(s + 1) * 1024]
        ps_d = [ps(f"ps_d{i}", [128, 512]) for i in range(8)]
        aq_s = [0, 0, 0, 0]
        qv_s = [0, 0, 0, 0]
        ev_ps = [0] * 8
        wd_T = [0, 0]
        dn_V = sV.n
        dn_T = sT.n
        qh_T = sT.n
        for tqp in range(n_tqp):
            tqs = [t for t in (2 * tqp, 2 * tqp + 1) if t < n_tq]
            for mq in range(4):
                for k in range(64):
                    par = k % 2
                    if mq == 0 and par == 0:
                        for ti, tq in enumerate(tqs):
                            s = ti * 2 + (k // 2) % 2
                            coff = tq * 512
                            cnb = min(512, T - coff)
                            hm3 = hm_sl(s).rearrange("p (j c) -> p j c", c=512)
                            qs3 = qs_sl(s).rearrange("p (j c) -> p j c", c=512)
                            E.wait(E.y, sA, aq_s[s])
                            E.wait(E.y, sT, qh_T)
                            y_h = dmay(hm3[:, :, 0:cnb],
                                       d_hmid[k:k + 2, :, coff:coff + cnb]
                                       .rearrange("j p c -> p j c"), shm[(k // 2) % 2])
                            E.wait(E.s, shm[(k // 2) % 2], y_h)
                            E.wait(E.s, sV, qv_s[s])
                            for dk in range(2):
                                aq = sA.inc(E.s.activation(
                                    qs_sl(s)[:, dk * 512:dk * 512 + cnb],
                                    hm_sl(s)[:, dk * 512:dk * 512 + cnb],
                                    AF.Identity, scale=lnw_sb[:, k + dk:k + dk + 1]))
                            aq_s[s] = aq
                            E.wait(E.v, sA, aq)
                            E.wait(E.v, sT, qh_T)
                            sV.inc(E.v.tensor_tensor(
                                qs3[:, :, 0:cnb], qs3[:, :, 0:cnb],
                                rs3rep[:, coff:coff + cnb].unsqueeze(1)
                                .broadcast_to([128, 2, cnb]), ALU.mult))
                            sV.inc(E.v.tensor_scalar(
                                qs3[:, :, 0:cnb], qs3[:, :, 0:cnb],
                                MAGIC, MAGIC, ALU.add, ALU.subtract))
                            qv_s[s] = sV.inc(E.v.tensor_scalar(
                                qh2[:, (ti * 64 + k) * 512:(ti * 64 + k + 2) * 512]
                                .rearrange("p (j c) -> p j c", c=512)[:, :, 0:cnb],
                                qs3[:, :, 0:cnb],
                                -128.0, 127.0, ALU.max, ALU.min))
                        qv_k = qv_s[(len(tqs) - 1) * 2 + (k // 2) % 2]
                    r, loc = divmod(k, 8)
                    wsrc = d_agqo.ap()[r, 2][:, loc * 2048 + mq * 512:
                                             loc * 2048 + mq * 512 + 512]
                    if par == 0:
                        E.wait(E.g, sT, wd_T[0])
                        g_l = dmag(wd_sb[:, 0:512], wsrc, swd[0])
                        E.wait(E.t, swd[0], g_l)
                    else:
                        E.wait(E.y, sT, wd_T[1])
                        g_l = dmay(wd_sb[:, 512:1024], wsrc, swd[1])
                        E.wait(E.t, swd[1], g_l)
                    if mq == 0:
                        E.wait(E.t, sV, qv_k)
                    for m4 in range(4):
                        for ti, tq in enumerate(tqs):
                            cnb = min(512, T - tq * 512)
                            if k == 0 and (mq or tqp):
                                E.wait(E.t, sV, ev_ps[ti * 4 + m4])
                            mm = E.t.matmul(ps_d[ti * 4 + m4][:, 0:cnb],
                                            wd_sl(par, m4), qh_sl(ti, k)[:, 0:cnb],
                                            start=(k == 0), stop=(k == 63),
                                            skip_group_check=True)
                    dn_T = sT.inc(mm)
                    wd_T[par] = dn_T
                for m4 in range(4):
                    gm = mq * 4 + m4
                    for ti, tq in enumerate(tqs):
                        coff = tq * 512
                        cnb = min(512, T - coff)
                        opar = (m4 + ti) % 2
                        E.wait(E.s, sV, dn_V)
                        E.s.dma_start(out=x1_st[:, 0:cnb],
                                      in_=d_x1[gm, :, coff:coff + cnb]).then_inc(sx8, 16)
                        sx8.n += 16
                        y_x = sx8.n
                        E.wait(E.v, sT, dn_T)
                        E.wait(E.v, sx8, y_x)
                        E.wait(E.v, sout, sout.n)
                        sV.inc(E.v.tensor_tensor(o_sl(opar)[:, 0:cnb],
                                                 ps_d[ti * 4 + m4][:, 0:cnb],
                                                 cmb_rep[:, coff:coff + cnb], ALU.mult))
                        v_o = sV.inc(E.v.scalar_tensor_tensor(
                            o_sl(opar)[:, 0:cnb], o_sl(opar)[:, 0:cnb], gb_c(gm),
                            x1_st[:, 0:cnb], ALU.mult, ALU.add))
                        ev_ps[ti * 4 + m4] = v_o
                        E.wait(E.s, sV, v_o)
                        E.s.dma_start(out=dout[gm, :, coff:coff + cnb],
                                      in_=o_sl(opar)[:, 0:cnb]).then_inc(sout, 16)
                        sout.n += 16
                        dn_V = v_o
            qh_T = dn_T
        for s_ in ALLQ:
            E.wait(E.y, s_, s_.n)
            E.wait(E.g, s_, s_.n)

    seg78.close()
    with nc.Block() as block:
        E.replay(block)
    root.close()
    return nc, sorted(dbg_outs.keys())



def stage_inputs(inputs, nch=16, mode="full", sinit_list=None):
    T = nch * Q
    local = (mode == "local")
    f32 = lambda k: np.ascontiguousarray(np.asarray(inputs[k], dtype=np.float32))
    x = f32("L0_attn_x")
    W_in = f32("L0_attn_W_in")
    conv_w = f32("L0_attn_conv_w")
    dt_bias = f32("L0_attn_dt_bias")
    A_log = f32("L0_attn_A_log")
    D = f32("L0_attn_D")
    norm_w = f32("L0_attn_norm_w")
    W_out = f32("L0_attn_W_out")
    w_gate = f32("L0_ffn_w_gate")
    w_up = f32("L0_ffn_w_up")
    w_down = f32("L0_ffn_w_down")
    ln_w = f32("L0_ffn_ln_weight")
    c = f32("L0_c")
    wmod = f32("L0_wmod")

    WTp = np.zeros((D_MODEL, NOT * 128), dtype=np.float32)
    WTp[:, :DIN] = W_in.T
    WinT = np.ascontiguousarray(WTp.reshape(NKT, 128, NOT, 128).transpose(2, 0, 1, 3))
    WoutT = np.ascontiguousarray(W_out.T.reshape(32, 128, 16, 128).transpose(2, 0, 1, 3))
    gflat = np.ascontiguousarray(
        w_gate.T.reshape(NKT, 128, 64, 128).transpose(2, 0, 1, 3)).reshape(-1)
    uflat = np.ascontiguousarray(
        w_up.T.reshape(NKT, 128, 64, 128).transpose(2, 0, 1, 3)).reshape(-1)
    dflat = np.ascontiguousarray(
        w_down.T.reshape(64, 128, 16, 128).transpose(0, 2, 1, 3)).reshape(-1)
    nsl = gflat.size // 8

    convw = np.ascontiguousarray(conv_w.reshape(34, 128, 4).transpose(1, 0, 2).reshape(128, 136))
    dtbA = np.ascontiguousarray(np.stack([dt_bias, A_log], axis=1))
    Drep = np.ascontiguousarray(
        D[2 * np.arange(32)[None, :] + (np.arange(128)[:, None] // 64)])
    normw = np.ascontiguousarray(norm_w.reshape(32, 128).T)
    lnw = np.ascontiguousarray(ln_w.reshape(64, 128).T)
    cT = np.ascontiguousarray(c.T)

    in_maps = []
    for core in range(8):
        b, h = divmod(core, 2)
        start = h * T
        if start - 3 >= 0:
            pre = x[b, start - 3:start]
        else:
            pre = x[b, 0:3]
        xs = np.concatenate([pre, x[b, start:start + T]], axis=0)
        xT = np.ascontiguousarray(xs.T.reshape(NKT, 128, T + 3))
        nmt = 96 if local else 12
        cols = np.empty(128 * nmt, dtype=np.int64)
        for mt in range(nmt):
            g = mt if local else core * 12 + mt
            v, t = divmod(g, 16)
            cols[mt * 128:(mt + 1) * 128] = v * D_MODEL + t * 128 + np.arange(128)
        wms = wmod[:, cols]
        wmod_s = np.ascontiguousarray(wms.reshape(NKT, 128, nmt, 128).transpose(2, 0, 1, 3))
        nslabs = 64 if local else 8
        def _pm(flat):
            sl = flat if local else flat[core * nsl:(core + 1) * nsl]
            return sl.reshape(nslabs, 16, 128, 128).transpose(2, 0, 1, 3).reshape(128, -1)
        wq_slice = np.ascontiguousarray(np.stack([_pm(gflat), _pm(uflat), _pm(dflat)]))
        masks = np.zeros((128, 8), dtype=np.float32)
        masks[:, 0] = 1.0 if h == 1 else 0.0
        masks[:, 1] = 1.0 if h == 1 else 0.0
        masks[:, 2] = 1.0 if h == 0 else 0.0
        masks[:, 3 + b] = 1.0
        masks[:, 7] = 1.0
        im = {
            "xT": xT, "WinT": WinT, "WoutT": WoutT, "wq_slice": wq_slice,
            "wmod_s": wmod_s, "cT": cT, "convw": convw, "dtbA": dtbA,
            "Drep": Drep, "normw": normw, "lnw": lnw, "masks": masks,
        }
        if local:
            im["SinitT"] = (np.zeros((128, HID), dtype=np.float32)
                            if sinit_list is None else
                            np.ascontiguousarray(sinit_list[core].astype(np.float32)))
        in_maps.append(im)
    return in_maps


def kernel(**inputs):
    nch = 16
    in_maps = stage_inputs(inputs, nch)
    nc, _ = build_program(nch)
    res = run_bass_kernel_spmd(nc, in_maps, core_ids=list(range(8)))
    T = nch * Q
    out = np.empty((4, 4096, 2048), dtype=np.float32)
    for core in range(8):
        b, h = divmod(core, 2)
        o = np.asarray(res.results[core]["out"])
        out[b, h * T:(h + 1) * T, :] = o.reshape(D_MODEL, T).T
    return out
